# version 01
# baseline (speedup 1.0000x reference)
import math
import numpy as np
import ml_dtypes
import concourse.bass as bass
import concourse.mybir as mybir
from concourse.bass_utils import run_bass_kernel_spmd
from concourse.alu_op_type import AluOpType
from contextlib import contextmanager





F32 = mybir.dt.float32
BF16 = mybir.dt.bfloat16
AF = mybir.ActivationFunctionType
OP = AluOpType

D = 1024
TC = 8192
NTT = TC // 512
SEQ = 2048
NH, HD, MBS, NCH = 16, 64, 16, 128
C1 = 0.79788456
C2 = 0.044715
SC_A = 2 * C1
SC_B = 2 * C1 * C2
C4 = 0.1070322243


def host_prep(inp, core):
    out = {}
    x = np.ascontiguousarray(inp["L0_attn_hidden_states"][core * 4:(core + 1) * 4].reshape(TC, D))
    out["x"] = x
    for nm, k in [("wqT", "L0_attn_Wq"), ("wkT", "L0_attn_Wk"), ("wvT", "L0_attn_Wv"),
                  ("wgT", "L0_attn_Wg"), ("woT", "L0_attn_Wo")]:
        out[nm] = np.ascontiguousarray(inp[k].T)
    out["lrwT"] = np.ascontiguousarray(inp["L0_attn_lr_w"][:, 0, :].T)
    out["lrb"] = np.ascontiguousarray(inp["L0_attn_lr_b"][:, 0])
    out["lti_rep"] = np.ascontiguousarray(np.tile(inp["L0_attn_learnable_token_idx"], 8)[:, None])
    W1 = inp["L0_attn_W1"]; W2 = inp["L0_attn_W2"]
    w1duo = np.zeros((8, 128, 256), np.float32)
    w2t = np.zeros((8, 128, 256), np.float32)
    w2cm = np.zeros((8, 2, 128, 128), np.float32)
    for a in range(8):
        w1duo[a, 0:64] = W1[2 * a]; w1duo[a, 64:128] = W1[2 * a + 1]
        w2t[a, 0:64] = W2[2 * a].T; w2t[a, 64:128] = W2[2 * a + 1].T
        for j in range(2):
            w2cm[a, j, :, 0:64] = W2[2 * a, 128 * j:128 * (j + 1), :]
            w2cm[a, j, :, 64:128] = W2[2 * a + 1, 128 * j:128 * (j + 1), :]
    out["w1duo"] = w1duo; out["w2t"] = w2t; out["w2cm"] = np.ascontiguousarray(w2cm.transpose(0, 2, 1, 3).reshape(8, 128, 256))
    b1 = inp["L0_attn_b1"][:, 0, :]; b2 = inp["L0_attn_b2"][:, 0, :]
    b1rep = np.zeros((2, 128, 256), np.float32); b2rep = np.zeros((2, 128, 64), np.float32)
    lnw_nj = np.zeros((2, 128, 64), np.float32); lnb_nj = np.zeros((2, 128, 64), np.float32)
    for hs in range(2):
        for p in range(8):
            h = 8 * hs + p
            b1rep[hs, 16 * p:16 * p + 16] = b1[h]
            b2rep[hs, 16 * p:16 * p + 16] = b2[h]
            lnw_nj[hs, 16 * p:16 * p + 16] = inp["L0_attn_norm_w"][h]
            lnb_nj[hs, 16 * p:16 * p + 16] = inp["L0_attn_norm_b"][h]
    out["b1rep"] = b1rep; out["b2rep"] = b2rep
    out["lnw_nj"] = lnw_nj; out["lnb_nj"] = lnb_nj
    out["postw"] = inp["L0_attn_post_w"]; out["postb"] = inp["L0_attn_post_b"]
    out["ffn_lng"] = inp["L0_ffn_ln_g"]; out["ffn_lnb"] = inp["L0_ffn_ln_b"]
    out["aw1"] = inp["L0_ffn_aw1"]; out["ab1"] = inp["L0_ffn_ab1"]
    out["aw2"] = inp["L0_ffn_aw2"]; out["ab2"] = inp["L0_ffn_ab2"]
    out["gatew"] = inp["L0_ffn_gate_w"]; out["gateb"] = inp["L0_ffn_gate_b"]
    out["ew1"] = inp["L0_ffn_ew1"]; out["eb1"] = inp["L0_ffn_eb1"]
    out["ew2"] = inp["L0_ffn_ew2"]; out["eb2"] = inp["L0_ffn_eb2"]
    out["gens"] = inp["L0_ffn_gens"]
    out["gcw"] = inp["L0_ffn_gc_w"][:, 0]; out["gcb"] = inp["L0_ffn_gc_b"]
    import ml_dtypes
    out["ident"] = np.eye(128, dtype=ml_dtypes.bfloat16)
    out["invi"] = np.ascontiguousarray(np.tile(1.0 / np.arange(1, 17, dtype=np.float32), 8)[:, None])
    tril = np.tril(np.ones((16, 16), np.float32))
    trilbd = np.zeros((128, 128), np.float32)
    onesbd = np.zeros((128, 128), np.float32)
    for p in range(8):
        trilbd[16 * p:16 * p + 16, 16 * p:16 * p + 16] = tril
        onesbd[16 * p:16 * p + 16, 16 * p:16 * p + 16] = 1.0
    out["trilbd"] = trilbd; out["onesbd"] = onesbd
    b2 = inp["L0_attn_b2"][:, 0, :]
    b2rep_bdt = np.zeros((2, 128, 128), np.float32)
    lnw_bdt = np.zeros((2, 128, 128), np.float32)
    lnb_bdt = np.zeros((2, 128, 128), np.float32)
    for hs in range(2):
        for pp in range(8):
            h = 8 * hs + pp
            co = 64 * (pp % 2)
            b2rep_bdt[hs, 16 * pp:16 * pp + 16, co:co + 64] = b2[h]
            lnw_bdt[hs, 16 * pp:16 * pp + 16, co:co + 64] = inp["L0_attn_norm_w"][h]
            lnb_bdt[hs, 16 * pp:16 * pp + 16, co:co + 64] = inp["L0_attn_norm_b"][h]
    out["b2rep_bdt"] = b2rep_bdt; out["lnw_bdt"] = lnw_bdt; out["lnb_bdt"] = lnb_bdt
    bd01 = np.zeros((128, 128), np.float32)
    for pp in range(8):
        bd01[16 * pp:16 * pp + 16, 64 * (pp % 2):64 * (pp % 2) + 64] = 1.0
    out["bd01"] = bd01
    trl = np.tril(np.ones((16, 16), np.float32))
    trilT01 = np.zeros((128, 128), np.float32)
    tril01 = np.zeros((128, 128), np.float32)
    ones01 = np.zeros((128, 128), np.float32)
    for pp in range(8):
        trilT01[16 * pp:16 * pp + 16, 16 * pp:16 * pp + 16] = trl.T
        tril01[16 * pp:16 * pp + 16, 16 * pp:16 * pp + 16] = trl
        ones01[16 * pp:16 * pp + 16, 16 * pp:16 * pp + 16] = 1.0
    out["trilT01"] = trilT01
    out["lti15_rep"] = np.full((128, 1), float(inp["L0_attn_learnable_token_idx"][15]), np.float32)
    import ml_dtypes as _md
    out["tril01b"] = tril01.astype(_md.bfloat16)
    out["ones01"] = ones01.astype(_md.bfloat16)
    out["identc"] = np.eye(128).astype(_md.bfloat16)
    out["ident4"] = np.ascontiguousarray(np.tile(np.eye(32), (4, 1))).astype(_md.bfloat16)
    et = np.zeros((2, 16, 16, 128), np.float32)
    for hs in range(2):
        for t in range(16):
            for p in range(8):
                et[hs, t, 8 * hs + p, 16 * p + t] = 1.0
    out["et"] = et.astype(ml_dtypes.bfloat16)
    out["postwrep"] = np.ascontiguousarray(np.tile(inp["L0_attn_post_w"][None, :], (128, 1)))
    out["postbrep"] = np.ascontiguousarray(np.tile(inp["L0_attn_post_b"][None, :], (128, 1)))
    out["lngrep"] = np.ascontiguousarray(np.tile(inp["L0_ffn_ln_g"][None, :], (128, 1)))
    out["lnbrep"] = np.ascontiguousarray(np.tile(inp["L0_ffn_ln_b"][None, :], (128, 1)))
    out["gatebrep"] = np.ascontiguousarray(np.tile(inp["L0_ffn_gate_b"][None, :], (128, 1)))
    out["epsc2"] = np.ascontiguousarray(np.tile(np.array([[1e-6, 1e-5]], np.float32), (128, 1)))
    gcw = inp["L0_ffn_gc_w"][:, 0]
    out["gcw1rep"] = np.ascontiguousarray(np.tile(gcw[None, :1024], (128, 1)))
    out["gcw2rep"] = np.ascontiguousarray(np.tile(gcw[None, 1024:], (128, 1)))
    out["gcbrep"] = np.full((128, 1), float(inp["L0_ffn_gc_b"][0]), np.float32)
    aw1 = inp["L0_ffn_aw1"]; aw2 = inp["L0_ffn_aw2"]
    ab1 = inp["L0_ffn_ab1"]; ab2 = inp["L0_ffn_ab2"]
    ew1s = np.concatenate([aw1[None, :, 0:2048], aw1[None, :, 2048:4096],
                           inp["L0_ffn_ew1"]], 0)
    ew2s = np.concatenate([aw2[None, 0:2048, :], aw2[None, 2048:4096, :],
                           inp["L0_ffn_ew2"]], 0)
    out["ew1s"] = np.ascontiguousarray(ew1s)
    out["ew2s"] = np.ascontiguousarray(ew2s)
    eb1s = np.concatenate([ab1[None, 0:2048], ab1[None, 2048:4096], inp["L0_ffn_eb1"]], 0)
    out["eb1c"] = np.ascontiguousarray(eb1s.reshape(8, 16, 128).transpose(0, 2, 1))
    eb2s = np.concatenate([ab2[None, :], np.zeros((1, 1024), np.float32), inp["L0_ffn_eb2"]], 0)
    out["eb2rep"] = np.ascontiguousarray(np.tile(eb2s[:, None, :], (1, 128, 1)))
    g = inp["L0_ffn_gens"]
    out["gensE"] = np.ascontiguousarray(np.stack([g[3], g[0].T, g[1].T, g[2].T]))
    out["gensET"] = np.ascontiguousarray(np.stack([g[3].T, g[0], g[1], g[2]]))
    out["idIbf"] = np.eye(1024).astype(ml_dtypes.bfloat16)
    ii = np.arange(128)
    rows = []
    for gg in range(4):
        rows.append(gg * 16 + ((ii // 8) % 2) * 8 + ii % 8)
    for gg in range(4):
        rows.append(gg * 128 + ii)
    rows.append(768 * (ii // 8 + 1))
    rows.append(256 * (ii // 8))
    for dd in range(8):
        rows.append(8 * ii + dd)
    rows.append(416 * (ii // 8 + 1))
    out["sctab"] = np.stack(rows).astype(np.int32)
    return out


class Ctr:
    def __init__(self):
        self.v = {}
    def inc(self, sem, n=1):
        k = id(sem)
        self.v[k] = self.v.get(k, 0) + n
        return self.v[k]
    def val(self, sem):
        return self.v.get(id(sem), 0)




def dram3(ap2d, p=128):
    return ap2d.rearrange("(j p) c -> p j c", p=p)


def sbuf3(tile_slice, cwidth):
    return tile_slice.rearrange("p (j c) -> p j c", c=cwidth)

class Prog:
    def __init__(self, nc, dram):
        self.nc = nc
        self.B = dram
        self._keep = []

    def sb(self, name, shape, dtype):
        t = self.nc.sbuf_tensor(name, shape, dtype)
        h = t.__enter__(); self._keep.append(t)
        setattr(self, "h_" + name, h)
        return h

    def ps(self, name, shape, dtype=F32):
        t = self.nc.psum_tensor(name, shape, dtype)
        h = t.__enter__(); self._keep.append(t)
        setattr(self, "h_" + name, h)
        return h

    def sem(self, name):
        t = self.nc.semaphore(name)
        h = t.__enter__(); self._keep.append(t)
        setattr(self, "h_" + name, h)
        return h


def p0_emit(p, tiles=None):
    B = p.B
    if tiles is None:
        xt_ = p.sb("p0_xt", [128, 2 * 8192], BF16)
        tdb_ = p.sb("p0_td", [128, 2 * 8192], BF16)
        xth = [xt_[:, 0:8192], xt_[:, 8192:16384]]
        tdh = [tdb_[:, 0:8192], tdb_[:, 8192:16384]]
    else:
        xth = tiles["xt"]
        tdh = tiles["xt"]
    s_cast = [p.sem("p0_cast0"), p.sem("p0_cast1")]
    s_out = [p.sem("p0_out0"), p.sem("p0_out1")]
    s_td = p.sem("p0_td")
    s_fm = [p.sem("p0_fm0"), p.sem("p0_fm1")]

    def gps(g):
        for h in range(8):
            if h >= 2:
                g.wait_ge(s_out[h % 2], 16 * (h // 2))
            src = dram3(B["x"][h * 1024:(h + 1) * 1024, :])
            g.dma_start(out=sbuf3(xth[h % 2], 1024), in_=src).then_inc(s_cast[h % 2], 16)

    def sync(q):
        for h in range(8):
            q.wait_ge(s_cast[h % 2], 16 * (h // 2 + 1))
            dst = dram3(B["xbf"][h * 1024:(h + 1) * 1024, :])
            q.dma_start(out=dst, in_=sbuf3(xth[h % 2], 1024)).then_inc(s_out[h % 2], 16)
        q.wait_ge(s_out[0], 16 * 4)
        q.wait_ge(s_out[1], 16 * 4)
        for c in range(8):
            if c >= 2:
                q.wait_ge(s_fm[c % 2], 16 * (c // 2))
            q.dma_start(out=tdh[c % 2],
                        in_=B["xbf"][:, 128 * c:128 * (c + 1)], transpose=True).then_inc(s_td, 16)
            q.wait_ge(s_td, 16 * (c + 1))
            q.dma_start(out=B["xfm"][128 * c:128 * (c + 1), :],
                        in_=tdh[c % 2]).then_inc(s_fm[c % 2], 16)
        q.wait_ge(s_fm[0], 16 * 4)
        q.wait_ge(s_fm[1], 16 * 4)

    return {"gpsimd": gps, "sync": sync}


def p1_emit(p, nproj=4, do_lr=True, do_lrq=True, tiles=None):
    B = p.B
    if tiles is None:
        xin_ = p.sb("p1_xin", [128, 2 * 4096], BF16)
        wsb_ = p.sb("p1_w", [128, 2 * 8192], BF16)
        stg_ = p.sb("p1_stg", [128, 2 * 4096], BF16)
        gtmp_ = p.sb("p1_gt", [128, 3 * 4096], BF16)
        lrs_ = p.sb("lrs_fm", [16, 8192], BF16)
        xinh = [xin_[:, 0:4096], xin_[:, 4096:8192]]
        wsbh = [wsb_[:, 0:8192], wsb_[:, 8192:16384]]
        stgh = [stg_[:, 0:4096], stg_[:, 4096:8192]]
        gth = [gtmp_[:, 0:4096], gtmp_[:, 4096:8192], gtmp_[:, 8192:12288]]
        lrsq = [lrs_[0:16, 2048 * q:2048 * (q + 1)] for q in range(4)]
        etsb = p.sb("p1_et", [16, 2 * 16 * 128], BF16)
        lrw = p.sb("p1_lrw", [128, 128], BF16)
        lrbt = p.sb("p1_lrb", [16, 1], F32)
        ppp_ = p.ps("p1_ps", [128, 2048])
        pppv = [ppp_[:, 512 * j:512 * (j + 1)] for j in range(4)]
        plr = p.ps("p1_plr", [16, 512])
        plq = p.ps("p1_plq", [128, 128])
    else:
        xinh = tiles["xin"]
        wsbh = tiles["wsb"]
        stgh = tiles["stg"]
        gth = tiles["gtmp"]
        lrsq = tiles["lrsq"]
        etsb = tiles["etsb"]
        lrw = tiles["lrw"]
        lrbt = tiles["lrbt"]
        pppv = tiles["ppp"]
        plr = tiles["plr"]
        plq = tiles["plq"]
    lrq = p.sb("lrq", [128, 8 * 128], F32)
    tokcol = p.sb("tokcol", [128, 4], F32)

    def lrsv(tt):
        return lrsq[tt // 4][:, (tt % 4) * 512:(tt % 4) * 512 + 512]
    s_in = [p.sem("p1_in0"), p.sem("p1_in1")]
    s_w = p.sem("p1_w")
    s_t = p.sem("p1_t")
    s_ea = p.sem("p1_ea")
    s_ev = p.sem("p1_ev")
    s_o = [p.sem("p1_o0"), p.sem("p1_o1")]
    s_gd = p.sem("p1_gd")
    s_gs = p.sem("p1_gs")
    s_gf = p.sem("p1_gf")
    s_lr = p.sem("p1_lr")
    s_lq = p.sem("p1_lq")
    PROJ = ["xqfm", "xkfm", "xvfm", "gfm"]
    WKEY = ["wqT", "wkT", "wvT", "wgT"]
    UPT = 4
    UPP = NTT * UPT

    def _ev_wait(u):
        pr = u // UPP
        if pr == 0:
            return ("a", u + 1)
        if pr in (1, 2):
            return ("v", u - UPP + 1)
        it = u // UPT; tt = it - 3 * NTT; k = u % UPT
        if k < 3:
            return ("a", UPP + tt * 3 + k + 1)
        return ("g", 2 * tt + 1)

    def sync(q):
        for pr in range(nproj + (1 if do_lr else 0)):
            for tt in range(NTT):
                it = pr * NTT + tt
                if it >= 2:
                    q.wait_ge(s_t, UPT * (it - 1))
                src = dram3(B["xfm"][:, tt * 512:(tt + 1) * 512])
                q.dma_start(out=sbuf3(xinh[it % 2], 512), in_=src).then_inc(s_in[it % 2], 16)

    def gps(g):
        for pr in range(nproj):
            if pr >= 2:
                g.wait_ge(s_t, UPP * (pr - 1))
            if pr >= 1:
                g.wait_ge(s_w, 16 * pr)
            src = dram3(B[WKEY[pr]])
            g.dma_start(out=sbuf3(wsbh[pr % 2], 1024), in_=src).then_inc(s_w, 16)
        g.wait_ge(s_w, 16 * nproj)
        g.dma_start(out=sbuf3(lrw[:, :], 16), in_=dram3(B["lrwT"])).then_inc(s_w, 16)
        g.wait_ge(s_w, 16 * (nproj + 1))
        g.dma_start(out=lrbt[:, :], in_=B["lrb"].rearrange("(p o) -> p o", o=1)).then_inc(s_w, 16)
        g.wait_ge(s_w, 16 * (nproj + 2))
        g.dma_start(out=etsb[:, :].rearrange("h (a t o) -> h a t o", a=2, t=16), in_=B["et"].rearrange("a t h o -> h a t o")).then_inc(s_w, 16)
        g.wait_ge(s_w, 16 * (nproj + 3))
        g.dma_start(out=tokcol[:, 0:1], in_=B["lti_rep"]).then_inc(s_w, 16)
        g.wait_ge(s_w, 16 * (nproj + 4))
        g.dma_start(out=tokcol[:, 1:2], in_=B["invi"]).then_inc(s_w, 16)
        g.wait_ge(s_w, 16 * (nproj + 5))

    def tensor(t):
        u = 0
        for pr in range(nproj):
            t.wait_ge(s_w, 16 * (pr + 1))
            for tt in range(NTT):
                it = pr * NTT + tt
                t.wait_ge(s_in[it % 2], 16 * (it // 2 + 1))
                for k in range(UPT):
                    if u >= 2:
                        sk, val = _ev_wait(u - 2)
                        t.wait_ge({"a": s_ea, "v": s_ev, "g": s_gs}[sk], val)
                    for half in range(2):
                        oc = 2 * k + half
                        for cc in range(8):
                            mm = t.matmul(pppv[(u % 2) * 2 + half],
                                     wsbh[pr % 2][:, cc * 1024 + oc * 128:cc * 1024 + oc * 128 + 128],
                                     xinh[it % 2][:, cc * 512:cc * 512 + 512],
                                     start=(cc == 0), stop=(cc == 7))
                            if half == 1 and cc == 7:
                                mm.then_inc(s_t, 1)
                    u += 1
        if not do_lr:
            return
        t.wait_ge(s_w, 16 * (nproj + 2))
        for tt in range(NTT):
            it = nproj * NTT + tt
            t.wait_ge(s_in[it % 2], 16 * (it // 2 + 1))
            if tt >= 1:
                t.wait_ge(s_lr, tt)
            for cc in range(8):
                mm = t.matmul(plr[:, :], lrw[:, cc * 16:cc * 16 + 16],
                         xinh[it % 2][:, cc * 512:cc * 512 + 512],
                         start=(cc == 0), stop=(cc == 7))
                if cc == 7:
                    mm.then_inc(s_t, UPT)
        if not do_lrq:
            return
        t.wait_ge(s_w, 16 * (nproj + 3))
        t.wait_ge(s_lr, NTT)
        for g_ in range(8):
            b, hs = g_ // 2, g_ % 2
            if g_ >= 1:
                t.wait_ge(s_lq, g_)
            for tk in range(16):
                rhs = lrsq[b][:, bass.DynSlice(tk, 128, 16)]
                mm = t.matmul(plq[:, :], etsb[:, (hs * 16 + tk) * 128:(hs * 16 + tk) * 128 + 128],
                         rhs, start=(tk == 0), stop=(tk == 15))
                if tk == 15:
                    mm.then_inc(s_t, 1)

    def vector(v):
        for pr in [q for q in (1, 2) if q < nproj]:
            for tt in range(NTT):
                it = pr * NTT + tt
                for k in range(UPT):
                    u = it * UPT + k
                    v.wait_ge(s_t, u + 1)
                    if k == 0 and it >= 2:
                        v.wait_ge(s_o[it % 2], 16 * (it // 2))
                    v.tensor_copy(stgh[it % 2][:, k * 1024:k * 1024 + 512], pppv[(u % 2) * 2])
                    v.drain()
                    v.tensor_copy(stgh[it % 2][:, k * 1024 + 512:k * 1024 + 1024],
                                  pppv[(u % 2) * 2 + 1]).then_inc(s_ev, 1)
        pr = 3
        for tt in (range(NTT) if nproj > 3 else []):
            it = pr * NTT + tt
            zb = stgh[it % 2][:, 0:4096]
            gu = gth[0][:, 0:4096]
            gt1 = gth[1][:, 0:4096]
            v.wait_ge(s_gs, 2 * tt + 1)
            v.tensor_tensor(gu, zb, zb, OP.mult)
            v.drain()
            v.tensor_scalar(gt1, gu, SC_B, SC_A, OP.mult, OP.add)
            v.drain()
            v.tensor_tensor(gth[2][:, 0:4096], gt1, zb, OP.mult).then_inc(s_gd, 1)
            v.wait_ge(s_gs, 2 * tt + 2)
            v.tensor_tensor(zb, gth[2][:, 0:4096], zb, OP.mult).then_inc(s_gf, 1)
        if not do_lr:
            return
        v.wait_ge(s_w, 16 * (nproj + 5))
        v.tensor_tensor(tokcol[:, 1:2], tokcol[:, 0:1], tokcol[:, 1:2], OP.add)
        v.drain()
        v.tensor_scalar(tokcol[:, 0:1], tokcol[:, 1:2], 0.0, None, OP.max)
        v.drain()
        v.tensor_scalar(tokcol[:, 2:3], tokcol[:, 0:1], -0.5, None, OP.mult)
        for g_ in (range(8) if do_lrq else []):
            v.wait_ge(s_t, nproj * UPP + NTT * UPT + g_ + 1)
            v.tensor_scalar(lrq[:, g_ * 128:g_ * 128 + 128], plq[:, :], 1.0 / 64, None, OP.mult).then_inc(s_lq, 1)

    def scalar(s):
        for pr in range(nproj):
            for tt in range(NTT):
                it = pr * NTT + tt
                if pr == 0:
                    for k in range(UPT):
                        u = it * UPT + k
                        s.wait_ge(s_t, u + 1)
                        if k == 0 and it >= 2:
                            s.wait_ge(s_o[it % 2], 16 * (it // 2))
                        s.activation(stgh[it % 2][:, k * 1024:k * 1024 + 512],
                                     pppv[(u % 2) * 2], AF.Identity)
                        s.activation(stgh[it % 2][:, k * 1024 + 512:k * 1024 + 1024],
                                     pppv[(u % 2) * 2 + 1], AF.Identity).then_inc(s_ea, 1)
                elif pr == 3:
                    for k in range(UPT):
                        u = it * UPT + k
                        s.wait_ge(s_t, u + 1)
                        if k == 0 and it >= 2:
                            s.wait_ge(s_o[it % 2], 16 * (it // 2))
                        s.activation(stgh[it % 2][:, k * 1024:k * 1024 + 512],
                                     pppv[(u % 2) * 2], AF.Identity)
                        a = s.activation(stgh[it % 2][:, k * 1024 + 512:k * 1024 + 1024],
                                     pppv[(u % 2) * 2 + 1], AF.Identity)
                        a.then_inc(s_gs if k == 3 else s_ea, 1)
                    s.wait_ge(s_gd, tt + 1)
                    s.activation(gth[2][:, 0:4096], gth[2][:, 0:4096], AF.Sigmoid).then_inc(s_gs, 1)
                if pr == 0:
                    s.wait_ge(s_ea, 4 * (tt + 1))
                elif pr in (1, 2):
                    s.wait_ge(s_ev, (pr - 1) * UPP + 4 * (tt + 1))
                else:
                    s.wait_ge(s_gf, tt + 1)
                dst = dram3(B[PROJ[pr]][:, tt * 512:(tt + 1) * 512])
                s.dma_start(out=dst, in_=sbuf3(stgh[it % 2][:, 0:4096], 512)).then_inc(s_o[it % 2], 16)
        for tt in (range(NTT) if do_lr else []):
            s.wait_ge(s_t, nproj * UPP + (tt + 1) * UPT)
            s.activation(lrsv(tt), plr[:, :], AF.Sigmoid,
                         bias=lrbt[:, 0:1]).then_inc(s_lr, 1)
        s.wait_ge(s_o[0], 16 * NTT * nproj // 2)
        s.wait_ge(s_o[1], 16 * NTT * nproj // 2)
        if do_lrq:
            s.wait_ge(s_lq, 8)

    return {"sync": sync, "gpsimd": gps, "tensor": tensor, "vector": vector, "scalar": scalar}




F32 = mybir.dt.float32
BF16 = mybir.dt.bfloat16
AF = mybir.ActivationFunctionType
OP = AluOpType
C1 = 0.79788456
C2 = 0.044715
SC_A = 2 * C1
SC_B = 2 * C1 * C2
C4 = 0.1070322243
EPS = 1e-6


def ts(i, n):
    return bass.ts(i, n)


def scan_alloc(p, ng=4):
    a = {}
    a["ng"] = ng
    for t, w in [("w1f", 1024), ("w2tf", 1024), ("w2cf", 1024)]:
        a[t] = p.sb("sc_" + t, [128, ng * w], F32)
    for t, w in [("w1b", 1024), ("w2tb", 1024), ("w2cb", 1024)]:
        a[t] = p.sb("sc_" + t, [128, ng * w], BF16)
    a["b1rep"] = p.sb("sc_b1rep", [128, ng * 256], F32)
    a["b2rep"] = p.sb("sc_b2rep", [128, ng * 128], F32)
    a["lnw"] = p.sb("sc_lnw", [128, 2 * 128], F32)
    a["lnb"] = p.sb("sc_lnb", [128, 2 * 128], F32)
    a["bd01"] = p.sb("sc_bd01", [128, 128], F32)
    a["triltnT"] = p.sb("sc_triltnT", [128, 128], F32)
    a["ident"] = p.sb("sc_ident", [128, 128], BF16)
    a["tril01b"] = p.sb("sc_tril01b", [128, 128], BF16)
    a["ident4"] = p.sb("sc_ident4", [128, 32], BF16)
    a["tgtf"] = p.sb("sc_tgtf", [128, 0], BF16) if False else p.sb("sc_tgtf", [128, ng * 64], BF16)
    a["tgtbdt"] = p.sb("sc_tgtbdt", [128, ng * 128], BF16)
    a["onest15"] = p.sb("sc_onest15", [128, 128], BF16)
    a["t15n"] = p.sb("sc_t15n", [128, 1], F32)
    a["epsc"] = p.sb("sc_epsc", [128, 1], F32)
    a["xw3"] = p.sb("sc_xw3", [128, 3 * ng * 1024], BF16)
    a["xkw"] = a["xw3"][:, 0:ng * 1024]
    a["xqw"] = a["xw3"][:, ng * 1024:2 * ng * 1024]
    a["xvw"] = a["xw3"][:, 2 * ng * 1024:3 * ng * 1024]
    a["scur"] = p.sb("sc_scur", [128, ng * 192], BF16)
    a["bdxk"] = p.sb("sc_bdxk", [128, ng * 128], BF16)
    a["bdxq"] = p.sb("sc_bdxq", [128, ng * 128], BF16)
    a["bdtgt"] = p.sb("sc_bdtgt", [128, ng * 128], BF16)
    a["z1"] = p.sb("sc_z1", [128, ng * 256], BF16)
    a["u"] = p.sb("sc_u", [128, ng * 256], BF16)
    a["wv"] = p.sb("sc_wv", [128, ng * 256], BF16)
    a["sg"] = p.sb("sc_sg", [128, ng * 256], BF16)
    a["x2"] = p.sb("sc_x2", [128, ng * 256], BF16)
    a["x2s"] = p.sb("sc_x2s", [128, ng * 256], BF16)
    a["ga"] = p.sb("sc_ga", [128, ng * 256], BF16)
    a["gb"] = p.sb("sc_gb", [128, ng * 256], BF16)
    a["gz1s"] = p.sb("sc_gz1s", [128, ng * 256], BF16)
    a["x2t"] = p.sb("sc_x2t", [128, ng * 256], BF16)
    a["z1b"] = p.sb("sc_z1b", [128, ng * 256], BF16)
    a["u2"] = p.sb("sc_u2", [128, ng * 256], BF16)
    a["sg2"] = p.sb("sc_sg2", [128, ng * 256], BF16)
    a["x2b"] = p.sb("sc_x2b", [128, ng * 256], BF16)
    a["x2bt"] = p.sb("sc_x2bt", [128, ng * 256], BF16)
    a["z2c"] = p.sb("sc_z2c", [128, ng * 128], F32)
    a["xh"] = p.sb("sc_xh", [128, ng * 128], F32)
    a["gx"] = p.sb("sc_gx", [128, ng * 128], F32)
    a["gxs"] = p.sb("sc_gxs", [128, ng * 128], F32)
    a["gz2s"] = p.sb("sc_gz2s", [128, ng * 128], BF16)
    a["gz2bdt"] = p.sb("sc_gz2bdt", [128, ng * 128], BF16)
    a["xkbdts"] = p.sb("sc_xkbdts", [128, ng * 128], BF16)
    a["gz1bd"] = p.sb("sc_gz1bd", [128, ng * 512], BF16)
    a["x2sbd"] = p.sb("sc_x2sbd", [128, ng * 512], BF16)
    a["gz2sbd"] = p.sb("sc_gz2sbd", [128, ng * 256], BF16)
    a["z2bc"] = p.sb("sc_z2bc", [128, ng * 128], F32)
    a["xh2"] = p.sb("sc_xh2", [128, ng * 128], F32)
    a["lnout"] = p.sb("sc_lnout", [128, ng * 128], BF16)
    a["cc1"] = p.sb("sc_cc1", [128, ng * 128], BF16)
    a["cc2"] = p.sb("sc_cc2", [128, ng * 128], BF16)
    a["cols"] = p.sb("sc_cols", [128, ng * 16], F32)
    a["dep"] = p.sb("sc_dep", [128, ng * 2 * 512], BF16)
    a["pz1s"] = p.ps("sc_pz1s", [128, 512])
    a["pzb1"] = p.ps("sc_pzb1", [128, 512])
    a["pgz1"] = p.ps("sc_pgz1", [128, 512])
    a["pz2s"] = p.ps("sc_pz2s", [128, 512])
    a["pw"] = p.ps("sc_pw", [128, 512])
    a["pab"] = p.ps("sc_pab", [128, 512])
    a["ptT"] = p.ps("sc_ptT", [128, 2048], BF16)
    a["sctab"] = p.sb("sctabs", [19, 128], mybir.dt.int32)
    a["lrqc4"] = p.sb("sc_lrqc4", [128, 8], F32)
    for s in ["s_win", "s_dep", "s_pe", "s_v", "s_a", "s_pl", "s_ini", "s_iv", "s_ipe", "s_ia"]:
        a[s] = p.sem("sc_" + s)
    return a




@contextmanager
def fori_shared(h, start, end):
    b = h.bass
    from concourse.bass import OrderedEngineSet
    if not hasattr(h, "_loopregs"):
        h._loopregs = b.alloc_registers("shloop_i", engines=OrderedEngineSet([h.engine]))
    regs = h._loopregs
    loop_id = b.next_id()
    loop_start = f"shfori_{loop_id}_loop"
    loop_end = f"shfori_{loop_id}_end"
    engines = OrderedEngineSet([h.engine])
    b.regs_mov(regs, start)
    b.br(loop_start, engines=engines)
    with b.body(loop_start, valid_engines=engines):
        yield b.snap(regs, donate=True, min_val=start, max_val=end - 1)
        b.regs_alu(regs, regs, 1, op=mybir.AluOpType.add)
        b.br_lt(regs, end, on_true=loop_start, on_false=loop_end, engines=engines)
    b.switch_bb(loop_end)


def prep_vals(h, a, i):
    b = h.bass
    from concourse.bass import OrderedEngineSet
    if not hasattr(h, "_xcore"):
        h._xcore = b.alloc_registers("xcore", engines=OrderedEngineSet([h.engine]))
        h._xtmp = b.alloc_registers("xtmp", engines=OrderedEngineSet([h.engine]))
    b.regs_alu(h._xcore, i, 8, op=mybir.AluOpType.python_divide)
    b.regs_alu(h._xcore, h._xcore, 2, op=mybir.AluOpType.mod)
    b.regs_alu(h._xcore, h._xcore, 8, op=mybir.AluOpType.mult)
    b.regs_alu(h._xtmp, i, 8, op=mybir.AluOpType.mod)
    b.regs_alu(h._xcore, h._xcore, h._xtmp, op=mybir.AluOpType.add)
    a["_corev"] = b.snap(h._xcore, donate=True, min_val=0, max_val=15)
    a["_iv"] = i


def wdiv8(h, i, mul, const, sem):
    if isinstance(i, int):
        v = (i // 8) * mul + const
        if v > 0:
            h.wait_ge(sem, v)
        return
    b = h.bass
    from concourse.bass import OrderedEngineSet
    if not hasattr(h, "_w8"):
        h._w8 = b.alloc_registers("w8", engines=OrderedEngineSet([h.engine]))
    b.regs_alu(h._w8, i, 8, op=mybir.AluOpType.python_divide)
    b.regs_alu(h._w8, h._w8, mul, op=mybir.AluOpType.mult)
    b.regs_alu(h._w8, h._w8, const, op=mybir.AluOpType.add)
    h.wait_ge(sem, b.snap(h._w8, donate=True, min_val=0, max_val=1 << 30))


def shared_regs(h):
    if not hasattr(h, "_p34regs"):
        h._p34regs = {t: h.alloc_register("shr_" + t)
                      for t in ("pe", "v", "a", "pl", "xw", "xd", "xs", "_w")}
    return h._p34regs


ENG_SEM = {"pe": "s_pe", "v": "s_v", "a": "s_a", "pl": "s_pl"}


class Sched:
    def __init__(self):
        self.ops = []
        self.idx = {}
        self.S = {}
        self.base = {}

    def add(self, eng, name, fn, deps=()):
        self.ops.append((eng, name, fn, list(deps)))

    def finalize(self):
        cnt = {}
        for eng, name, fn, deps in self.ops:
            cnt[eng] = cnt.get(eng, 0) + 1
            self.idx[name] = (eng, cnt[eng])
        self.S = cnt

    def wait_val(self, dep_name, delta, i):
        eng, k = self.idx[dep_name]
        return eng, (i + delta) * self.S[eng] + k

    def alloc_regs(self, h, eng):
        self._regs = getattr(self, "_regs", {})
        self._regs[eng] = shared_regs(h)

    def reg_init(self, h, eng, i0):
        rs = self._regs[eng]
        for t in ("pe", "v", "a", "pl"):
            h.reg_mov(rs[t], i0 * self.S.get(t, 0))
        for t, c in getattr(self, "aux", {}).items():
            h.reg_mov(rs[t], i0 * c)

    def reg_step(self, h, eng):
        rs = self._regs[eng]
        for t in ("pe", "v", "a", "pl"):
            if self.S.get(t, 0):
                h.reg_alu(rs[t], rs[t], self.S[t], mybir.AluOpType.add)
        for t, c in getattr(self, "aux", {}).items():
            h.reg_alu(rs[t], rs[t], c, mybir.AluOpType.add)

    def prep(self, h, eng, i):
        raise RuntimeError("use reg_init/reg_step")

    def emit_engine(self, h, eng, a, i):
        for e, name, fn, deps in self.ops:
            if e != eng:
                continue
            for (dn, delta) in deps:
                deng, k = self.idx[dn]
                S = self.S[deng]
                b = self.base.get(deng, 0)
                if isinstance(i, int):
                    val = b + (i + delta) * S + k
                    if val <= 0:
                        continue
                    h.wait_ge(a[ENG_SEM[deng]], val)
                else:
                    rs = self._regs[eng]
                    h.reg_add(rs["_w"], rs[deng], b + delta * S + k)
                    h.wait_ge(a[ENG_SEM[deng]], rs["_w"])
            instr = fn(h, i)
            if instr is not None:
                instr.then_inc(a[ENG_SEM[eng]], 1)
            else:
                h.nop().then_inc(a[ENG_SEM[eng]], 1)
            if eng in ("v", "pl", "a"):
                h.drain()


def build_sched(a, b_of, hs_of):
    ng = a["ng"]
    sc = Sched()
    G = list(range(ng))
    DS = bass.DynSlice

    def njs(t, g, w=256):
        return a[t][:, g * w:(g + 1) * w]

    def bdt(t, g):
        return a[t][:, g * 128:g * 128 + 128]

    def col(g, k):
        return a["cols"][:, g * 16 + k:g * 16 + k + 1]

    def xidx(g, i):
        assert isinstance(i, int)
        return g * 16 + ((i // 8) % 2) * 8 + (i % 8)

    def winx(t, g, i, r0=0, r1=128):
        if isinstance(i, int):
            v = a[t][r0:r1, :].rearrange("p (x q c) -> p x q c", q=4, c=16)
            return v[:, DS(xidx(g, i), 1), :, :]
        v = a[t][r0:r1, g * 1024:(g + 1) * 1024].rearrange("p (x q c) -> p x q c", q=4, c=16)
        return v[:, DS(a["_corev"], 1), :, :]

    def winf(t, g, i, r0=0, r1=128):
        if isinstance(i, int):
            v = a[t][r0:r1, :].rearrange("p (x s) -> p x s", s=64)
            return v[:, DS(xidx(g, i), 1), :]
        v = a[t][r0:r1, g * 1024:(g + 1) * 1024].rearrange("p (x s) -> p x s", s=64)
        return v[:, DS(a["_corev"], 1), :]

    def pi(i, g):
        return (i * ng + g) % 2

    def pz1(i, g): return a["pz1s"][:, 0:256]
    def pg1_(i, g): return a["pgz1"][:, 0:256]
    def pzb(i, g): return a["pzb1"][:, 0:256]
    def pz2(i, g): return a["pz2s"][:, 0:128]
    def pz2b(i, g): return a["pgz1"][:, 256:384]
    def pa_(g): return a["pab"][:, (g % 2) * 128:(g % 2) * 128 + 128]
    def pb1_(): return a["pab"][:, 256:512]

    def ptA(i, g, r):
        return a["ptT"][:, pi(i, g) * 1024 + r * 128:pi(i, g) * 1024 + r * 128 + 128]

    def ptB(i, g, r):
        return a["ptT"][:, pi(i, g) * 1024 + 512 + r * 256:pi(i, g) * 1024 + 512 + r * 256 + 256]

    def prevdep(evac, g, ww=0):
        if g >= 1:
            return (evac + str(g - 1), 0 + ww)
        return (evac + str(ng - 1), -1 + ww)

    def slotdep(evac, g, ww=0):
        if g >= 2:
            return (evac + str(g - 2), 0 + ww)
        return (evac + str(g + ng - 2), -1 + ww)

    NW = 3 * ng

    for g in G:
        def f_wcur(h, i, g=g):
            wdiv8(h, i, 16 * 4 * NW, a.get("_sc_w0", 0) + 16 * 4 * NW, a["s_win"])
            dst = a["scur"][:, g * 192:(g + 1) * 192].rearrange("p (t o s) -> p t o s", t=3, o=1)
            v3 = a["xw3"].rearrange("p (t g x s) -> p t g x s", t=3, g=a["ng"], s=64)
            if isinstance(i, int):
                src = v3[:, :, g, DS(xidx(0, i), 1), :]
            else:
                src = v3[:, :, g, DS(a["_corev"], 1), :]
            return h.tensor_copy(dst, src)
        sc.add("pl", "pl_wcur" + str(g), f_wcur,
               deps=[("pe_z1" + str(g), -1), ("pl_bdtg" + str(g), -1), ("v_dep2" + str(g), -1)])

    def scur(g, which, r0=0, r1=128):
        off = {"xk": 0, "xq": 64, "xv": 128}[which]
        return a["scur"][r0:r1, g * 192 + off:g * 192 + off + 64]

    for g in G:
        def f_bdk(h, i, g=g):
            d0 = bdt("bdxk", g)[0:64, :].rearrange("p (q c) -> p q c", c=32)
            d1 = bdt("bdxk", g)[64:128, :].rearrange("p (q c) -> p q c", c=32)
            h.tensor_copy(d0[:, :, 0:16], scur(g, "xk", 0, 64).rearrange("p (q t) -> p q t", t=16))
            return h.tensor_copy(d1[:, :, 16:32], scur(g, "xk", 64, 128).rearrange("p (q t) -> p q t", t=16))
        sc.add("pl", "pl_bdk" + str(g), f_bdk, deps=[("pe_z1" + str(g), -1), ("pl_wcur" + str(g), 0)])
    for g in G:
        def f_bdq(h, i, g=g):
            d0 = bdt("bdxq", g)[0:64, :].rearrange("p (q c) -> p q c", c=32)
            d1 = bdt("bdxq", g)[64:128, :].rearrange("p (q c) -> p q c", c=32)
            h.tensor_copy(d0[:, :, 0:16], scur(g, "xq", 0, 64).rearrange("p (q t) -> p q t", t=16))
            return h.tensor_copy(d1[:, :, 16:32], scur(g, "xq", 64, 128).rearrange("p (q t) -> p q t", t=16))
        sc.add("pl", "pl_bdq" + str(g), f_bdq, deps=[("pe_zb1" + str(g), -1), ("pl_wcur" + str(g), 0)])
    for g in G:
        def f_tgtf(h, i, g=g):
            return h.tensor_tensor(a["tgtf"][:, g * 64:g * 64 + 64], scur(g, "xv"), scur(g, "xk"), OP.subtract)
        sc.add("pl", "pl_tgtf" + str(g), f_tgtf, deps=[("pe_ttgt" + str(g), -1), ("pl_wcur" + str(g), 0)])
    for g in G:
        def f_bdtg(h, i, g=g):
            d = bdt("bdtgt", g).rearrange("p (q c) -> p q c", c=32)
            s = a["tgtf"][:, g * 64:g * 64 + 64].rearrange("p (q t) -> p q t", t=16)
            h.tensor_copy(d[0:64, :, 0:16], s[0:64])
            return h.tensor_copy(d[64:128, :, 16:32], s[64:128])
        sc.add("pl", "pl_bdtg" + str(g), f_bdtg, deps=[("pe_ttgt" + str(g), -1)])
    for g in G:
        def f_sig1(h, i, g=g):
            return h.tensor_scalar(njs("sg", g), njs("sg", g), 1.0, None, OP.add)
        sc.add("pl", "pl_sig1" + str(g), f_sig1, deps=[("a_exp1" + str(g), 0)])
    for g in G:
        def f_x2s(h, i, g=g):
            return h.tensor_scalar(njs("x2s", g), njs("x2", g), a["t15n"][:, 0:1], None, OP.mult)
        sc.add("pl", "pl_x2s" + str(g), f_x2s, deps=[("v_x2" + str(g), 0), ("pe_w2cub" + str(g), -1)])
    for g in G:
        def f_sig2(h, i, g=g):
            return h.tensor_scalar(njs("sg2", g), njs("sg2", g), 1.0, None, OP.add)
        sc.add("pl", "pl_sig2" + str(g), f_sig2, deps=[("a_exp2" + str(g), 0)])
    for g in G:
        def f_cast2(h, i, g=g):
            return h.tensor_copy(a["w2tb"][:, g * 1024:(g + 1) * 1024], a["w2tf"][:, g * 1024:(g + 1) * 1024])
        sc.add("pl", "pl_cast2" + str(g), f_cast2, deps=[("v_wrm2b" + str(g), 0), ("pe_gz1" + str(g), 0)])
    for g in G:
        def f_cast3(h, i, g=g):
            return h.tensor_copy(a["w2cb"][:, g * 1024:(g + 1) * 1024], a["w2cf"][:, g * 1024:(g + 1) * 1024])
        sc.add("pl", "pl_cast3" + str(g), f_cast3, deps=[("v_wrm3b" + str(g), 0), ("pe_zb2" + str(g), 0)])

    _EV = {"a1": "v_cc1a", "a2": "v_cc2a", "b2": "v_b2rm"}

    def pa_prev(cls, g):
        s = g % 2
        chain = [("a1", gg) for gg in G if gg % 2 == s]
        for gg in G:
            if gg % 2 == s:
                chain += [("a2", gg), ("b2", gg)]
        k = chain.index((cls, g))
        if k == 0:
            pc, pg2 = chain[-1]
            return (_EV[pc] + str(pg2), -1)
        pc, pg2 = chain[k - 1]
        return (_EV[pc] + str(pg2), 0)

    for g in G:
        def f_a1(h, i, g=g):
            r = None
            for q in range(4):
                r = h.matmul(a["pab"][q * 32:q * 32 + 32, (g % 2) * 128 + q * 32:(g % 2) * 128 + q * 32 + 32],
                             bdt("bdxk", g)[:, q * 32:q * 32 + 32],
                             bdt("bdxq", g)[:, q * 32:q * 32 + 32],
                             start=True, stop=True, tile_position=(0, q * 32))
            return r
        d = [("pl_bdk" + str(g), 0), ("pl_bdq" + str(g), 0), pa_prev("a1", g), prevdep("v_cc1a", g)]
        if g < 2:
            other_last = max(gg for gg in G if gg % 2 != g % 2) if ng > 1 else g
            d.append(("v_b2rm" + str(other_last), -1))
        sc.add("pe", "pe_a1" + str(g), f_a1, deps=d)
    for g in G:
        def f_tbdk(h, i, g=g):
            return h.transpose(ptA(i, g, 0), bdt("bdxk", g), a["ident"][:, :])
        sc.add("pe", "pe_tbdk" + str(g), f_tbdk,
               deps=[("pl_bdk" + str(g), 0), slotdep("a_exkt", g), slotdep("v_dep1", g),
                     slotdep("a_ex2bt", g)])
    for g in G:
        def f_ttgt(h, i, g=g):
            return h.transpose(ptA(i, g, 1), bdt("bdtgt", g), a["ident"][:, :])
        sc.add("pe", "pe_ttgt" + str(g), f_ttgt,
               deps=[("pl_bdtg" + str(g), 0), slotdep("a_etgt", g), ("a_exkt" + str(g), 0)])
    for g in G:
        def f_z1(h, i, g=g):
            r = None
            for q in range(4):
                r = h.matmul(pz1(i, g)[q * 32:q * 32 + 32, :],
                             bdt("bdxk", g)[:, q * 32:q * 32 + 32],
                             a["w1b"][:, g * 1024 + q * 256:g * 1024 + q * 256 + 256],
                             start=True, stop=True, tile_position=(0, q * 32))
            return r
        sc.add("pe", "pe_z1" + str(g), f_z1,
               deps=[("pl_bdk" + str(g), 0), ("a_cast1" + str(g), -1), prevdep("v_z1e", g)])
    for g in G:
        def f_tx2(h, i, g=g):
            h.transpose(ptB(i, g, 0)[:, 0:128], njs("x2", g)[:, 0:128], a["ident"][:, :])
            return h.transpose(ptB(i, g, 0)[:, 128:256], njs("x2", g)[:, 128:256], a["ident"][:, :])
        sc.add("pe", "pe_tx2" + str(g), f_tx2,
               deps=[("v_x2" + str(g), 0), slotdep("a_ex2t", g), slotdep("v_z1be", g),
                     ("a_etgt" + str(g), 0)])
    for g in G:
        def f_z2(h, i, g=g):
            r = None
            for q in range(4):
                for c in range(2):
                    r = h.matmul(pz2(i, g)[q * 32:q * 32 + 32, :],
                                 a["x2t"][:, g * 256 + c * 128 + q * 32:g * 256 + c * 128 + q * 32 + 32],
                                 a["w2cb"][:, g * 1024 + q * 256 + c * 128:g * 1024 + q * 256 + c * 128 + 128],
                                 start=(c == 0), stop=(c == 1), tile_position=(0, q * 32))
            return r
        sc.add("pe", "pe_z2" + str(g), f_z2,
               deps=[("a_ex2t" + str(g), 0), ("pl_cast3" + str(g), -1), prevdep("v_z2e", g)])
    for g in G:
        def f_tgz2(h, i, g=g):
            return h.transpose(ptA(i, g, 2), bdt("gz2s", g), a["ident"][:, :])
        sc.add("pe", "pe_tgz2" + str(g), f_tgz2,
               deps=[("v_gz2m" + str(g), 0), slotdep("a_egz2t", g), ("a_etgt" + str(g), 0)])
    for g in G:
        def f_gz1(h, i, g=g):
            r = None
            for q in range(4):
                r = h.matmul(pg1_(i, g)[q * 32:q * 32 + 32, :],
                             bdt("gz2bdt", g)[:, q * 32:q * 32 + 32],
                             a["w2tb"][:, g * 1024 + q * 256:g * 1024 + q * 256 + 256],
                             start=True, stop=True, tile_position=(0, q * 32))
            return r
        sc.add("pe", "pe_gz1" + str(g), f_gz1,
               deps=[("a_egz2t" + str(g), 0), ("pl_cast2" + str(g), -1), prevdep("v_gz1a", g),
                     prevdep("v_z2be", g)])
    for g in G:
        def f_zb1(h, i, g=g):
            r = None
            for q in range(4):
                h.matmul(pzb(i, g)[q * 32:q * 32 + 32, :],
                         bdt("cc1", g)[:, q * 32:q * 32 + 32],
                         njs("gz1s", g), start=True, stop=False, tile_position=(0, q * 32))
                r = h.matmul(pzb(i, g)[q * 32:q * 32 + 32, :],
                             bdt("bdxq", g)[:, q * 32:q * 32 + 32],
                             a["w1b"][:, g * 1024 + q * 256:g * 1024 + q * 256 + 256],
                             start=False, stop=True, tile_position=(0, q * 32))
            return r
        sc.add("pe", "pe_zb1" + str(g), f_zb1,
               deps=[("v_cc1b" + str(g), 0), ("v_gz1s" + str(g), 0), ("pl_bdq" + str(g), 0),
                     ("a_cast1" + str(g), -1), prevdep("v_z1be", g)])
    for g in G:
        def f_tx2b(h, i, g=g):
            h.transpose(ptB(i, g, 1)[:, 0:128], njs("x2b", g)[:, 0:128], a["ident"][:, :])
            return h.transpose(ptB(i, g, 1)[:, 128:256], njs("x2b", g)[:, 128:256], a["ident"][:, :])
        sc.add("pe", "pe_tx2b" + str(g), f_tx2b,
               deps=[("v_x2b" + str(g), 0), slotdep("a_ex2bt", g)])
    for g in G:
        def f_a2(h, i, g=g):
            r = None
            for q in range(4):
                for c in range(2):
                    r = h.matmul(a["pab"][q * 32:q * 32 + 32, (g % 2) * 128 + q * 32:(g % 2) * 128 + q * 32 + 32],
                                 a["x2t"][:, g * 256 + c * 128 + q * 32:g * 256 + c * 128 + q * 32 + 32],
                                 a["x2bt"][:, g * 256 + c * 128 + q * 32:g * 256 + c * 128 + q * 32 + 32],
                                 start=(c == 0), stop=(c == 1), tile_position=(0, q * 32))
            return r
        d = [("a_ex2bt" + str(g), 0), ("a_ex2t" + str(g), 0), pa_prev("a2", g)]
        sc.add("pe", "pe_a2" + str(g), f_a2, deps=d)
    for g in G:
        def f_zb2(h, i, g=g):
            r = None
            for q in range(4):
                h.matmul(pz2b(i, g)[q * 32:q * 32 + 32, :],
                         bdt("cc2", g)[:, q * 32:q * 32 + 32],
                         bdt("gz2s", g), start=True, stop=False, tile_position=(0, q * 32))
                for c in range(2):
                    r = h.matmul(pz2b(i, g)[q * 32:q * 32 + 32, :],
                                 a["x2bt"][:, g * 256 + c * 128 + q * 32:g * 256 + c * 128 + q * 32 + 32],
                                 a["w2cb"][:, g * 1024 + q * 256 + c * 128:g * 1024 + q * 256 + c * 128 + 128],
                                 start=False, stop=(c == 1), tile_position=(0, q * 32))
            return r
        sc.add("pe", "pe_zb2" + str(g), f_zb2,
               deps=[("v_cc2b" + str(g), 0), ("v_gz2m" + str(g), 0), ("a_ex2bt" + str(g), 0),
                     ("pl_cast3" + str(g), -1), prevdep("v_z2be", g), ("v_gz1a" + str(g), 0)])
    for g in G:
        def f_tdep(h, i, g=g):
            return h.transpose(ptA(i, g, 3)[:, :], bdt("lnout", g)[:, :], a["ident"][:, :])
        sc.add("pe", "pe_tdep" + str(g), f_tdep,
               deps=[("v_lo2" + str(g), 0), slotdep("v_dep1", g), ("a_egz2t" + str(g), 0)])
    for g in G:
        pwprev = ("v_wrm3b" + str(g - 1), 0) if g > 0 else ("v_wrm3b" + str(ng - 1), -1)
        def f_w1ua(h, i, g=g):
            return h.matmul(a["pw"][:, 0:512], bdt("xkbdts", g)[0:64, :],
                            a["gz1bd"][0:64, g * 512:(g + 1) * 512], start=True, stop=True)
        sc.add("pe", "pe_w1ua" + str(g), f_w1ua,
               deps=[("a_exkt" + str(g), 0), ("a_gz1bde" + str(g), 0),
                     ("a_gz1bdo" + str(g), 0), pwprev])
        def f_w1ub(h, i, g=g):
            return h.matmul(a["pw"][:, 0:512], bdt("xkbdts", g)[64:128, :],
                            a["gz1bd"][64:128, g * 512:(g + 1) * 512], start=True, stop=True,
                            tile_position=(64, 0))
        sc.add("pe", "pe_w1ub" + str(g), f_w1ub, deps=[("v_wrm1a" + str(g), 0)])
        def f_w2tua(h, i, g=g):
            return h.matmul(a["pw"][:, 0:512], a["gz2s"][0:64, g * 128:g * 128 + 128],
                            a["x2sbd"][0:64, g * 512:(g + 1) * 512], start=True, stop=True)
        sc.add("pe", "pe_w2tua" + str(g), f_w2tua,
               deps=[("v_wrm1b" + str(g), 0), ("a_x2sbde" + str(g), 0),
                     ("a_x2sbdo" + str(g), 0)])
        def f_w2tub(h, i, g=g):
            return h.matmul(a["pw"][:, 0:512], a["gz2s"][64:128, g * 128:g * 128 + 128],
                            a["x2sbd"][64:128, g * 512:(g + 1) * 512], start=True, stop=True,
                            tile_position=(64, 0))
        sc.add("pe", "pe_w2tub" + str(g), f_w2tub, deps=[("v_wrm2a" + str(g), 0)])
        def f_w2cua(h, i, g=g):
            r = None
            for c in range(2):
                r = h.matmul(a["pw"][:, c * 256:c * 256 + 256],
                             a["x2s"][0:64, g * 256 + c * 128:g * 256 + c * 128 + 128],
                             a["gz2sbd"][0:64, g * 256:(g + 1) * 256], start=True, stop=True)
            return r
        sc.add("pe", "pe_w2cua" + str(g), f_w2cua,
               deps=[("v_wrm2b" + str(g), 0), ("a_gz2sbde" + str(g), 0),
                     ("a_gz2sbdo" + str(g), 0), ("pl_x2s" + str(g), 0)])
        def f_w2cub(h, i, g=g):
            r = None
            for c in range(2):
                r = h.matmul(a["pw"][:, c * 256:c * 256 + 256],
                             a["x2s"][64:128, g * 256 + c * 128:g * 256 + c * 128 + 128],
                             a["gz2sbd"][64:128, g * 256:(g + 1) * 256], start=True, stop=True,
                             tile_position=(64, 0))
            return r
        sc.add("pe", "pe_w2cub" + str(g), f_w2cub, deps=[("v_wrm3a" + str(g), 0)])
        b1prev = ("v_b1rm" + str(g - 1), 0) if g > 0 else ("v_b1rm" + str(ng - 1), -1)
        def f_b1u(h, i, g=g):
            return h.matmul(pb1_(), a["onest15"][:, :], njs("gz1s", g), start=True, stop=True)
        sc.add("pe", "pe_b1u" + str(g), f_b1u,
               deps=[("v_gz1s" + str(g), 0), b1prev, ("v_wrm3b" + str(g), 0)])
        b2prev = pa_prev("b2", g)
        def f_b2u(h, i, g=g):
            return h.matmul(pa_(g), a["onest15"][:, :], bdt("gz2s", g), start=True, stop=True)
        sc.add("pe", "pe_b2u" + str(g), f_b2u, deps=[("v_gz2m" + str(g), 0), b2prev, ("v_b1rm" + str(g), 0)])
    def interleave(n1, n2):
        ops = sc.ops
        idx1 = [j for j, o in enumerate(ops) if o[1].startswith(n1)]
        idx2 = [j for j, o in enumerate(ops) if o[1].startswith(n2)]
        block = []
        for j1, j2 in zip(idx1, idx2):
            block.append(ops[j1]); block.append(ops[j2])
        lo = min(idx1 + idx2)
        keep = [o for j, o in enumerate(ops) if j not in set(idx1 + idx2)]
        sc.ops = keep[:lo] + block + keep[lo:]
    sc._pz1 = pz1; sc._pzb = pzb; sc._pg1 = pg1_; sc._pz2 = pz2; sc._pz2b = pz2b
    sc._pa = pa_; sc._pb1 = pb1_; sc._ptA = ptA; sc._ptB = ptB
    sc._njs = njs; sc._bdt = bdt; sc._col = col; sc._winf = winf; sc._winx = winx; sc._xidx = xidx; sc._scur = scur
    return sc


def add_va_ops(sc, a, b_of, hs_of):
    ng = a["ng"]
    G = list(range(ng))
    DS = bass.DynSlice
    njs, bdt, col = sc._njs, sc._bdt, sc._col
    pz1, pzb, pg1_, pz2, pz2b = sc._pz1, sc._pzb, sc._pg1, sc._pz2, sc._pz2b
    pa_, pb1_, ptA, ptB = sc._pa, sc._pb1, sc._ptA, sc._ptB

    def lnw(g):
        return a["lnw"][:, hs_of[g] * 128:hs_of[g] * 128 + 128]

    def lnb(g):
        return a["lnb"][:, hs_of[g] * 128:hs_of[g] * 128 + 128]

    def lrqcol(g, i):
        return a["lrqc4"][:, g:g + 1]

    for g in G:
        def f(h, i, g=g):
            return h.activation(bdt("xkbdts", g), ptA(i, g, 0), AF.Identity, scale=a["t15n"][:, 0:1])
        sc.add("a", "a_exkt" + str(g), f, deps=[("pe_tbdk" + str(g), 0), ("pe_w1ub" + str(g), -1)])
    for g in G:
        def f(h, i, g=g):
            return h.activation(bdt("tgtbdt", g), ptA(i, g, 1), AF.Identity)
        sc.add("a", "a_etgt" + str(g), f, deps=[("pe_ttgt" + str(g), 0), ("v_lt3" + str(g), -1)])
    for g in G:
        def f(h, i, g=g):
            return h.activation(njs("sg", g), njs("wv", g), AF.Exp, scale=-1.0)
        sc.add("a", "a_exp1" + str(g), f, deps=[("v_v2arg" + str(g), 0), ("v_gz1a" + str(g), -1)])
    for g in G:
        def f(h, i, g=g):
            return h.activation(njs("x2t", g), ptB(i, g, 0), AF.Identity)
        sc.add("a", "a_ex2t" + str(g), f, deps=[("pe_tx2" + str(g), 0), ("pe_a2" + str(g), -1)])
    for g in G:
        def f(h, i, g=g):
            return h.activation(col(g, 3), col(g, 2), AF.Ln, scale=-1.0, bias=a["epsc"][:, 0:1])
        sc.add("a", "a_lnv" + str(g), f, deps=[("v_st2" + str(g), 0)])
    for g in G:
        def f(h, i, g=g):
            return h.activation(col(g, 4), col(g, 3), AF.Exp, scale=-0.5)
        sc.add("a", "a_expv" + str(g), f, deps=[])
    for g in G:
        def f(h, i, g=g):
            return h.activation(bdt("gz2bdt", g), ptA(i, g, 2), AF.Identity)
        sc.add("a", "a_egz2t" + str(g), f, deps=[("pe_tgz2" + str(g), 0), ("pe_gz1" + str(g), -1)])
    for g in G:
        def f(h, i, g=g):
            return h.activation(njs("sg2", g), njs("u2", g), AF.Exp, scale=-1.0)
        sc.add("a", "a_exp2" + str(g), f, deps=[("v_v22arg" + str(g), 0), ("v_x2b" + str(g), -1)])
    for g in G:
        def f(h, i, g=g):
            return h.activation(njs("x2bt", g), ptB(i, g, 1), AF.Identity)
        sc.add("a", "a_ex2bt" + str(g), f, deps=[("pe_tx2b" + str(g), 0), ("pe_zb2" + str(g), -1)])
    for g in G:
        def f(h, i, g=g):
            return h.activation(col(g, 8), col(g, 7), AF.Ln, scale=-1.0, bias=a["epsc"][:, 0:1])
        sc.add("a", "a_lnv2" + str(g), f, deps=[("v_st22" + str(g), 0)])
    for g in G:
        def f(h, i, g=g):
            return h.activation(col(g, 9), col(g, 8), AF.Exp, scale=-0.5)
        sc.add("a", "a_expv2" + str(g), f, deps=[])
    for g in G:
        def f(h, i, g=g):
            return h.activation(a["w1b"][:, g * 1024:(g + 1) * 1024],
                                a["w1f"][:, g * 1024:(g + 1) * 1024], AF.Identity)
        sc.add("a", "a_cast1" + str(g), f, deps=[("v_wrm1b" + str(g), 0), ("pe_zb1" + str(g), 0)])

    def bdsplit(h, src, dst_t, g, blk, par):
        r = None
        for q in (par, par + 2):
            d = dst_t[32 * q:32 * q + 32, g * 2 * blk + par * blk:g * 2 * blk + par * blk + blk]
            r = h.activation(d, src[32 * q:32 * q + 32, :], AF.Identity)
        return r
    for g in G:
        for par, sfx in ((0, "e"), (1, "o")):
            def f(h, i, g=g, par=par):
                return bdsplit(h, njs("gz1s", g), a["gz1bd"], g, 256, par)
            sc.add("a", "a_gz1bd" + sfx + str(g), f,
                   deps=[("v_gz1s" + str(g), 0), ("v_wrm1b" + str(g), -1)])
    for g in G:
        for par, sfx in ((0, "e"), (1, "o")):
            def f(h, i, g=g, par=par):
                return bdsplit(h, njs("x2s", g), a["x2sbd"], g, 256, par)
            sc.add("a", "a_x2sbd" + sfx + str(g), f,
                   deps=[("pl_x2s" + str(g), 0), ("v_wrm2b" + str(g), -1)])
    for g in G:
        for par, sfx in ((0, "e"), (1, "o")):
            def f(h, i, g=g, par=par):
                return bdsplit(h, bdt("gz2s", g), a["gz2sbd"], g, 128, par)
            sc.add("a", "a_gz2sbd" + sfx + str(g), f,
                   deps=[("v_gz2m" + str(g), 0), ("v_wrm3b" + str(g), -1)])

    def V(name, fn, deps=()):
        for g in G:
            sc.add("v", name + str(g), (lambda h, i, g=g, fn=fn: fn(h, i, g)), deps=[
                (d[0] + str(g), d[1]) for d in deps])

    def f_lrq4(h, i, g=0):
        base = (2 * b_of[0] + 0) * 128
        lv = a["lrq"][:, base:base + 512].rearrange("p (gg c) -> p gg c", gg=4)
        if isinstance(i, int):
            src = lv[:, :, DS(i, 1)]
        else:
            src = lv[:, :, DS(a["_iv"], 1)]
        return h.tensor_copy(a["lrqc4"][:, 0:4].rearrange("p (gg o) -> p gg o", o=1), src)
    sc.add("v", "v_lrq40", f_lrq4)
    V("v_cc1a", lambda h, i, g: h.tensor_tensor(njs("gx", g, 128), pa_(g), a["triltnT"][:, :], OP.mult),
      deps=[("pe_a1", 0)])
    V("v_cc1b", lambda h, i, g: h.tensor_tensor(bdt("cc1", g), njs("gx", g, 128), a["triltnT"][:, :], OP.add),
      deps=[("pe_zb1", -1)])
    V("v_z1e", lambda h, i, g: h.tensor_tensor(njs("z1", g), pz1(i, g), njs("b1rep", g), OP.add),
      deps=[("pe_z1", 0)])
    V("v_uw", lambda h, i, g: h.scalar_tensor_tensor(njs("u", g), njs("z1", g), SC_B, njs("z1", g), OP.mult, OP.mult))
    V("v_v2arg", lambda h, i, g: h.scalar_tensor_tensor(njs("wv", g), njs("u", g), SC_A, njs("z1", g), OP.add, OP.mult))
    V("v_sgr", lambda h, i, g: h.reciprocal(njs("sg", g), njs("sg", g)), deps=[("pl_sig1", 0)])
    V("v_x2", lambda h, i, g: h.tensor_tensor(njs("x2", g), njs("sg", g), njs("z1", g), OP.mult),
      deps=[("pe_tx2", -1), ("pl_x2s", -1)])
    V("v_gau", lambda h, i, g: h.scalar_tensor_tensor(njs("ga", g), njs("z1", g), C4, njs("z1", g), OP.mult, OP.mult))
    V("v_gbb", lambda h, i, g: h.scalar_tensor_tensor(njs("gb", g), njs("sg", g), 1.0, njs("z1", g), OP.subtract, OP.mult))
    V("v_gbc", lambda h, i, g: h.scalar_tensor_tensor(njs("gb", g), njs("ga", g), C1, njs("gb", g), OP.add, OP.mult))
    V("v_gad", lambda h, i, g: h.tensor_scalar(njs("ga", g), njs("gb", g), -2.0, 1.0, OP.mult, OP.add))
    V("v_z2e", lambda h, i, g: h.tensor_tensor(njs("z2c", g, 128), pz2(i, g), njs("b2rep", g, 128), OP.add),
      deps=[("pe_z2", 0)])
    V("v_z2m", lambda h, i, g: h.tensor_tensor(njs("z2c", g, 128), njs("z2c", g, 128), a["bd01"][:, :], OP.mult))
    V("v_mu", lambda h, i, g: h.reduce_sum(col(g, 0), njs("z2c", g, 128), axis=mybir.AxisListType.X))
    V("v_e2a", lambda h, i, g: h.tensor_tensor(njs("gxs", g, 128), njs("z2c", g, 128), njs("z2c", g, 128), OP.mult))
    V("v_e2", lambda h, i, g: h.reduce_sum(col(g, 1), njs("gxs", g, 128), axis=mybir.AxisListType.X))
    V("v_st1", lambda h, i, g: h.tensor_scalar(a["cols"][:, g * 16:g * 16 + 2], a["cols"][:, g * 16:g * 16 + 2],
                                               1.0 / 64, None, OP.mult))
    V("v_st2", lambda h, i, g: h.scalar_tensor_tensor(col(g, 2), col(g, 0), col(g, 0), col(g, 1), OP.mult, OP.subtract))
    V("v_xh", lambda h, i, g: h.tensor_scalar(njs("xh", g, 128), njs("z2c", g, 128), col(g, 0), col(g, 4),
                                              OP.subtract, OP.mult), deps=[("a_expv", 0)])
    V("v_lt1", lambda h, i, g: h.tensor_tensor(njs("gxs", g, 128), njs("xh", g, 128), lnw(g), OP.mult))
    V("v_lt2", lambda h, i, g: h.tensor_tensor(njs("gxs", g, 128), njs("gxs", g, 128), lnb(g), OP.add))
    V("v_lt3", lambda h, i, g: h.tensor_tensor(njs("gxs", g, 128), njs("gxs", g, 128), bdt("tgtbdt", g), OP.subtract),
      deps=[("a_etgt", 0)])
    V("v_gx", lambda h, i, g: h.tensor_tensor(njs("gx", g, 128), njs("gxs", g, 128), lnw(g), OP.mult))
    V("v_m1", lambda h, i, g: h.reduce_sum(col(g, 5), njs("gx", g, 128), axis=mybir.AxisListType.X))
    V("v_m2a", lambda h, i, g: h.tensor_tensor(njs("z2c", g, 128), njs("gx", g, 128), njs("xh", g, 128), OP.mult))
    V("v_m2", lambda h, i, g: h.reduce_sum(col(g, 6), njs("z2c", g, 128), axis=mybir.AxisListType.X))
    V("v_m12", lambda h, i, g: h.tensor_scalar(a["cols"][:, g * 16 + 5:g * 16 + 7],
                                               a["cols"][:, g * 16 + 5:g * 16 + 7], -1.0 / 64, None, OP.mult))
    V("v_g1", lambda h, i, g: h.tensor_scalar(njs("gxs", g, 128), njs("gx", g, 128), col(g, 5), None, OP.add))
    V("v_g2", lambda h, i, g: h.scalar_tensor_tensor(njs("gxs", g, 128), njs("xh", g, 128), col(g, 6),
                                                     njs("gxs", g, 128), OP.mult, OP.add))
    V("v_gz2s", lambda h, i, g: h.tensor_scalar(bdt("gz2s", g), njs("gxs", g, 128), col(g, 4), lrqcol(g, i),
                                                OP.mult, OP.mult), deps=[("pe_w2tub", -1), ("a_gz2sbdo", -1)])
    V("v_gz2m", lambda h, i, g: h.tensor_tensor(bdt("gz2s", g), bdt("gz2s", g), a["bd01"][:, :], OP.mult))
    V("v_gz1a", lambda h, i, g: h.tensor_tensor(njs("wv", g), pg1_(i, g), njs("sg", g), OP.mult),
      deps=[("pe_gz1", 0)])
    V("v_gz1s", lambda h, i, g: h.tensor_tensor(njs("gz1s", g), njs("wv", g), njs("ga", g), OP.mult),
      deps=[("pe_b1u", -1)])
    V("v_z1be", lambda h, i, g: h.tensor_tensor(njs("z1b", g), pzb(i, g), njs("b1rep", g), OP.add),
      deps=[("pe_zb1", 0)])
    V("v_u2", lambda h, i, g: h.scalar_tensor_tensor(njs("u2", g), njs("z1b", g), SC_B, njs("z1b", g), OP.mult, OP.mult))
    V("v_v22arg", lambda h, i, g: h.scalar_tensor_tensor(njs("u2", g), njs("u2", g), SC_A, njs("z1b", g), OP.add, OP.mult))
    V("v_sgr2", lambda h, i, g: h.reciprocal(njs("sg2", g), njs("sg2", g)), deps=[("pl_sig2", 0)])
    V("v_x2b", lambda h, i, g: h.tensor_tensor(njs("x2b", g), njs("sg2", g), njs("z1b", g), OP.mult),
      deps=[("pe_tx2b", -1)])
    V("v_cc2a", lambda h, i, g: h.tensor_tensor(njs("gx", g, 128), pa_(g), a["triltnT"][:, :], OP.mult),
      deps=[("pe_a2", 0)])
    V("v_cc2b", lambda h, i, g: h.tensor_tensor(bdt("cc2", g), njs("gx", g, 128), a["triltnT"][:, :], OP.add),
      deps=[("pe_zb2", -1)])
    V("v_z2be", lambda h, i, g: h.tensor_tensor(njs("z2bc", g, 128), pz2b(i, g), njs("b2rep", g, 128), OP.add),
      deps=[("pe_zb2", 0)])
    V("v_z2bm", lambda h, i, g: h.tensor_tensor(njs("z2bc", g, 128), njs("z2bc", g, 128), a["bd01"][:, :], OP.mult))
    V("v_mu2", lambda h, i, g: h.reduce_sum(col(g, 5), njs("z2bc", g, 128), axis=mybir.AxisListType.X))
    V("v_e22a", lambda h, i, g: h.tensor_tensor(njs("gxs", g, 128), njs("z2bc", g, 128), njs("z2bc", g, 128), OP.mult))
    V("v_e22", lambda h, i, g: h.reduce_sum(col(g, 6), njs("gxs", g, 128), axis=mybir.AxisListType.X))
    V("v_st21", lambda h, i, g: h.tensor_scalar(a["cols"][:, g * 16 + 5:g * 16 + 7],
                                                a["cols"][:, g * 16 + 5:g * 16 + 7], 1.0 / 64, None, OP.mult))
    V("v_st22", lambda h, i, g: h.scalar_tensor_tensor(col(g, 7), col(g, 5), col(g, 5), col(g, 6), OP.mult, OP.subtract))
    V("v_xh2", lambda h, i, g: h.tensor_scalar(njs("xh2", g, 128), njs("z2bc", g, 128), col(g, 5), col(g, 9),
                                               OP.subtract, OP.mult), deps=[("a_expv2", 0)])
    V("v_lo1", lambda h, i, g: h.tensor_tensor(njs("gxs", g, 128), njs("xh2", g, 128), lnw(g), OP.mult))
    V("v_lo2", lambda h, i, g: h.tensor_tensor(bdt("lnout", g), njs("gxs", g, 128), lnb(g), OP.add),
      deps=[("pe_tdep", -1)])

    def f_dep1(h, i, g):
        wdiv8(h, i, 16 * 4 * a["ng"], a.get("_sc_d0", 0), a["s_dep"])
        if isinstance(i, int):
            d = a["dep"][:, :].rearrange("p (x q t2) -> p x q t2", q=4, t2=16)[:, DS(sc._xidx(g, i), 1), :, :]
        else:
            d = a["dep"][:, g * 1024:(g + 1) * 1024].rearrange(
                "p (x q t2) -> p x q t2", q=4, t2=16)[:, DS(a["_corev"], 1), :, :]
        pA = ptA(i, g, 3).rearrange("p (o q c) -> p o q c", o=1, c=32)
        h.tensor_copy(d, pA[:, :, :, 0:16])
        h.drain()
        return h.tensor_tensor(d, d, pA[:, :, :, 16:32], OP.add)
    V("v_dep1", f_dep1, deps=[("pe_tdep", 0)])

    def f_dep2(h, i, g):
        d = sc._winf("dep", g, i)
        return h.tensor_tensor(d, d, sc._scur(g, "xq").rearrange("p (o s) -> p o s", o=1), OP.add)
    V("v_dep2", f_dep2)

    for g in G:
        def addw(name, fn, deps=()):
            sc.add("v", name + str(g), (lambda h, i, g=g, fn=fn: fn(h, i, g)),
                   deps=[(d[0] + str(g), d[1]) for d in deps])
        addw("v_wrm1a", lambda h, i, g: h.tensor_tensor(a["w1f"][:, g * 1024:g * 1024 + 512],
             a["w1f"][:, g * 1024:g * 1024 + 512], a["pw"][:, 0:512], OP.add),
             deps=[("pe_w1ua", 0), ("a_cast1", -1)])
        addw("v_wrm1b", lambda h, i, g: h.tensor_tensor(a["w1f"][:, g * 1024 + 512:g * 1024 + 1024],
             a["w1f"][:, g * 1024 + 512:g * 1024 + 1024], a["pw"][:, 0:512], OP.add),
             deps=[("pe_w1ub", 0)])
        addw("v_wrm2a", lambda h, i, g: h.tensor_tensor(a["w2tf"][:, g * 1024:g * 1024 + 512],
             a["w2tf"][:, g * 1024:g * 1024 + 512], a["pw"][:, 0:512], OP.add),
             deps=[("pe_w2tua", 0), ("pl_cast2", -1)])
        addw("v_wrm2b", lambda h, i, g: h.tensor_tensor(a["w2tf"][:, g * 1024 + 512:g * 1024 + 1024],
             a["w2tf"][:, g * 1024 + 512:g * 1024 + 1024], a["pw"][:, 0:512], OP.add),
             deps=[("pe_w2tub", 0)])
        def wrm3(h, g, half):
            r = None
            for c in range(2):
                d = a["w2cf"][:, g * 1024 + half * 512:g * 1024 + half * 512 + 512].rearrange(
                    "p (s k) -> p s k", s=2)[:, :, c * 128:c * 128 + 128]
                s_ = a["pw"][:, c * 256:c * 256 + 256].rearrange("p (s k) -> p s k", s=2)
                r = h.tensor_tensor(d, d, s_, OP.add)
            return r
        addw("v_wrm3a", lambda h, i, g: wrm3(h, g, 0),
             deps=[("pe_w2cua", 0), ("pl_cast3", -1)])
        addw("v_wrm3b", lambda h, i, g: wrm3(h, g, 1),
             deps=[("pe_w2cub", 0)])
        addw("v_b1rm", lambda h, i, g: h.tensor_tensor(njs("b1rep", g), njs("b1rep", g), pb1_(), OP.add),
             deps=[("pe_b1u", 0)])
        addw("v_b2rm", lambda h, i, g: h.tensor_tensor(njs("b2rep", g, 128), njs("b2rep", g, 128), pa_(g), OP.add),
             deps=[("pe_b2u", 0)])
    EARLY = {
        "pe": ["pe_a1", "pe_z1"],
        "v": ["v_lrq4", "v_cc1a", "v_cc1b", "v_z1e", "v_uw", "v_v2arg", "v_sgr", "v_x2", "v_gau",
              "v_gbb", "v_gbc", "v_gad"],
        "a": ["a_exp1"],
        "pl": ["pl_wcur", "pl_bdk", "pl_bdq", "pl_tgtf", "pl_bdtg", "pl_sig1"],
    }
    LATE = {
        "pe": ["pe_tbdk", "pe_ttgt", "pe_tx2", "pe_z2", "pe_tgz2", "pe_gz1", "pe_zb1",
               "pe_tx2b", "pe_a2", "pe_zb2", "pe_tdep", "pe_w1ua", "pe_w1ub", "pe_w2tua",
               "pe_w2tub", "pe_w2cua", "pe_w2cub", "pe_b1u", "pe_b2u"],
        "v": ["v_z2e", "v_z2m", "v_mu", "v_e2a", "v_e2", "v_st1", "v_st2", "v_xh",
              "v_lt1", "v_lt2", "v_lt3", "v_gx", "v_m1", "v_m2a", "v_m2", "v_m12",
              "v_g1", "v_g2", "v_gz2s", "v_gz2m", "v_gz1a", "v_gz1s", "v_z1be", "v_u2",
              "v_v22arg", "v_sgr2", "v_x2b", "v_cc2a", "v_cc2b", "v_z2be", "v_z2bm",
              "v_mu2", "v_e22a", "v_e22", "v_st21", "v_st22", "v_xh2", "v_lo1", "v_lo2",
              "v_dep1", "v_dep2", "v_wrm1a", "v_wrm1b", "v_wrm2a", "v_wrm2b", "v_wrm3a",
              "v_wrm3b", "v_b1rm", "v_b2rm"],
        "a": ["a_exkt", "a_etgt", "a_ex2t", "a_lnv", "a_expv", "a_egz2t", "a_gz1bde",
              "a_gz1bdo", "a_exp2", "a_ex2bt", "a_lnv2", "a_expv2", "a_x2sbde",
              "a_x2sbdo", "a_gz2sbde", "a_gz2sbdo", "a_cast1"],
        "pl": ["pl_x2s", "pl_sig2", "pl_cast2", "pl_cast3"],
    }

    def keyf(op):
        eng, name = op[0], op[1]
        base, g = name[:-1], int(name[-1])
        if base in EARLY[eng]:
            return (0, EARLY[eng].index(base), g)
        return (1, g, LATE[eng].index(base))
    sc.ops = sorted(sc.ops, key=keyf)
    sc.finalize()
    return sc


def scan_emitters(p, a, B, sc, b_of, hs_of, nsteps=128, fori=True, lrq_dma=False,
                  half=0, bases=None, pre_sync=None, pre_gps=None):
    ng = a["ng"]
    NWIN = (nsteps + 7) // 8
    NW = 3 * ng
    S = sc.S
    DS = bass.DynSlice
    bz = bases or {}
    w0 = bz.get("win", 0)
    d0 = bz.get("dep", 0)
    ini0 = bz.get("ini", 0)
    e0 = {e: bz.get(e, 0) for e in ("pe", "v", "a", "pl")}
    sc.base = dict(e0)

    def winslice(t, g, w, pg):
        hs, b = hs_of[g], b_of[g]
        return B[t][512 * hs + 128 * pg:512 * hs + 128 * pg + 128,
                    b * 2048 + 128 * w:b * 2048 + 128 * w + 128].rearrange(
            "p (s t) -> p s t", t=16)

    def winbuf(t, g, w, pg):
        return a[t][:, g * 1024 + (w % 2) * 512:g * 1024 + (w % 2) * 512 + 512].rearrange(
            "p (s pg t) -> p s pg t", pg=4, t=16)[:, :, pg, :]

    init_dmas = []
    for g in range(ng):
        hs = hs_of[g]
        for dst, key in [("w1f", "w1duo"), ("w1b", "w1duo"), ("w2tf", "w2t"), ("w2tb", "w2t"),
                         ("w2cf", "w2cm"), ("w2cb", "w2cm")]:
            init_dmas.append((a[dst][:, g * 1024:(g + 1) * 1024],
                              B[key][4 * hs:4 * hs + 4].rearrange("a p c -> p a c")))
        init_dmas.append((a["b1rep"][:, g * 256:(g + 1) * 256], B["b1rep"][hs, :, :]))
        init_dmas.append((a["b2rep"][:, g * 128:(g + 1) * 128], B["b2rep_bdt"][hs, :, :]))
    if half == 0:
        for dst, key in [("lnw", "lnw_bdt"), ("lnb", "lnb_bdt")]:
            init_dmas.append((a[dst][:, 0:128], B[key][0, :, :]))
            init_dmas.append((a[dst][:, 128:256], B[key][1, :, :]))
        init_dmas.append((a["bd01"][:, :], B["bd01"][:, :]))
        init_dmas.append((a["triltnT"][:, :], B["trilT01"][:, :]))
        init_dmas.append((a["ident"][:, :], B["identc"][:, :]))
        init_dmas.append((a["tril01b"][:, :], B["tril01b"][:, :]))
        init_dmas.append((a["ident4"][:, :], B["ident4"][:, :]))
        init_dmas.append((a["t15n"][:, 0:1], B["lti15_rep"][:, :]))
        init_dmas.append((a["cols"][:, 0:1], B["lti_rep"][:, :]))
        init_dmas.append((a["cols"][:, 1:2], B["invi"][:, :]))
        init_dmas.append((a["onest15"][:, :], B["ones01"][:, :]))
        init_dmas.append((a["sctab"][:, :], B["sctab"][:, :]))
        if lrq_dma:
            init_dmas.append((a["lrq"][:, :], B["lrq_in"][:, :]))
    NINI = len(init_dmas)

    def emit_steps(h, eng):
        a["_sc_w0"] = w0
        a["_sc_d0"] = d0
        a["_memo"] = {}
        sc.base = dict(e0)
        if fori:
            for i in range(2):
                sc.emit_engine(h, eng, a, i)
            sc.alloc_regs(h, eng)
            sc.reg_init(h, eng, 2)
            with fori_shared(h, 2, nsteps) as i:
                prep_vals(h, a, i)
                sc.emit_engine(h, eng, a, i)
                sc.reg_step(h, eng)
        else:
            for i in range(nsteps):
                sc.emit_engine(h, eng, a, i)

    def gps(g_):
        if pre_gps is not None:
            pre_gps(g_)
        if half == 1:
            for e in ("pe", "v", "a", "pl"):
                g_.wait_ge(a["s_" + e], e0[e])
        for k, (dst, srcin) in enumerate(init_dmas):
            if ini0 + k > 0:
                g_.wait_ge(a["s_ini"], ini0 + 16 * k)
            g_.dma_start(out=dst, in_=srcin).then_inc(a["s_ini"], 16)
        g_.wait_ge(a["s_ini"], ini0 + 16 * NINI)
        g_.wait_ge(a["s_iv"], 4)
        emit_steps(g_, "pl")

    def sync(q):
        if pre_sync is not None:
            pre_sync(q)
        q.sem_inc(a["s_dep"], 16 * 4 * ng)
        dpre = d0 + 16 * 4 * ng
        ndep = [0]

        def load_win(w):
            if w >= NWIN:
                return
            if w >= 2:
                q.wait_ge(a["s_pl"], e0["pl"] + S["pl"] * (8 * (w - 1)))
                q.wait_ge(a["s_v"], e0["v"] + S["v"] * (8 * (w - 1)))
            elif half == 1:
                q.wait_ge(a["s_pl"], e0["pl"])
                q.wait_ge(a["s_v"], e0["v"])
            for g in range(ng):
                for j, (t, srckey) in enumerate([("xqw", "xqfm"), ("xkw", "xkfm"), ("xvw", "xvfm")]):
                    for pg in range(4):
                        cnt = w0 + 16 * ((w * NW + g * 3 + j) * 4 + pg)
                        if cnt > 0:
                            q.wait_ge(a["s_win"], cnt)
                        q.dma_start(out=winbuf(t, g, w, pg), in_=winslice(srckey, g, w, pg)
                                    ).then_inc(a["s_win"], 16)

        load_win(0)
        load_win(1)
        for w in range(NWIN):
            q.wait_ge(a["s_v"], e0["v"] + S["v"] * min(8 * w + 8, nsteps))
            for g in range(ng):
                for pg in range(4):
                    if dpre + 16 * ndep[0] > 0:
                        q.wait_ge(a["s_dep"], dpre + 16 * ndep[0])
                    q.dma_start(out=winslice("xqwfm", g, w, pg), in_=winbuf("dep", g, w, pg)
                                ).then_inc(a["s_dep"], 16)
                    ndep[0] += 1
            load_win(w + 2)
        q.wait_ge(a["s_win"], w0 + 16 * NW * NWIN * 4)
        q.wait_ge(a["s_pe"], e0["pe"] + S["pe"] * nsteps)
        q.wait_ge(a["s_v"], e0["v"] + S["v"] * nsteps)
        q.wait_ge(a["s_a"], e0["a"] + S["a"] * nsteps)
        q.wait_ge(a["s_pl"], e0["pl"] + S["pl"] * nsteps)
        q.wait_ge(a["s_dep"], dpre + 16 * 4 * ng * NWIN)

    def pe(t):
        if half == 0:
            t.wait_ge(a["s_iv"], 2)
            t.matmul(a["pzb1"][:, 0:128], a["tril01b"][:, :], a["gz2bdt"][:, 0:128], start=True,
                     stop=True).then_inc(a["s_ipe"], 1)
        t.wait_ge(a["s_iv"], 4)
        emit_steps(t, "pe")

    def vec(v):
        if half == 1:
            emit_steps(v, "v")
            return
        v.wait_ge(a["s_ini"], ini0 + 16 * NINI)
        v.tensor_tensor(a["cols"][:, 0:1], a["cols"][:, 0:1], a["cols"][:, 1:2], OP.add)
        v.drain()
        v.tensor_scalar(a["cols"][:, 0:1], a["cols"][:, 0:1], 0.0, None, OP.max)
        v.drain()
        v.tensor_scalar(a["cols"][:, 0:1], a["cols"][:, 0:1], -1.0, None, OP.mult).then_inc(a["s_iv"], 1)
        v.drain()
        v.tensor_scalar(a["gz2bdt"][:, 0:128], a["ident"][:, :], a["cols"][:, 0:1], None, OP.mult).then_inc(a["s_iv"], 1)
        v.tensor_scalar(a["t15n"][:, 0:1], a["t15n"][:, 0:1], 1.0 / 16, None, OP.add)
        v.drain()
        v.tensor_scalar(a["t15n"][:, 0:1], a["t15n"][:, 0:1], 0.0, None, OP.max)
        v.drain()
        v.tensor_scalar(a["t15n"][:, 0:1], a["t15n"][:, 0:1], -1.0, None, OP.mult)
        v.drain()
        v.tensor_scalar(a["onest15"][:, :], a["onest15"][:, :], a["t15n"][:, 0:1], None, OP.mult)
        v.drain()
        v.memset(a["epsc"][:, 0:1], EPS)
        v.memset(a["pab"][:, :], 0.0)
        v.drain()
        v.memset(a["dep"][:, :], 0.0)
        v.drain()
        v.memset(a["bdxk"][:, :], 0.0)
        v.memset(a["bdxq"][:, :], 0.0)
        v.memset(a["bdtgt"][:, :], 0.0)
        v.memset(a["gz1bd"][:, :], 0.0)
        v.memset(a["x2sbd"][:, :], 0.0)
        v.memset(a["gz2sbd"][:, :], 0.0).then_inc(a["s_iv"], 1)
        v.wait_ge(a["s_ipe"], 1)
        v.tensor_copy(a["triltnT"][:, :], a["pzb1"][:, 0:128]).then_inc(a["s_iv"], 1)
        emit_steps(v, "v")

    def act(s):
        s.wait_ge(a["s_iv"], 4)
        emit_steps(s, "a")

    return {"sync": sync, "gpsimd": gps, "tensor": pe, "vector": vec, "scalar": act,
            "nini": NINI}




DS = bass.DynSlice
EPS5 = 1e-5
EPS6 = 1e-6
AX = mybir.AxisListType.X


def f32v(h):
    return h.bitcast(F32)


def bf16v(h):
    return h.bitcast(BF16)


class SerialQ:
    def __init__(self, q, a, w0, d0):
        self.q, self.a = q, a
        self.w, self.d = w0, d0

    def load(self, out, in_, transpose=False):
        if self.w > 0:
            self.q.wait_ge(self.a["s_win"], self.w)
        if transpose:
            self.q.dma_start(out=out, in_=in_, transpose=True).then_inc(self.a["s_win"], 16)
        else:
            self.q.dma_start(out=out, in_=in_).then_inc(self.a["s_win"], 16)
        self.w += 16

    def store(self, out, in_):
        if self.d > 0:
            self.q.wait_ge(self.a["s_dep"], self.d)
        self.q.dma_start(out=out, in_=in_).then_inc(self.a["s_dep"], 16)
        self.d += 16


def rwait(h, i, sem, key, coef, const):
    if isinstance(i, int):
        v = coef * i + const
        if v > 0:
            h.wait_ge(sem, v)
    else:
        rs = get_regs(h)
        h.reg_add(rs["xs"], rs[key], const)
        h.wait_ge(sem, rs["xs"])


def p34_carve(a):
    m = {}
    w1fb = bf16v(a["w1f"])
    w2tfb = bf16v(a["w2tf"])
    w2cfb = bf16v(a["w2cf"])
    m["wts"] = f32v(a["b1rep"])[:, 0:512]
    m["xqw_tm"] = [a["xqw"], a["xkw"]]
    m["g_tm"] = [a["xvw"], a["dep"]]
    m["lnG"] = [a["w1b"], a["w2tb"]]
    m["infm"] = [a["w2cb"], w1fb[:, 0:4096]]
    m["xnfm_st"] = [w1fb[:, 4096:8192], w2cfb[:, 0:4096]]
    m["woT"] = w2tfb
    m["x_tm"] = [f32v(a["gz1bd"]), f32v(a["x2sbd"])]
    m["x1"] = [a["w2cf"][:, 2048:3072], a["w2cf"][:, 3072:4096]]
    m["xn_tm"] = [a["z1"], a["u"]]
    m["postw"] = [f32v(a["wv"]), f32v(a["sg"])]
    m["postb"] = [f32v(a["x2"]), f32v(a["x2s"])]
    m["lng"] = [f32v(a["ga"]), f32v(a["gb"])]
    m["lnb"] = [f32v(a["gz1s"]), f32v(a["x2t"])]
    m["sqs"] = [f32v(a["z2c"]), f32v(a["xh"])]
    m["sqs2"] = [f32v(a["gx"]), f32v(a["gxs"])]
    m["cols"] = f32v(a["z1b"])
    m["gatew"] = bf16v(a["scur"])[:, 0:48]
    m["gateb"] = f32v(a["scur"])[:, 32:38]
    m["gt"] = f32v(a["u2"])
    m["e_xn"] = [a["xqw"], a["xkw"]]
    m["ew1"] = [w1fb, w2tfb]
    m["ew2"] = [w2cfb, bf16v(a["w1b"]), bf16v(a["w2tb"])]
    m["h1"] = [a["dep"], a["w2cb"]]
    m["moeo"] = f32v(a["xvw"])
    m["escr"] = [f32v(a["z2c"]), f32v(a["xh"])]
    m["eb2"] = f32v(a["gz1bd"])
    m["eb1"] = f32v(a["scur"])[:, 64:80]
    m["Rs"] = w1fb
    m["abel5"] = [a["w2cf"][:, 0:1024], a["w2cf"][:, 1024:2048]]
    m["moe5"] = [f32v(a["xvw"])[:, 0:1024], f32v(a["xvw"])[:, 1024:2048]]
    m["x15"] = [f32v(a["gz1bd"]), f32v(a["x2sbd"])]
    m["moebf"] = [a["wv"], a["sg"]]
    m["moefm"] = [a["x2"], a["x2s"]]
    m["gcw1"] = [f32v(a["ga"]), f32v(a["gb"])]
    m["gcw2"] = [f32v(a["gz1s"]), f32v(a["x2t"])]
    m["y"] = [a["w2tf"][:, 0:1024], a["w2tf"][:, 1024:2048]]
    m["scr5"] = [a["w2tf"][:, 2048:3072], a["w2tf"][:, 3072:4096]]
    m["sq5"] = [f32v(a["z2bc"]), f32v(a["xh2"])]
    m["eLH"] = [w1fb, w2tfb]
    m["eRH"] = [[w2cfb], [a["w2cb"], a["dep"]]]
    m["eo"] = [a["z1"], a["u"]]
    m["easm"] = [[a["wv"], a["sg"], a["x2"], a["x2s"]],
                 [a["ga"], a["gb"], a["gz1s"], a["x2t"]]]
    return m


def rhtile(m, slot, kb):
    if slot == 0:
        return m["eRH"][0][0][:, kb * 1024:(kb + 1) * 1024]
    h = m["eRH"][1][kb // 4]
    return h[:, (kb % 4) * 1024:(kb % 4 + 1) * 1024]


def build_p3(a, m, psA, psB, psG, ptT, nc):
    sc = Sched()

    def wwin3(h, i, k, sg, st):
        rwait(h, i, a["s_win"], "xw", 384,
              a["_p3w0"] + 160 + 48 * (4 * sg + st) + 16 * k)

    def col(sg, st, k):
        o = (sg * 4 + st) * 32 + k
        return m["cols"][:, o:o + 1]

    def uc(base, sg, st, k=1):
        u = sg * 4 + st - k
        if u >= 0:
            return (base + str(u // 4) + str(u % 4), 0)
        return (base + str((u + 8) // 4) + str((u + 8) % 4), -1)

    for sg in range(2):
        for st in range(4):
            sfx = str(sg) + str(st)
            xqw = m["xqw_tm"][sg][:, st * 1024:(st + 1) * 1024]
            gtm = m["g_tm"][sg][:, st * 1024:(st + 1) * 1024]
            lnG = m["lnG"][sg][:, st * 1024:(st + 1) * 1024]
            x1 = m["x1"][sg]
            xtm = m["x_tm"][sg][:, 0:1024]
            xn = m["xn_tm"][sg][:, 0:1024]

            def f_sqa(h, i, xqw=xqw, sg=sg, st=st):
                wwin3(h, i, 1, sg, st)
                return h.tensor_tensor(m["sqs"][0][:, 0:512], xqw[:, 0:512], xqw[:, 0:512], OP.mult)
            sc.add("pl", "p3sqa" + sfx, f_sqa, deps=[uc("p3r2a", sg, st)])
            def f_sqb(h, i, xqw=xqw):
                return h.tensor_tensor(m["sqs"][1][:, 0:512], xqw[:, 512:1024], xqw[:, 512:1024], OP.mult)
            sc.add("pl", "p3sqb" + sfx, f_sqb, deps=[uc("p3r2b", sg, st)])
            def f_r1(h, i, xqw=xqw, sg=sg, st=st):
                wwin3(h, i, 1, sg, st)
                return h.reduce_sum(col(sg, st, 0), xqw, axis=AX)
            sc.add("v", "p3r1" + sfx, f_r1)
            sc.add("v", "p3r2a" + sfx, (lambda h, i, sg=sg, st=st: h.reduce_sum(
                col(sg, st, 1), m["sqs"][0][:, 0:512], axis=AX)), deps=[("p3sqa" + sfx, 0)])
            sc.add("v", "p3r2b" + sfx, (lambda h, i, sg=sg, st=st: h.reduce_sum(
                col(sg, st, 2), m["sqs"][1][:, 0:512], axis=AX)), deps=[("p3sqb" + sfx, 0)])
            sc.add("v", "p3mu" + sfx, (lambda h, i, sg=sg, st=st: h.tensor_scalar(
                col(sg, st, 0), col(sg, st, 0), 1.0 / 1024, None, OP.mult)))
            sc.add("v", "p3ad" + sfx, (lambda h, i, sg=sg, st=st: h.tensor_tensor(
                col(sg, st, 1), col(sg, st, 1), col(sg, st, 2), OP.add)))
            sc.add("v", "p3m2" + sfx, (lambda h, i, sg=sg, st=st: h.tensor_scalar(
                col(sg, st, 1), col(sg, st, 1), 1.0 / 1024, None, OP.mult)))
            sc.add("v", "p3nv" + sfx, (lambda h, i, sg=sg, st=st: h.scalar_tensor_tensor(
                col(sg, st, 2), col(sg, st, 0), col(sg, st, 0), col(sg, st, 1), OP.mult, OP.subtract)))
            sc.add("a", "p3ln" + sfx, (lambda h, i, sg=sg, st=st: h.activation(
                col(sg, st, 3), col(sg, st, 2), AF.Ln, scale=-1.0, bias=m["cols"][:, 440:441])),
                deps=[("p3nv" + sfx, 0)])
            sc.add("a", "p3rs" + sfx, (lambda h, i, sg=sg, st=st: h.activation(
                col(sg, st, 4), col(sg, st, 3), AF.Exp, scale=-0.5)))
            sc.add("v", "p3ap" + sfx, (lambda h, i, xqw=xqw, lnG=lnG, sg=sg, st=st: h.tensor_scalar(
                lnG, xqw, col(sg, st, 0), col(sg, st, 4), OP.subtract, OP.mult)),
                deps=[("p3rs" + sfx, 0), ("p3t" + sfx, -1)])
            for hh in range(2):
                def f_pw(h, i, lnG=lnG, hh=hh):
                    d = lnG[:, hh * 512:(hh + 1) * 512]
                    return h.tensor_tensor(d, d, m["postw"][hh][:, 0:512], OP.mult)
                sc.add("pl", "p3pw" + str(hh) + sfx, f_pw, deps=[("p3ap" + sfx, 0)])
                def f_pb(h, i, lnG=lnG, hh=hh):
                    d = lnG[:, hh * 512:(hh + 1) * 512]
                    return h.tensor_tensor(d, d, m["postb"][hh][:, 0:512], OP.add)
                sc.add("pl", "p3pb" + str(hh) + sfx, f_pb)
            def f_gg(h, i, lnG=lnG, gtm=gtm, sg=sg, st=st):
                wwin3(h, i, 2, sg, st)
                return h.tensor_tensor(lnG, lnG, gtm, OP.mult)
            sc.add("v", "p3gg" + sfx, f_gg, deps=[("p3pb1" + sfx, 0)])
            def f_t(h, i, lnG=lnG, sg=sg):
                r = None
                for c in range(8):
                    r = h.transpose(ptT[:, sg * 1024 + c * 128:sg * 1024 + c * 128 + 128],
                                    lnG[:, c * 128:c * 128 + 128], a["ident"][:, :])
                return r
            sc.add("pe", "p3t" + sfx, f_t, deps=[("p3gg" + sfx, 0), uc("p3ex", sg, st)])
            def f_et(h, i, sg=sg, st=st):
                dst = m["infm"][sg].rearrange("p (c s) -> p c s", s=512)[:, :, st * 128:st * 128 + 128]
                src = ptT[:, sg * 1024:(sg + 1) * 1024].rearrange("p (c t) -> p c t", t=128)
                return h.activation(dst, src, AF.Identity)
            sc.add("a", "p3et" + sfx, f_et, deps=[("p3t" + sfx, 0), ("p3wo1" + sfx, -1)])
            for hh in range(2):
                def f_wo(h, i, sg=sg, st=st, hh=hh):
                    h.wait_ge(a["s_ia"], a["_p3ia"])
                    ps = (psA if sg == 0 else psB)[hh]
                    r = None
                    for c in range(8):
                        r = h.matmul(ps, m["infm"][sg][:, c * 512 + st * 128:c * 512 + st * 128 + 128],
                                     m["woT"][:, c * 1024 + hh * 512:c * 1024 + hh * 512 + 512],
                                     start=(c == 0), stop=(c == 7))
                    return r
                sc.add("pe", "p3wo" + str(hh) + sfx, f_wo,
                       deps=[("p3et" + sfx, 0), uc("p3res" + str(hh), sg, st)])
            for hh in range(2):
                def f_res(h, i, sg=sg, st=st, hh=hh, x1=x1, xtm=xtm):
                    rwait(h, i, a["s_win"], "xw", 384,
                          a["_p3w0"] + 160 + 48 * (4 * sg + st) + 48)
                    if hh == 0 and (not isinstance(i, int) or 16 * (12 * (i * 2 + sg) + st) > 0):
                        rwait(h, i, a["s_dep"], "xd", 384,
                              a["_p3d0"] + 16 * (12 * sg + st))
                    ps = (psA if sg == 0 else psB)[hh]
                    return h.tensor_tensor(x1[:, hh * 512:(hh + 1) * 512], ps,
                                           xtm[:, hh * 512:(hh + 1) * 512], OP.add)
                sc.add("v", "p3res" + str(hh) + sfx, f_res, deps=[("p3wo" + str(hh) + sfx, 0)])
            for hh in range(2):
                def f_sq2(h, i, hh=hh, x1=x1):
                    return h.tensor_tensor(m["sqs2"][hh][:, 0:512], x1[:, hh * 512:(hh + 1) * 512],
                                           x1[:, hh * 512:(hh + 1) * 512], OP.mult)
                sc.add("pl", "p3sq2" + str(hh) + sfx, f_sq2,
                       deps=[("p3res" + str(hh) + sfx, 0), uc("p3r2" + str(hh) + "c", sg, st)])
            sc.add("v", "p3r1b" + sfx, (lambda h, i, x1=x1, sg=sg, st=st: h.reduce_sum(
                col(sg, st, 5), x1, axis=AX)), deps=[("p3res1" + sfx, 0)])
            for hh in range(2):
                sc.add("v", "p3r2" + str(hh) + "c" + sfx, (lambda h, i, sg=sg, st=st, hh=hh: h.reduce_sum(
                    col(sg, st, 6 + hh), m["sqs2"][hh][:, 0:512], axis=AX)),
                    deps=[("p3sq2" + str(hh) + sfx, 0)])
            sc.add("v", "p3mu2" + sfx, (lambda h, i, sg=sg, st=st: h.tensor_scalar(
                col(sg, st, 5), col(sg, st, 5), 1.0 / 1024, None, OP.mult)))
            sc.add("v", "p3ad2" + sfx, (lambda h, i, sg=sg, st=st: h.tensor_tensor(
                col(sg, st, 6), col(sg, st, 6), col(sg, st, 7), OP.add)))
            sc.add("v", "p3m22" + sfx, (lambda h, i, sg=sg, st=st: h.tensor_scalar(
                col(sg, st, 6), col(sg, st, 6), 1.0 / 1024, None, OP.mult)))
            sc.add("v", "p3nv2" + sfx, (lambda h, i, sg=sg, st=st: h.scalar_tensor_tensor(
                col(sg, st, 7), col(sg, st, 5), col(sg, st, 5), col(sg, st, 6), OP.mult, OP.subtract)))
            sc.add("a", "p3l2" + sfx, (lambda h, i, sg=sg, st=st: h.activation(
                col(sg, st, 8), col(sg, st, 7), AF.Ln, scale=-1.0, bias=m["cols"][:, 441:442])),
                deps=[("p3nv2" + sfx, 0)])
            sc.add("a", "p3rs2" + sfx, (lambda h, i, sg=sg, st=st: h.activation(
                col(sg, st, 9), col(sg, st, 8), AF.Exp, scale=-0.5)))
            sc.add("a", "p3iv" + sfx, (lambda h, i, sg=sg, st=st: h.activation(
                col(sg, st, 10), col(sg, st, 8), AF.Exp, scale=-1.0)))
            sc.add("v", "p3t2" + sfx, (lambda h, i, sg=sg, st=st: h.scalar_tensor_tensor(
                col(sg, st, 11), col(sg, st, 7), -1.0, col(sg, st, 10), OP.mult, OP.mult)),
                deps=[("p3iv" + sfx, 0)])
            sc.add("v", "p3t3" + sfx, (lambda h, i, sg=sg, st=st: h.tensor_scalar(
                col(sg, st, 11), col(sg, st, 11), EPS5, None, OP.add)))
            sc.add("a", "p3l3" + sfx, (lambda h, i, sg=sg, st=st: h.activation(
                col(sg, st, 12), col(sg, st, 11), AF.Ln, scale=1.0)), deps=[("p3t3" + sfx, 0)])
            sc.add("a", "p3r3" + sfx, (lambda h, i, sg=sg, st=st: h.activation(
                col(sg, st, 13), col(sg, st, 12), AF.Exp, scale=-0.5)))
            sc.add("v", "p3x1n" + sfx, (lambda h, i, sg=sg, st=st, x1=x1: h.tensor_scalar(
                x1, x1, col(sg, st, 5), col(sg, st, 9), OP.subtract, OP.mult)),
                deps=[("p3rs2" + sfx, 0)])
            sc.add("v", "p3xn" + sfx, (lambda h, i, sg=sg, st=st, x1=x1, xn=xn: h.tensor_scalar(
                xn, x1, col(sg, st, 13), None, OP.mult)),
                deps=[("p3r3" + sfx, 0), ("p3x1n" + sfx, 0), uc("p3tx", sg, st)])
            for hh in range(2):
                def f_lg(h, i, xn=xn, hh=hh):
                    d = xn[:, hh * 512:(hh + 1) * 512]
                    return h.tensor_tensor(d, d, m["lng"][hh][:, 0:512], OP.mult)
                sc.add("pl", "p3lg" + str(hh) + sfx, f_lg, deps=[("p3xn" + sfx, 0)])
                def f_lb(h, i, xn=xn, hh=hh):
                    d = xn[:, hh * 512:(hh + 1) * 512]
                    return h.tensor_tensor(d, d, m["lnb"][hh][:, 0:512], OP.add)
                sc.add("pl", "p3lb" + str(hh) + sfx, f_lb)
            def f_tx(h, i, xn=xn, sg=sg):
                r = None
                for c in range(8):
                    r = h.transpose(ptT[:, sg * 1024 + c * 128:sg * 1024 + c * 128 + 128],
                                    xn[:, c * 128:c * 128 + 128], a["ident"][:, :])
                return r
            sc.add("pe", "p3tx" + sfx, f_tx,
                   deps=[("p3lb1" + sfx, 0), ("p3et" + sfx, 0), ("p3ex" + sfx, -1)])
            def f_ex(h, i, sg=sg, st=st):
                rwait(h, i, a["s_dep"], "xd", 384, a["_p3d0"] + 192 * sg - 192)
                dst = m["xnfm_st"][sg].rearrange("p (c s) -> p c s", s=512)[:, :, st * 128:st * 128 + 128]
                src = ptT[:, sg * 1024:(sg + 1) * 1024].rearrange("p (c t) -> p c t", t=128)
                return h.activation(dst, src, AF.Identity)
            sc.add("a", "p3ex" + sfx, f_ex, deps=[("p3tx" + sfx, 0), ("p3ga" + sfx, -1)])
            def f_ga(h, i, sg=sg, st=st):
                h.wait_ge(a["s_ia"], a["_p3ia"])
                r = None
                for c in range(8):
                    r = h.matmul(psG[sg], m["xnfm_st"][sg][:, c * 512 + st * 128:c * 512 + st * 128 + 128],
                                 m["gatew"][:, c * 6:c * 6 + 6], start=(c == 0), stop=(c == 7))
                return r
            sc.add("pe", "p3ga" + sfx, f_ga, deps=[("p3ex" + sfx, 0), uc("p3g1", sg, st, 4)])
            gt = m["gt"][:, (sg * 4 + st) * 32:(sg * 4 + st) * 32 + 30]
            sc.add("v", "p3g1" + sfx, (lambda h, i, gt=gt, sg=sg: h.tensor_tensor(
                gt[:, 0:6], psG[sg], m["gateb"], OP.add)), deps=[("p3ga" + sfx, 0)])
            sc.add("v", "p3g2" + sfx, (lambda h, i, gt=gt, sg=sg, st=st: h.reduce_max(
                col(sg, st, 14), gt[:, 0:6], axis=AX)))
            sc.add("v", "p3g3" + sfx, (lambda h, i, gt=gt, sg=sg, st=st: h.tensor_scalar(
                gt[:, 8:14], gt[:, 0:6], col(sg, st, 14), None, OP.is_equal)))
            sc.add("v", "p3g4" + sfx, (lambda h, i, gt=gt: h.scalar_tensor_tensor(
                gt[:, 24:30], gt[:, 8:14], -1e30, gt[:, 0:6], OP.mult, OP.add)))
            sc.add("v", "p3g5" + sfx, (lambda h, i, gt=gt, sg=sg, st=st: h.reduce_max(
                col(sg, st, 15), gt[:, 24:30], axis=AX)))
            sc.add("v", "p3g6" + sfx, (lambda h, i, gt=gt, sg=sg, st=st: h.tensor_scalar(
                gt[:, 16:22], gt[:, 0:6], col(sg, st, 15), None, OP.is_equal)))
            sc.add("v", "p3g7" + sfx, (lambda h, i, sg=sg, st=st: h.tensor_tensor(
                col(sg, st, 16), col(sg, st, 15), col(sg, st, 14), OP.subtract)))
            sc.add("a", "p3g8" + sfx, (lambda h, i, sg=sg, st=st: h.activation(
                col(sg, st, 16), col(sg, st, 16), AF.Exp, scale=1.0)), deps=[("p3g7" + sfx, 0)])
            sc.add("v", "p3g9" + sfx, (lambda h, i, sg=sg, st=st: h.tensor_scalar(
                col(sg, st, 17), col(sg, st, 16), 1.0, None, OP.add)), deps=[("p3g8" + sfx, 0)])
            sc.add("v", "p3g10" + sfx, (lambda h, i, sg=sg, st=st: h.reciprocal(
                col(sg, st, 17), col(sg, st, 17))))
            sc.add("v", "p3g11" + sfx, (lambda h, i, sg=sg, st=st: h.tensor_tensor(
                col(sg, st, 16), col(sg, st, 16), col(sg, st, 17), OP.mult)))
            sc.add("v", "p3g12" + sfx, (lambda h, i, gt=gt, sg=sg, st=st: h.tensor_scalar(
                gt[:, 8:14], gt[:, 8:14], col(sg, st, 17), None, OP.mult)),
                deps=[("p3g11" + sfx, 0)])
            sc.add("v", "p3g13" + sfx, (lambda h, i, gt=gt, sg=sg, st=st: h.scalar_tensor_tensor(
                gt[:, 8:14], gt[:, 16:22], col(sg, st, 16), gt[:, 8:14], OP.mult, OP.add)))
    def f_wtall(h, i):
        d = m["wts"].rearrange("p (e j d) -> p e j d", e=8, j=8)[:, 2:8, DS(i, 1), :]
        src = m["gt"].rearrange("p (u o c) -> p u o c", o=1, c=32)[:, 0:8, :, 8:14]
        return h.tensor_copy(d, src.rearrange("p u o e -> p e o u"))
    sc.add("pl", "p3wtall0", f_wtall,
           deps=[("p3g13" + str(sg_) + str(st_), 0) for sg_ in range(2) for st_ in range(4)])
    sc.finalize()
    return sc


def p3_sync(q, a, B, m, sc, core_rows=8192):
    Q = SerialQ(q, a, a["_p3w0"], a["_p3d0"])
    for hh in range(2):
        Q.load(m["postw"][hh][:, 0:512], B["postwrep"][:, hh * 512:(hh + 1) * 512])
        Q.load(m["postb"][hh][:, 0:512], B["postbrep"][:, hh * 512:(hh + 1) * 512])
        Q.load(m["lng"][hh][:, 0:512], B["lngrep"][:, hh * 512:(hh + 1) * 512])
        Q.load(m["lnb"][hh][:, 0:512], B["lnbrep"][:, hh * 512:(hh + 1) * 512])
    Q.load(m["gateb"], B["gatebrep"][:, :])
    Q.load(m["cols"][:, 440:442], B["epsc2"][:, :])
    bv = sc.base.get("v", 0)
    ba = sc.base.get("a", 0)
    Sv, Sa = sc.S["v"], sc.S["a"]

    def vdone(name, i):
        eng, k = sc.idx[name]
        return (bv if eng == "v" else ba) + i * (Sv if eng == "v" else Sa) + k

    for s in range(16):
        i, sg = s // 2, s % 2

        def x1store(st, s=s, sg=sg, i=i):
            q.wait_ge(a["s_v"], vdone("p3x1n" + str(sg) + str(st), i))
            Q.store(B["x1h"][(s * 4 + st) * 128:(s * 4 + st + 1) * 128, :], m["x1"][sg][:, :])
        for st in range(4):
            u = s * 4 + st
            if u >= 8:
                q.wait_ge(a["s_v"], vdone("p3ap" + str(sg) + str(st), i - 1))
            Q.load(m["xqw_tm"][sg][:, st * 1024:(st + 1) * 1024],
                   B["xqwfm"][:, u * 128:(u + 1) * 128], transpose=True)
            if u >= 8:
                q.wait_ge(a["s_v"], vdone("p3gg" + str(sg) + str(st), i - 1))
            Q.load(m["g_tm"][sg][:, st * 1024:(st + 1) * 1024],
                   B["gfm"][:, u * 128:(u + 1) * 128], transpose=True)
            if st >= 1:
                q.wait_ge(a["s_v"], vdone("p3res1" + str(sg) + str(st - 1), i))
            elif s >= 2:
                q.wait_ge(a["s_v"], vdone("p3res1" + str(sg) + "3", i - 1))
            Q.load(m["x_tm"][sg][:, :], B["x"][u * 128:(u + 1) * 128, :])
            if st >= 1:
                x1store(st - 1)
        x1store(3)
        q.wait_ge(a["s_a"], vdone("p3ex" + str(sg) + "3", i))
        for c in range(8):
            Q.store(B["xnfmh"][c * 128:(c + 1) * 128, s * 512:(s + 1) * 512],
                    m["xnfm_st"][sg][:, c * 512:(c + 1) * 512])
    a["_p3w_end"] = Q.w
    a["_p3d_end"] = Q.d


def p3_weights(pl, a, B, m, ia0):
    n = ia0
    for c in range(8):
        if n > 0:
            pl.wait_ge(a["s_ia"], n)
        pl.dma_start(out=m["woT"][:, c * 1024:(c + 1) * 1024],
                     in_=B["woT"][c * 128:(c + 1) * 128, :]).then_inc(a["s_ia"], 16)
        n += 16
    pl.wait_ge(a["s_ia"], n)
    gw = B["gatew"].rearrange("(c p) e -> p c e", p=128)
    pl.dma_start(out=m["gatew"].rearrange("p (c e) -> p c e", e=6), in_=gw).then_inc(a["s_ia"], 16)


def build_p4(a, m, ps1, ps2, nc):
    sc = Sched()

    def f_wc(h, i, g=0):
        src = m["wts"].rearrange("p (s d) -> p s d", d=8)[:, DS(i, 1), :]
        return h.tensor_copy(a["lrqc4"][:, 0:8].rearrange("p (o d) -> p o d", o=1), src)
    sc.add("v", "p4wc0", f_wc)
    for sg in range(2):
        sfx = str(sg)
        xnt = m["e_xn"][sg]
        for hbk in range(16):
            hc, hb = hbk // 8, hbk % 8
            def f_m1(h, i, sg=sg, hc=hc, hb=hb, hbk=hbk, xnt=xnt):
                if hbk == 0:
                    wdiv8(h, i, 416, a["_p4ia0"] + 416, a["s_ia"])
                    rwait(h, i, a["s_win"], "xw", 384, a["_p4w0"] + 16 * (12 * sg + 8))
                ps = ps1[hbk % 2]
                r = None
                for kb in range(8):
                    lh = m["ew1"][kb // 4][:, (kb % 4) * 2048 + hc * 1024 + hb * 128:
                                           (kb % 4) * 2048 + hc * 1024 + hb * 128 + 128]
                    r = h.matmul(ps, lh, xnt[:, kb * 512:kb * 512 + 512],
                                 start=(kb == 0), stop=(kb == 7))
                return r
            if hbk >= 2:
                d = [("p4h" + str(hbk - 2) + sfx, 0)]
            else:
                d = [("p4h" + str(14 + hbk) + str(1 - sg), 0 if sg == 1 else -1)]
            if hbk == 0:
                d.append(("p4m2_31" + str(1 - sg), 0 if sg == 1 else -1))
            sc.add("pe", "p4m1_" + str(hbk) + sfx, f_m1, deps=d)
            def f_h(h, i, hc=hc, hb=hb, hbk=hbk):
                return h.activation(m["h1"][hc][:, hb * 512:hb * 512 + 512], ps1[hbk % 2], AF.Gelu,
                                    bias=m["eb1"][:, hbk:hbk + 1], scale=1.0)
            sc.add("a", "p4h" + str(hbk) + sfx, f_h, deps=[("p4m1_" + str(hbk) + sfx, 0)])
        for st in range(4):
            for hh in range(2):
                def f_m2(h, i, st=st, hh=hh):
                    r = None
                    for hbk in range(16):
                        lh = m["h1"][hbk // 8][:, (hbk % 8) * 512 + st * 128:(hbk % 8) * 512 + st * 128 + 128]
                        if hbk < 8:
                            rt = m["ew2"][0][:, hbk * 1024 + hh * 512:hbk * 1024 + hh * 512 + 512]
                        else:
                            t = m["ew2"][1 + (hbk - 8) // 4]
                            rt = t[:, ((hbk - 8) % 4) * 1024 + hh * 512:((hbk - 8) % 4) * 1024 + hh * 512 + 512]
                        r = h.matmul(ps2[hh], lh, rt, start=(hbk == 0), stop=(hbk == 15))
                    return r
                if st >= 1:
                    war = ("p4mo" + str(st - 1) + str(hh) + sfx, 0)
                else:
                    war = ("p4mo3" + str(hh) + str(1 - sg), 0 if sg == 1 else -1)
                sc.add("pe", "p4m2_" + str(st) + str(hh) + sfx, f_m2,
                       deps=[("p4h15" + sfx, 0), war])
            for hh in range(2):
                def f_sc(h, i, sg=sg, st=st, hh=hh):
                    rwait(h, i, a["s_win"], "xw", 384, a["_p4w0"] + 16 * (12 * sg + 9 + st))
                    return h.tensor_tensor(m["escr"][hh][:, 0:512], ps2[hh],
                                           m["eb2"][:, hh * 512:(hh + 1) * 512], OP.add)
                sc.add("v", "p4sc" + str(st) + str(hh) + sfx, f_sc,
                       deps=[("p4m2_" + str(st) + str(hh) + sfx, 0), ("p4mo" + str(st) + str(hh) + sfx, -1)])
            for hh in range(2):
                def f_mo(h, i, sg=sg, st=st, hh=hh):
                    if hh == 0 and (not isinstance(i, int) or i * 8 + sg * 4 + st >= 1):
                        rwait(h, i, a["s_dep"], "xd", 128, a["_p4d0"] + 16 * (4 * sg + st))
                    mo = m["moeo"][:, sg * 1024 + hh * 512:sg * 1024 + hh * 512 + 512]
                    scr = m["escr"][hh][:, 0:512]
                    dlt = sg * 4 + st
                    return h.scalar_tensor_tensor(mo, scr, a["lrqc4"][:, dlt:dlt + 1], mo,
                                                  OP.mult, OP.add)
                sc.add("v", "p4mo" + str(st) + str(hh) + sfx, f_mo,
                       deps=[("p4sc" + str(st) + str(hh) + sfx, 0)])
    sc.finalize()
    return sc


def p4_sync(q, a, B, m, sc):
    Q = SerialQ(q, a, a["_p4w0"], a["_p4d0"])
    bv = sc.base["v"]
    bpe = sc.base["pe"]
    Sv, Spe = sc.S["v"], sc.S["pe"]
    for ep in range(8):
        tgt = B["abelh"] if ep < 2 else B["moeh"]
        src_old = B["abelh"] if ep < 2 else B["moeh"]
        for s in range(16):
            i = ep * 8 + s // 2
            sg = s % 2
            if ep * 16 + s >= 2:
                eng, k = sc.idx["p4m1_15" + str(sg)]
                q.wait_ge(a["s_pe"], bpe + (i - 1) * Spe + k)
            for c in range(8):
                Q.load(m["e_xn"][sg][:, c * 512:(c + 1) * 512],
                       B["xnfmh"][c * 128:(c + 1) * 128, s * 512:(s + 1) * 512])
            for st in range(4):
                u = s * 4 + st
                gu = ep * 64 + u
                if gu >= 1:
                    q.wait_ge(a["s_dep"], a["_p4d0"] + 16 * gu)
                if ep >= 1:
                    q.wait_ge(a["s_dep"], a["_p4d0"] + 16 * ((ep - 1) * 64 + u + 1))
                Q.load(m["moeo"][:, sg * 1024:(sg + 1) * 1024],
                       src_old[u * 128:(u + 1) * 128, :])
                eng, k = sc.idx["p4mo" + str(st) + "1" + str(sg)]
                q.wait_ge(a["s_v"], bv + i * Sv + k)
                Q.store(tgt[(s * 4 + st) * 128:(s * 4 + st + 1) * 128, :],
                        m["moeo"][:, sg * 1024:(sg + 1) * 1024])
    a["_p4w_end"] = Q.w
    a["_p4d_end"] = Q.d


def p4_weights(g, a, B, m, ep, pe_done_val, v_done_val, ia0):
    g.wait_ge(a["s_pe"], pe_done_val)
    g.wait_ge(a["s_v"], v_done_val)
    n = ia0
    def d(out, in_):
        nonlocal n
        if n > 0:
            g.wait_ge(a["s_ia"], n)
        g.dma_start(out=out, in_=in_).then_inc(a["s_ia"], 16)
        n += 16
    for kb in range(8):
        d(m["ew1"][kb // 4][:, (kb % 4) * 2048:(kb % 4 + 1) * 2048],
          B["ew1s"][ep, kb * 128:(kb + 1) * 128, :])
    for t in range(16):
        if t < 8:
            dst = m["ew2"][0][:, t * 1024:(t + 1) * 1024]
        else:
            dst = m["ew2"][1 + (t - 8) // 4][:, ((t - 8) % 4) * 1024:((t - 8) % 4 + 1) * 1024]
        d(dst, B["ew2s"][ep, t * 128:(t + 1) * 128, :])
    d(m["eb1"][:, :], B["eb1c"][ep, :, :])
    d(m["eb2"][:, :], B["eb2rep"][ep, :, :])
    return n


EXPM_CHUNKS = [
    [1.0, 1.0, 0.5, 1.0 / 6],
    [1.0 / math.factorial(4), 1.0 / math.factorial(5), 1.0 / math.factorial(6), 1.0 / math.factorial(7)],
    [1.0 / math.factorial(8), 1.0 / math.factorial(9), 1.0 / math.factorial(10), 1.0 / math.factorial(11)],
    [1.0 / math.factorial(12), 1.0 / math.factorial(13)],
]


def expm_tasks():
    c = EXPM_CHUNKS
    tasks = []
    for ei in range(4):
        X, XT = ("geh", ei), ("gehT", ei)
        tasks.append(("mm", XT, X, ("A2h", -1), None, ei))
        tasks.append(("mm", X, XT, ("A2Th", -1), None, ei))
        tasks.append(("mm", XT, ("A2h", -1), ("A3h", -1), None, ei))
        tasks.append(("mm", ("A2h", -1), ("A2Th", -1), ("A4Th", -1), None, ei))
        tasks.append(("pinit", X, None, ("ph", -1), None, ei))
        tasks.append(("mm", ("A4Th", -1), ("ph", -1), ("qh", -1), c[2], ei))
        tasks.append(("mm", ("A4Th", -1), ("qh", -1), ("ph", -1), c[1], ei))
        tasks.append(("mm", ("A4Th", -1), ("ph", -1), ("Eh", ei), c[0], ei))
    tasks.append(("mm", ("Eh", 3), ("Eh", 0), ("T1h", -1), None, -1))
    tasks.append(("mm", ("Eh", 2), ("T1h", -1), ("T2h", -1), None, -1))
    tasks.append(("mm", ("Eh", 1), ("T2h", -1), ("Rh", -1), None, -1))
    return tasks


def _bt(B, key):
    k, idx = key
    return B[k] if idx < 0 else B[k][idx]


def expm_make_plan(ct):
    tasks = expm_tasks()
    plan = []
    for ti, t in enumerate(tasks):
        kind = t[0]
        e = {"kind": kind, "sl": ti % 2, "ti": ti, "t": t}
        e["nload"] = (16 + (32 if t[4] is not None else 0)) if kind == "mm" else 16
        plan.append(e)
    w_base = ct["w"]
    acc = ct["w"]
    for e in plan:
        e["w0"] = acc
        acc += 16 * e["nload"]
    ct["w"] = acc
    pe_cnt, pl_cnt, d_cnt = ct["pe"], ct["v"], ct["d"]
    pe_of, pl_of, d_of = {}, {}, {}
    for e in plan:
        if e["kind"] == "mm":
            for rb in range(8):
                pe_cnt += 1
                pe_of[(e["ti"], rb)] = pe_cnt
            e["pe_end"] = pe_cnt
    for e in plan:
        for rb in range(8):
            pl_cnt += 1
            pl_of[(e["ti"], rb)] = pl_cnt
        e["pl_end"] = pl_cnt
    for e in plan:
        for rb in range(8):
            d_cnt += 16
            d_of[(e["ti"], rb)] = d_cnt
        e["d_end"] = d_cnt
    d_base = ct["d"]
    ct["pe"], ct["v"], ct["d"] = pe_cnt, pl_cnt, d_cnt
    return {"plan": plan, "pe_of": pe_of, "pl_of": pl_of, "d_of": d_of,
            "w_base": w_base, "d_base": d_base}


def gens_cast(pl, a, B, ia0):
    n = ia0
    for ei in range(4):
        for src, dst in ((B["gensE"][ei], B["geh"][ei]), (B["gensET"][ei], B["gehT"][ei])):
            if n > 0:
                pl.wait_ge(a["s_ia"], n)
            pl.dma_start(out=dst, in_=src).then_inc(a["s_ia"], 16)
            n += 16
    return n


def expm_sync(q, a, B, m, P):
    plan, pl_of = P["plan"], P["pl_of"]
    q.wait_ge(a["s_ia"], P["ia_gens"])
    Q = SerialQ(q, a, P["w_base"], P["d_base"])
    for e in plan:
        ti, sl, t = e["ti"], e["sl"], e["t"]
        if ti >= 2:
            prev = plan[ti - 2]
            if prev["kind"] == "mm":
                q.wait_ge(a["s_pe"], prev["pe_end"])
            q.wait_ge(a["s_v"], prev["pl_end"])
        for tj in range(max(0, ti - 2), ti):
            q.wait_ge(a["s_dep"], plan[tj]["d_end"])
        if e["kind"] == "mm":
            lh, rh, asm = t[1], t[2], t[4]
            for kb in range(8):
                Q.load(m["eLH"][sl][:, kb * 1024:(kb + 1) * 1024],
                       _bt(B, lh)[kb * 128:(kb + 1) * 128, :])
            for kb in range(8):
                Q.load(rhtile(m, sl, kb), _bt(B, rh)[kb * 128:(kb + 1) * 128, :])
        out = t[3]

        def st(rb):
            q.wait_ge(a["s_v"], pl_of[(ti, rb)])
            Q.store(_bt(B, out)[rb * 128:(rb + 1) * 128, :], m["eo"][rb % 2][:, :])
        if e["kind"] == "mm" and t[4] is not None:
            ei = t[5]
            srcs = [_bt(B, ("geh", ei)), B["A2h"], B["A3h"], B["idI"]]
            for rb in range(8):
                if rb >= 2:
                    st(rb - 2)
                for si in range(4):
                    Q.load(m["easm"][rb % 2][si][:, :], srcs[si][rb * 128:(rb + 1) * 128, :])
            st(6)
            st(7)
        elif e["kind"] == "pinit":
            ei = t[5]
            for rb in range(8):
                if rb >= 2:
                    st(rb - 2)
                Q.load(m["easm"][rb % 2][0][:, :], _bt(B, ("geh", ei))[rb * 128:(rb + 1) * 128, :])
                Q.load(m["easm"][rb % 2][3][:, :], B["idI"][rb * 128:(rb + 1) * 128, :])
            st(6)
            st(7)
        else:
            for rb in range(8):
                st(rb)


def expm_pe(pe, a, m, psE, P):
    plan, pl_of = P["plan"], P["pl_of"]
    for e in plan:
        if e["kind"] != "mm":
            continue
        ti, sl = e["ti"], e["sl"]
        for rb in range(8):
            pe.wait_ge(a["s_win"], e["w0"] + 16 * 16)
            if rb >= 2:
                pe.wait_ge(a["s_v"], pl_of[(ti, rb - 2)])
            else:
                for tj in range(ti - 1, -1, -1):
                    pe.wait_ge(a["s_v"], pl_of[(tj, 6 + rb)])
                    break
            ps = psE[rb % 2]
            r = None
            for kb in range(8):
                for hh in range(2):
                    r = pe.matmul(ps[hh][:, :],
                                  m["eLH"][sl][:, kb * 1024 + rb * 128:kb * 1024 + rb * 128 + 128],
                                  rhtile(m, sl, kb)[:, hh * 512:(hh + 1) * 512],
                                  start=(kb == 0), stop=(kb == 7))
            r.then_inc(a["s_pe"], 1)


def expm_pl(pl, a, m, psE, P):
    plan, pe_of, d_of = P["plan"], P["pe_of"], P["d_of"]
    c13 = EXPM_CHUNKS[3]
    for e in plan:
        ti, sl, t = e["ti"], e["sl"], e["t"]
        asm = t[4]
        for rb in range(8):
            if rb >= 2:
                pl.wait_ge(a["s_dep"], d_of[(ti, rb - 2)])
            elif ti >= 1:
                pl.wait_ge(a["s_dep"], d_of[(ti - 1, 6 + rb)])
            if e["kind"] == "mm":
                pl.wait_ge(a["s_pe"], pe_of[(ti, rb)])
                ps = psE[rb % 2]
                if asm is None:
                    pl.tensor_copy(m["eo"][rb % 2][:, 0:512], ps[0][:, :])
                    pl.tensor_copy(m["eo"][rb % 2][:, 512:1024], ps[1][:, :]).then_inc(a["s_v"], 1)
                    pl.drain()
                else:
                    cI, cA, cA2, cA3 = asm
                    pl.wait_ge(a["s_win"], e["w0"] + 16 * (16 + 4 * (rb + 1)))
                    X_, A2_, A3_, I_ = m["easm"][rb % 2]
                    r = None
                    for hh in range(2):
                        hsl = slice(hh * 512, (hh + 1) * 512)
                        scr = m["escr"][hh][:, 0:512]
                        pl.scalar_tensor_tensor(scr, X_[:, hsl], cA, ps[hh][:, :], OP.mult, OP.add)
                        pl.drain()
                        pl.scalar_tensor_tensor(scr, A2_[:, hsl], cA2, scr, OP.mult, OP.add)
                        pl.drain()
                        pl.scalar_tensor_tensor(scr, A3_[:, hsl], cA3, scr, OP.mult, OP.add)
                        pl.drain()
                        r = pl.scalar_tensor_tensor(m["eo"][rb % 2][:, hsl], I_[:, hsl], cI, scr,
                                                    OP.mult, OP.add)
                    r.then_inc(a["s_v"], 1)
                    pl.drain()
            else:
                pl.wait_ge(a["s_win"], e["w0"] + 16 * 2 * (rb + 1))
                r = None
                for hh in range(2):
                    hsl = slice(hh * 512, (hh + 1) * 512)
                    scr = m["escr"][hh][:, 0:512]
                    pl.tensor_scalar(scr, m["easm"][rb % 2][0][:, hsl], c13[1], None, OP.mult)
                    pl.drain()
                    r = pl.scalar_tensor_tensor(m["eo"][rb % 2][:, hsl], m["easm"][rb % 2][3][:, hsl],
                                                c13[0], scr, OP.mult, OP.add)
                r.then_inc(a["s_v"], 1)
                pl.drain()


def build_p5(a, m, ps2, ptT):
    sc = Sched()

    def wwin5(h, i, k, sg, st):
        rwait(h, i, a["s_win"], "xw", 384,
              a["_p5w0"] + 16 * 13 + 48 * (4 * sg + st) + 16 * k)

    def col(sg, st, k):
        o = (sg * 4 + st) * 32 + k
        return m["cols"][:, o:o + 1]

    def uc(base, sg, st, k=1):
        u = sg * 4 + st - k
        if u >= 0:
            return (base + str(u // 4) + str(u % 4), 0)
        return (base + str((u + 8) // 4) + str((u + 8) % 4), -1)

    for sg in range(2):
        for st in range(4):
            sfx = str(sg) + str(st)

            def f_c5(h, i, sg=sg, st=st):
                wwin5(h, i, 1, sg, st)
                return h.tensor_copy(m["moebf"][sg][:, :], m["moe5"][sg][:, :])
            sc.add("pl", "p5c" + sfx, f_c5, deps=[("p5t" + sfx, -1)])
            def f_t5(h, i, sg=sg):
                r = None
                for c in range(8):
                    r = h.transpose(ptT[:, sg * 1024 + c * 128:sg * 1024 + c * 128 + 128],
                                    m["moebf"][sg][:, c * 128:c * 128 + 128], a["ident"][:, :])
                return r
            sc.add("pe", "p5t" + sfx, f_t5, deps=[("p5c" + sfx, 0), ("p5e" + sfx, -1)])
            def f_e5(h, i, sg=sg):
                return h.activation(m["moefm"][sg][:, :], ptT[:, sg * 1024:(sg + 1) * 1024], AF.Identity)
            sc.add("a", "p5e" + sfx, f_e5, deps=[("p5t" + sfx, 0), ("p5rot1" + sfx, -1)])
            for hh in range(2):
                def f_r5(h, i, sg=sg, hh=hh):
                    h.wait_ge(a["s_win"], a["_p5w0"] + 16 * 13)
                    r = None
                    for c in range(8):
                        r = h.matmul(ps2[hh], m["moefm"][sg][:, c * 128:c * 128 + 128],
                                     m["Rs"][:, c * 1024 + hh * 512:c * 1024 + hh * 512 + 512],
                                     start=(c == 0), stop=(c == 7))
                    return r
                sc.add("pe", "p5rot" + str(hh) + sfx, f_r5,
                       deps=[("p5e" + sfx, 0), ("p5mx" + str(hh) + sfx, -1), ("p5d2" + str(hh) + sfx, -1)])
            for hh in range(2):
                def f_d1(h, i, sg=sg, st=st, hh=hh):
                    wwin5(h, i, 2, sg, st)
                    return h.tensor_tensor(m["sq5"][hh][:, 0:512],
                                           m["abel5"][sg][:, hh * 512:(hh + 1) * 512],
                                           m["gcw1"][hh][:, 0:512], OP.mult)
                sc.add("v", "p5d1" + str(hh) + sfx, f_d1, deps=[("p5rd1" + str(hh) + sfx, -1)])
                sc.add("v", "p5rd1" + str(hh) + sfx, (lambda h, i, sg=sg, st=st, hh=hh: h.reduce_sum(
                    col(sg, st, hh), m["sq5"][hh][:, 0:512], axis=AX)),
                    deps=[("p5d1" + str(hh) + sfx, 0)])
            for hh in range(2):
                def f_d2(h, i, sg=sg, hh=hh):
                    return h.tensor_tensor(m["sq5"][hh][:, 0:512], ps2[hh],
                                           m["gcw2"][hh][:, 0:512], OP.mult)
                sc.add("v", "p5d2" + str(hh) + sfx, f_d2,
                       deps=[("p5rot" + str(hh) + sfx, 0), ("p5rd2" + str(hh) + sfx, -1)])
                sc.add("v", "p5rd2" + str(hh) + sfx, (lambda h, i, sg=sg, st=st, hh=hh: h.reduce_sum(
                    col(sg, st, 2 + hh), m["sq5"][hh][:, 0:512], axis=AX)),
                    deps=[("p5d2" + str(hh) + sfx, 0)])
            sc.add("v", "p5s1" + sfx, (lambda h, i, sg=sg, st=st: h.tensor_tensor(
                col(sg, st, 0), col(sg, st, 0), col(sg, st, 1), OP.add)),
                deps=[("p5rd1" + "1" + sfx, 0)])
            sc.add("v", "p5s2" + sfx, (lambda h, i, sg=sg, st=st: h.tensor_tensor(
                col(sg, st, 2), col(sg, st, 2), col(sg, st, 3), OP.add)),
                deps=[("p5rd2" + "1" + sfx, 0)])
            sc.add("v", "p5s3" + sfx, (lambda h, i, sg=sg, st=st: h.tensor_tensor(
                col(sg, st, 0), col(sg, st, 0), col(sg, st, 2), OP.add)))
            sc.add("v", "p5s4" + sfx, (lambda h, i, sg=sg, st=st: h.tensor_tensor(
                col(sg, st, 0), col(sg, st, 0), m["cols"][:, 442:443], OP.add)))
            sc.add("a", "p5sg" + sfx, (lambda h, i, sg=sg, st=st: h.activation(
                col(sg, st, 1), col(sg, st, 0), AF.Exp, scale=-1.0)), deps=[("p5s4" + sfx, 0)])
            sc.add("v", "p5sp" + sfx, (lambda h, i, sg=sg, st=st: h.tensor_scalar(
                col(sg, st, 1), col(sg, st, 1), 1.0, None, OP.add)), deps=[("p5sg" + sfx, 0)])
            sc.add("v", "p5sr" + sfx, (lambda h, i, sg=sg, st=st: h.reciprocal(
                col(sg, st, 1), col(sg, st, 1))))
            for hh in range(2):
                def f_mx(h, i, sg=sg, st=st, hh=hh):
                    d = m["scr5"][hh][:, 0:512]
                    return h.tensor_tensor(d, ps2[hh], m["abel5"][sg][:, hh * 512:(hh + 1) * 512],
                                           OP.subtract)
                sc.add("v", "p5mx" + str(hh) + sfx, f_mx,
                       deps=[("p5sr" + sfx, 0), ("p5d2" + str(hh) + sfx, 0)])
                def f_mx2(h, i, sg=sg, st=st, hh=hh):
                    if hh == 0 and (not isinstance(i, int) or (i * 2 + sg) * 4 + st >= 1):
                        rwait(h, i, a["s_dep"], "xd", 128, a["_p5d0"] + 16 * (4 * sg + st))
                    d = m["y"][sg][:, hh * 512:(hh + 1) * 512]
                    return h.scalar_tensor_tensor(d, m["scr5"][hh][:, 0:512], col(sg, st, 1),
                                                  m["abel5"][sg][:, hh * 512:(hh + 1) * 512],
                                                  OP.mult, OP.add)
                sc.add("v", "p5mx2" + str(hh) + sfx, f_mx2, deps=[uc("p5on", sg, st)])
                def f_y(h, i, sg=sg, st=st, hh=hh):
                    wwin5(h, i, 3, sg, st)
                    d = m["y"][sg][:, hh * 512:(hh + 1) * 512]
                    return h.scalar_tensor_tensor(d, m["x15"][sg][:, hh * 512:(hh + 1) * 512],
                                                  2.0, d, OP.mult, OP.add)
                sc.add("v", "p5y" + str(hh) + sfx, f_y, deps=[("p5mx2" + str(hh) + sfx, 0)])
            for hh in range(2):
                def f_sq(h, i, sg=sg, hh=hh):
                    return h.tensor_tensor(m["sq5"][hh][:, 0:512], m["y"][sg][:, hh * 512:(hh + 1) * 512],
                                           m["y"][sg][:, hh * 512:(hh + 1) * 512], OP.mult)
                sc.add("pl", "p5sq" + str(hh) + sfx, f_sq,
                       deps=[("p5y" + str(hh) + sfx, 0), uc("p5rq" + str(hh), sg, st)])
                sc.add("v", "p5rq" + str(hh) + sfx, (lambda h, i, sg=sg, st=st, hh=hh: h.reduce_sum(
                    col(sg, st, 6 + hh), m["sq5"][hh][:, 0:512], axis=AX)),
                    deps=[("p5sq" + str(hh) + sfx, 0)])
            sc.add("v", "p5r1" + sfx, (lambda h, i, sg=sg, st=st: h.reduce_sum(
                col(sg, st, 5), m["y"][sg], axis=AX)), deps=[("p5y1" + sfx, 0)])
            sc.add("v", "p5mu" + sfx, (lambda h, i, sg=sg, st=st: h.tensor_scalar(
                col(sg, st, 5), col(sg, st, 5), 1.0 / 1024, None, OP.mult)))
            sc.add("v", "p5ad" + sfx, (lambda h, i, sg=sg, st=st: h.tensor_tensor(
                col(sg, st, 6), col(sg, st, 6), col(sg, st, 7), OP.add)),
                deps=[("p5rq1" + sfx, 0)])
            sc.add("v", "p5m2" + sfx, (lambda h, i, sg=sg, st=st: h.tensor_scalar(
                col(sg, st, 6), col(sg, st, 6), 1.0 / 1024, None, OP.mult)))
            sc.add("v", "p5nv" + sfx, (lambda h, i, sg=sg, st=st: h.scalar_tensor_tensor(
                col(sg, st, 7), col(sg, st, 5), col(sg, st, 5), col(sg, st, 6), OP.mult, OP.subtract)))
            sc.add("a", "p5ln" + sfx, (lambda h, i, sg=sg, st=st: h.activation(
                col(sg, st, 8), col(sg, st, 7), AF.Ln, scale=-1.0, bias=m["cols"][:, 441:442])),
                deps=[("p5nv" + sfx, 0)])
            sc.add("a", "p5rs" + sfx, (lambda h, i, sg=sg, st=st: h.activation(
                col(sg, st, 9), col(sg, st, 8), AF.Exp, scale=-0.5)))
            def f_on(h, i, sg=sg, st=st):
                return h.tensor_scalar(m["y"][sg], m["y"][sg], col(sg, st, 5), col(sg, st, 9),
                                       OP.subtract, OP.mult)
            sc.add("v", "p5on" + sfx, f_on, deps=[("p5rs" + sfx, 0)])
    sc.finalize()
    return sc


def p5_sync(q, a, B, m, sc):
    Q = SerialQ(q, a, a["_p5w0"], a["_p5d0"])
    for c in range(8):
        Q.load(m["Rs"][:, c * 1024:(c + 1) * 1024], B["Rh"][c * 128:(c + 1) * 128, :])
    for hh in range(2):
        Q.load(m["gcw1"][hh][:, 0:512], B["gcw1rep"][:, hh * 512:(hh + 1) * 512])
        Q.load(m["gcw2"][hh][:, 0:512], B["gcw2rep"][:, hh * 512:(hh + 1) * 512])
    Q.load(m["cols"][:, 442:443], B["gcbrep"][:, :])
    bv = sc.base["v"]
    bpl = sc.base["pl"]
    Sv, Spl = sc.S["v"], sc.S["pl"]

    def vdone(name, i):
        eng, k = sc.idx[name]
        b = {"v": bv, "pl": bpl}[eng]
        S = {"v": Sv, "pl": Spl}[eng]
        return b + i * S + k

    for s in range(16):
        i, sg = s // 2, s % 2

        def ostore(st, s=s, sg=sg, i=i):
            q.wait_ge(a["s_v"], vdone("p5on" + str(sg) + str(st), i))
            Q.store(B["out"][(s * 4 + st) * 128:(s * 4 + st + 1) * 128, :], m["y"][sg][:, :])
        for st in range(4):
            u = s * 4 + st

            def pwait(sem, name):
                if st >= 1:
                    q.wait_ge(sem, vdone(name + str(sg) + str(st - 1), i))
                elif s >= 2:
                    q.wait_ge(sem, vdone(name + str(sg) + "3", i - 1))
            pwait(a["s_pl"], "p5c")
            Q.load(m["moe5"][sg][:, :], B["moeh"][u * 128:(u + 1) * 128, :])
            pwait(a["s_v"], "p5mx21")
            Q.load(m["abel5"][sg][:, :], B["abelh"][u * 128:(u + 1) * 128, :])
            pwait(a["s_v"], "p5y1")
            Q.load(m["x15"][sg][:, :], B["x1h"][u * 128:(u + 1) * 128, :])
            if st >= 1:
                ostore(st - 1)
        ostore(3)
    a["_p5w_end"] = Q.w
    a["_p5d_end"] = Q.d


get_regs = shared_regs


def emit_phase(h, eng, sc, a, keys, nsteps=8):
    for k, v in keys.items():
        a[k] = v
    if not sc.S.get(eng):
        return
    sc.emit_engine(h, eng, a, 0)
    sc.alloc_regs(h, eng)
    sc.reg_init(h, eng, 1)
    a["_memo"] = {}
    with fori_shared(h, 1, nsteps) as i:
        sc.emit_engine(h, eng, a, i)
        sc.reg_step(h, eng)


def build_all(p, a, B, ct0, phases=15):
    m = p34_carve(a)
    psA = [a["pz1s"][:, 0:512], a["pzb1"][:, 0:512]]
    psB = [a["pgz1"][:, 0:512], a["pz2s"][:, 0:512]]
    psG = [a["pw"][:, 0:6], a["pab"][:, 0:6]]
    ps1 = [a["pz1s"][:, 0:512], a["pzb1"][:, 0:512]]
    ps2 = [a["pgz1"][:, 0:512], a["pz2s"][:, 0:512]]
    psE = [[a["pz1s"][:, 0:512], a["pzb1"][:, 0:512]],
           [a["pgz1"][:, 0:512], a["pz2s"][:, 0:512]]]
    ptT = a["ptT"]
    ct = dict(ct0)

    PH3, PHX, PH4, PH5 = phases & 1, phases & 2, phases & 4, phases & 8
    sc3 = build_p3(a, m, psA, psB, psG, ptT, p.nc)
    sc3.aux = {"xw": 384, "xd": 384}
    sc3.base = {e: ct[e] for e in ("pe", "v", "a", "pl")}
    k3 = {"_p3w0": ct["w"], "_p3d0": ct["d"], "_p3ia": ct["ia"] + 16 * 9}
    if PH3:
        for e in sc3.S:
            ct[e] += 8 * sc3.S[e]
        ct["w"] += 16 * (10 + 3 * 64)
        ct["d"] += 16 * 12 * 16
        ct["ia"] += 16 * 9
    bar3 = dict(ct)

    if PHX:
        PX = expm_make_plan(ct)
        PX["ia_gens"] = bar3["ia"] + 16 * 8
        ct["ia"] += 16 * 8
    else:
        PX = None
    barX = dict(ct)

    sc4 = build_p4(a, m, ps1, ps2, p.nc)
    sc4.aux = {"xw": 384, "xd": 128}
    sc4.base = {e: ct[e] for e in ("pe", "v", "a", "pl")}
    k4 = {"_p4w0": ct["w"], "_p4d0": ct["d"], "_p4ia0": ct["ia"]}
    p4_pe_ends = [sc4.base["pe"] + 8 * sc4.S["pe"] * ep for ep in range(8)]
    p4_v_ends = [sc4.base["v"] + 8 * sc4.S["v"] * ep for ep in range(8)]
    if PH4:
        for e in sc4.S:
            ct[e] += 64 * sc4.S[e]
        ct["w"] += 16 * 12 * 16 * 8
        ct["d"] += 16 * 4 * 16 * 8
        ct["ia"] += 16 * 26 * 8
    bar4 = dict(ct)

    sc5 = build_p5(a, m, ps2, ptT)
    sc5.aux = {"xw": 384, "xd": 128}
    sc5.base = {e: ct[e] for e in ("pe", "v", "a", "pl")}
    k5 = {"_p5w0": ct["w"], "_p5d0": ct["d"]}
    if PH5:
        for e in sc5.S:
            ct[e] += 8 * sc5.S[e]
        ct["w"] += 16 * (13 + 3 * 64)
        ct["d"] += 16 * 4 * 16
    ct_end = dict(ct)

    def barrier(h, vals):
        for k in ("pe", "v", "a", "pl"):
            h.wait_ge(a["s_" + k], vals[k])

    def compute(eng):
        def f(h):
            barrier(h, ct0)
            if PH3:
                if eng == "pl":
                    p3_weights(h, a, B, m, ct0['ia'])
                if eng == "v":
                    h.memset(m["wts"][:, :], 1.0)
                    h.drain()
                emit_phase(h, eng, sc3, a, k3)
            if PHX:
                if eng == "pe":
                    h.wait_ge(a["s_v"], bar3["v"])
                    expm_pe(h, a, m, psE, PX)
                if eng == "pl":
                    gens_cast(h, a, B, bar3["ia"])
                if eng == "v":
                    expm_pl(h, a, m, psE, PX)
            if PH4:
                if eng == "pl":
                    nia = barX["ia"]
                    for ep in range(8):
                        pdv = (PX["plan"][-1]["pe_end"] if PHX else barX["pe"]) if ep == 0 else p4_pe_ends[ep]
                        vdv = bar3["v"] if ep == 0 else p4_v_ends[ep]
                        nia = p4_weights(h, a, B, m, ep, pdv, vdv, nia)
                if eng == "pe":
                    h.wait_ge(a["s_v"], barX["v"])
                emit_phase(h, eng, sc4, a, k4, nsteps=64)
            if PH5:
                emit_phase(h, eng, sc5, a, k5)
            barrier(h, ct_end)
            h.wait_ge(a["s_dep"], ct_end["d"])
        return f

    def syncf(q):
        barrier(q, ct0)
        if PH3:
            for k, v in k3.items():
                a[k] = v
            p3_sync(q, a, B, m, sc3)
        if PHX:
            q.wait_ge(a["s_pe"], bar3["pe"])
            q.wait_ge(a["s_a"], bar3["a"])
            q.wait_ge(a["s_v"], bar3["v"])
            q.wait_ge(a["s_dep"], bar3["d"])
            expm_sync(q, a, B, m, PX)
        if PH4:
            for k, v in k4.items():
                a[k] = v
            p4_sync(q, a, B, m, sc4)
        if PH5:
            q.wait_ge(a["s_pe"], bar4["pe"])
            q.wait_ge(a["s_v"], bar4["v"])
            q.wait_ge(a["s_dep"], bar4["d"])
            for k, v in k5.items():
                a[k] = v
            p5_sync(q, a, B, m, sc5)
        q.wait_ge(a["s_dep"], ct_end["d"])
        barrier(q, ct_end)
    return {"compute": compute, "sync": syncf, "ct_end": ct_end, "m": m}




NT = 8192
NG = 4
NSTEPS = 128
HS_OF = [0, 1, 0, 1]


def declare_dram(nc):
    B = {}

    def inp(nm, shp, dt=F32):
        B[nm] = nc.dram_tensor(nm, shp, dt, kind="ExternalInput").ap()

    def scratch(nm, shp, dt):
        B[nm] = nc.dram_tensor(nm, shp, dt, kind="Internal").ap()

    inp("x", [NT, 1024])
    for nm in ("wqT", "wkT", "wvT", "wgT", "woT"):
        inp(nm, [1024, 1024])
    inp("lrwT", [1024, 16]); inp("lrb", [16])
    inp("lti_rep", [128, 1]); inp("invi", [128, 1]); inp("lti15_rep", [128, 1])
    inp("et", [2, 16, 16, 128], BF16)
    inp("sctab", [19, 128], mybir.dt.int32)
    inp("w1duo", [8, 128, 256]); inp("w2t", [8, 128, 256]); inp("w2cm", [8, 128, 256])
    inp("b1rep", [2, 128, 256]); inp("b2rep_bdt", [2, 128, 128])
    inp("lnw_bdt", [2, 128, 128]); inp("lnb_bdt", [2, 128, 128])
    inp("bd01", [128, 128]); inp("trilT01", [128, 128])
    inp("identc", [128, 128], BF16); inp("tril01b", [128, 128], BF16)
    inp("ident4", [128, 32], BF16); inp("ones01", [128, 128], BF16)
    inp("gatew", [1024, 6])
    for nm in ("postwrep", "postbrep", "lngrep", "lnbrep", "gcw1rep", "gcw2rep"):
        inp(nm, [128, 1024])
    inp("gatebrep", [128, 6]); inp("epsc2", [128, 2]); inp("gcbrep", [128, 1])
    inp("ew1s", [8, 1024, 2048]); inp("ew2s", [8, 2048, 1024])
    inp("eb1c", [8, 128, 16]); inp("eb2rep", [8, 128, 1024])
    inp("gensE", [4, 1024, 1024]); inp("gensET", [4, 1024, 1024])
    inp("idI", [1024, 1024], BF16)
    inp("abelh", [NT, 1024]); inp("moeh", [NT, 1024])
    scratch("xbf", [NT, 1024], BF16); scratch("xfm", [1024, NT], BF16)
    for nm in ("xqfm", "xkfm", "xvfm", "gfm", "xqwfm"):
        scratch(nm, [1024, NT], BF16)
    scratch("x1h", [NT, 1024], F32)
    scratch("xnfmh", [1024, NT], BF16)
    scratch("geh", [4, 1024, 1024], BF16); scratch("gehT", [4, 1024, 1024], BF16)
    for nm in ("A2h", "A2Th", "A3h", "A4Th", "ph", "qh", "T1h", "T2h", "Rh"):
        scratch(nm, [1024, 1024], BF16)
    scratch("Eh", [4, 1024, 1024], BF16)
    B["out"] = nc.dram_tensor("out", [NT, 1024], F32, kind="ExternalOutput").ap()
    return B


def build_program():
    nc = bass.Bass("TRN2", target_bir_lowering=False)
    B = declare_dram(nc)
    p = Prog(nc, B)
    a = scan_alloc(p, ng=NG)
    w1fb = a["w1f"].bitcast(BF16)
    w2tfb = a["w2tf"].bitcast(BF16)
    w2cfb = a["w2cf"].bitcast(BF16)
    t0 = {"xt": [w1fb[:, 0:8192], w2tfb[:, 0:8192]]}
    t1 = {
        "xin": [a["xqw"], a["xkw"]],
        "wsb": [w1fb[:, 0:8192], w2tfb[:, 0:8192]],
        "stg": [a["xvw"], a["dep"]],
        "gtmp": [w2cfb[:, 0:4096], w2cfb[:, 4096:8192], a["w1b"]],
        "lrsq": [a["w2cb"][0:16, 0:2048], a["w2cb"][0:16, 2048:4096],
                 a["gz1bd"][0:16, 0:2048], a["x2sbd"][0:16, 0:2048]],
        "etsb": a["w2tb"][0:16, :],
        "lrw": a["scur"][:, 0:128],
        "lrbt": a["b2rep"].bitcast(F32)[0:16, 0:1],
        "ppp": [a["pz1s"][:, 0:512], a["pzb1"][:, 0:512], a["pgz1"][:, 0:512], a["pz2s"][:, 0:512]],
        "plr": a["pw"][0:16, 0:512],
        "plq": a["pab"][:, 0:128],
    }
    em0 = p0_emit(p, tiles=t0)
    em1 = p1_emit(p, nproj=4, do_lr=True, do_lrq=True, tiles=t1)
    a["lrq"] = p.h_lrq
    NTT, UPT, UPP = 16, 4, 64
    P1_T = 4 * NTT * UPT + NTT * UPT + 8
    P1_O = 16 * NTT * 4 // 2

    b_ofA = [0, 0, 1, 1]
    b_ofB = [2, 2, 3, 3]
    scA = build_sched(a, b_ofA, HS_OF)
    scA = add_va_ops(scA, a, b_ofA, HS_OF)
    scB = build_sched(a, b_ofB, HS_OF)
    scB = add_va_ops(scB, a, b_ofB, HS_OF)
    S = scA.S
    NWIN = NSTEPS // 8
    NW = 3 * NG
    WIN_H = 16 * NW * NWIN * 4
    DEP_H = 16 * 4 * NG * (NWIN + 1)
    NINI_A = 8 * NG + 4 + 9 + 2

    def pre_syncA(q):
        q.wait_ge(p.h_p1_t, P1_T)
        q.wait_ge(p.h_p1_o0, P1_O)
        q.wait_ge(p.h_p1_o1, P1_O)
        q.wait_ge(p.h_p1_lq, 8)

    def pre_gpsA(g):
        g.wait_ge(p.h_p1_t, P1_T)
        g.wait_ge(p.h_p1_lq, 8)

    emA = scan_emitters(p, a, B, scA, b_ofA, HS_OF, nsteps=NSTEPS, fori=True,
                                lrq_dma=False, half=0, pre_sync=pre_syncA, pre_gps=pre_gpsA)
    NINI_A = emA["nini"]
    basesB = {"win": WIN_H, "dep": DEP_H, "ini": 16 * NINI_A,
              "pe": S["pe"] * NSTEPS, "v": S["v"] * NSTEPS,
              "a": S["a"] * NSTEPS, "pl": S["pl"] * NSTEPS}
    emB = scan_emitters(p, a, B, scB, b_ofB, HS_OF, nsteps=NSTEPS, fori=True,
                                lrq_dma=False, half=1, bases=basesB)
    ct0 = {"pe": 2 * S["pe"] * NSTEPS, "v": 2 * S["v"] * NSTEPS,
           "a": 2 * S["a"] * NSTEPS, "pl": 2 * S["pl"] * NSTEPS,
           "w": 2 * WIN_H, "d": 2 * DEP_H, "ia": 0}
    em34 = build_all(p, a, B, ct0)

    def syncf(q):
        em0["sync"](q)
        em1["sync"](q)
        emA["sync"](q)
        emB["sync"](q)
        em34["sync"](q)

    def gpsf(g):
        em0["gpsimd"](g)
        g.wait_ge(p.h_p0_fm0, 64)
        g.wait_ge(p.h_p0_fm1, 64)
        em1["gpsimd"](g)
        emA["gpsimd"](g)
        emB["gpsimd"](g)
        em34["compute"]("pl")(g)

    def tensorf(t):
        em1["tensor"](t)
        t.wait_ge(p.h_p1_o0, P1_O)
        t.wait_ge(p.h_p1_o1, P1_O)
        emA["tensor"](t)
        emB["tensor"](t)
        em34["compute"]("pe")(t)

    def vectorf(v):
        em1["vector"](v)
        v.wait_ge(p.h_p1_o0, P1_O)
        v.wait_ge(p.h_p1_o1, P1_O)
        emA["vector"](v)
        emB["vector"](v)
        em34["compute"]("v")(v)

    def scalarf(s):
        em1["scalar"](s)
        emA["scalar"](s)
        emB["scalar"](s)
        em34["compute"]("a")(s)

    with nc.allow_low_precision(reason="bf16 matmuls with fp32 accumulation, tolerance-tested"), \
            nc.Block() as block:
        block.sync(syncf)
        block.gpsimd(gpsf)
        block.tensor(tensorf)
        block.vector(vectorf)
        block.scalar(scalarf)
    return nc


def make_inmap(inp, core):
    hp = host_prep(inp, core)
    keys = ["x", "wqT", "wkT", "wvT", "wgT", "woT", "lrwT", "lrb", "lti_rep", "invi",
            "lti15_rep", "et", "w1duo", "w2t", "w2cm", "b1rep", "b2rep_bdt", "lnw_bdt",
            "lnb_bdt", "bd01", "trilT01", "identc", "tril01b", "ident4", "ones01",
            "gatew", "postwrep", "postbrep", "lngrep", "lnbrep", "gatebrep", "epsc2",
            "gcw1rep", "gcw2rep", "gcbrep", "ew1s", "ew2s", "eb1c", "eb2rep",
            "gensE", "gensET", "sctab"]
    im = {k: hp[k] for k in keys}
    im["idI"] = hp["idIbf"]
    im["abelh"] = np.zeros((NT, 1024), np.float32)
    im["moeh"] = np.zeros((NT, 1024), np.float32)
    return im



_CACHE = {}


def kernel(**inputs):
    if "nc" not in _CACHE:
        _CACHE["nc"] = build_program()
    nc = _CACHE["nc"]
    in_maps = [make_inmap(inputs, c) for c in range(8)]
    res = run_bass_kernel_spmd(nc, in_maps, core_ids=list(range(8)))
    outs = [np.asarray(res.results[c]["out"]) for c in range(8)]
    return np.concatenate(outs, 0).reshape(32, 2048, 1024).astype(np.float32)


# version 02 (final)
# speedup vs baseline: 1.0272x; 1.0272x over previous
import math
import numpy as np
import ml_dtypes
import concourse.bass as bass
import concourse.mybir as mybir
from concourse.bass_utils import run_bass_kernel_spmd
from concourse.alu_op_type import AluOpType
from contextlib import contextmanager





F32 = mybir.dt.float32
BF16 = mybir.dt.bfloat16
AF = mybir.ActivationFunctionType
OP = AluOpType

D = 1024
TC = 8192
NTT = TC // 512
SEQ = 2048
NH, HD, MBS, NCH = 16, 64, 16, 128
C1 = 0.79788456
C2 = 0.044715
SC_A = 2 * C1
SC_B = 2 * C1 * C2
C4 = 0.1070322243


def host_prep(inp, core):
    out = {}
    x = np.ascontiguousarray(inp["L0_attn_hidden_states"][core * 4:(core + 1) * 4].reshape(TC, D))
    out["x"] = x
    for nm, k in [("wqT", "L0_attn_Wq"), ("wkT", "L0_attn_Wk"), ("wvT", "L0_attn_Wv"),
                  ("wgT", "L0_attn_Wg"), ("woT", "L0_attn_Wo")]:
        out[nm] = np.ascontiguousarray(inp[k].T)
    out["lrwT"] = np.ascontiguousarray(inp["L0_attn_lr_w"][:, 0, :].T)
    out["lrb"] = np.ascontiguousarray(inp["L0_attn_lr_b"][:, 0])
    out["lti_rep"] = np.ascontiguousarray(np.tile(inp["L0_attn_learnable_token_idx"], 8)[:, None])
    W1 = inp["L0_attn_W1"]; W2 = inp["L0_attn_W2"]
    w1duo = np.zeros((8, 128, 256), np.float32)
    w2t = np.zeros((8, 128, 256), np.float32)
    w2cm = np.zeros((8, 2, 128, 128), np.float32)
    for a in range(8):
        w1duo[a, 0:64] = W1[2 * a]; w1duo[a, 64:128] = W1[2 * a + 1]
        w2t[a, 0:64] = W2[2 * a].T; w2t[a, 64:128] = W2[2 * a + 1].T
        for j in range(2):
            w2cm[a, j, :, 0:64] = W2[2 * a, 128 * j:128 * (j + 1), :]
            w2cm[a, j, :, 64:128] = W2[2 * a + 1, 128 * j:128 * (j + 1), :]
    out["w1duo"] = w1duo; out["w2t"] = w2t; out["w2cm"] = np.ascontiguousarray(w2cm.transpose(0, 2, 1, 3).reshape(8, 128, 256))
    b1 = inp["L0_attn_b1"][:, 0, :]; b2 = inp["L0_attn_b2"][:, 0, :]
    b1rep = np.zeros((2, 128, 256), np.float32); b2rep = np.zeros((2, 128, 64), np.float32)
    lnw_nj = np.zeros((2, 128, 64), np.float32); lnb_nj = np.zeros((2, 128, 64), np.float32)
    for hs in range(2):
        for p in range(8):
            h = 8 * hs + p
            b1rep[hs, 16 * p:16 * p + 16] = b1[h]
            b2rep[hs, 16 * p:16 * p + 16] = b2[h]
            lnw_nj[hs, 16 * p:16 * p + 16] = inp["L0_attn_norm_w"][h]
            lnb_nj[hs, 16 * p:16 * p + 16] = inp["L0_attn_norm_b"][h]
    out["b1rep"] = b1rep; out["b2rep"] = b2rep
    out["lnw_nj"] = lnw_nj; out["lnb_nj"] = lnb_nj
    out["postw"] = inp["L0_attn_post_w"]; out["postb"] = inp["L0_attn_post_b"]
    out["ffn_lng"] = inp["L0_ffn_ln_g"]; out["ffn_lnb"] = inp["L0_ffn_ln_b"]
    out["aw1"] = inp["L0_ffn_aw1"]; out["ab1"] = inp["L0_ffn_ab1"]
    out["aw2"] = inp["L0_ffn_aw2"]; out["ab2"] = inp["L0_ffn_ab2"]
    out["gatew"] = inp["L0_ffn_gate_w"]; out["gateb"] = inp["L0_ffn_gate_b"]
    out["ew1"] = inp["L0_ffn_ew1"]; out["eb1"] = inp["L0_ffn_eb1"]
    out["ew2"] = inp["L0_ffn_ew2"]; out["eb2"] = inp["L0_ffn_eb2"]
    out["gens"] = inp["L0_ffn_gens"]
    out["gcw"] = inp["L0_ffn_gc_w"][:, 0]; out["gcb"] = inp["L0_ffn_gc_b"]
    import ml_dtypes
    out["ident"] = np.eye(128, dtype=ml_dtypes.bfloat16)
    out["invi"] = np.ascontiguousarray(np.tile(1.0 / np.arange(1, 17, dtype=np.float32), 8)[:, None])
    tril = np.tril(np.ones((16, 16), np.float32))
    trilbd = np.zeros((128, 128), np.float32)
    onesbd = np.zeros((128, 128), np.float32)
    for p in range(8):
        trilbd[16 * p:16 * p + 16, 16 * p:16 * p + 16] = tril
        onesbd[16 * p:16 * p + 16, 16 * p:16 * p + 16] = 1.0
    out["trilbd"] = trilbd; out["onesbd"] = onesbd
    b2 = inp["L0_attn_b2"][:, 0, :]
    b2rep_bdt = np.zeros((2, 128, 128), np.float32)
    lnw_bdt = np.zeros((2, 128, 128), np.float32)
    lnb_bdt = np.zeros((2, 128, 128), np.float32)
    for hs in range(2):
        for pp in range(8):
            h = 8 * hs + pp
            co = 64 * (pp % 2)
            b2rep_bdt[hs, 16 * pp:16 * pp + 16, co:co + 64] = b2[h]
            lnw_bdt[hs, 16 * pp:16 * pp + 16, co:co + 64] = inp["L0_attn_norm_w"][h]
            lnb_bdt[hs, 16 * pp:16 * pp + 16, co:co + 64] = inp["L0_attn_norm_b"][h]
    out["b2rep_bdt"] = b2rep_bdt; out["lnw_bdt"] = lnw_bdt; out["lnb_bdt"] = lnb_bdt
    bd01 = np.zeros((128, 128), np.float32)
    for pp in range(8):
        bd01[16 * pp:16 * pp + 16, 64 * (pp % 2):64 * (pp % 2) + 64] = 1.0
    out["bd01"] = bd01
    trl = np.tril(np.ones((16, 16), np.float32))
    trilT01 = np.zeros((128, 128), np.float32)
    tril01 = np.zeros((128, 128), np.float32)
    ones01 = np.zeros((128, 128), np.float32)
    for pp in range(8):
        trilT01[16 * pp:16 * pp + 16, 16 * pp:16 * pp + 16] = trl.T
        tril01[16 * pp:16 * pp + 16, 16 * pp:16 * pp + 16] = trl
        ones01[16 * pp:16 * pp + 16, 16 * pp:16 * pp + 16] = 1.0
    out["trilT01"] = trilT01
    out["lti15_rep"] = np.full((128, 1), float(inp["L0_attn_learnable_token_idx"][15]), np.float32)
    import ml_dtypes as _md
    out["tril01b"] = tril01.astype(_md.bfloat16)
    out["ones01"] = ones01.astype(_md.bfloat16)
    out["identc"] = np.eye(128).astype(_md.bfloat16)
    out["ident4"] = np.ascontiguousarray(np.tile(np.eye(32), (4, 1))).astype(_md.bfloat16)
    et = np.zeros((2, 16, 16, 128), np.float32)
    for hs in range(2):
        for t in range(16):
            for p in range(8):
                et[hs, t, 8 * hs + p, 16 * p + t] = 1.0
    out["et"] = et.astype(ml_dtypes.bfloat16)
    out["postwrep"] = np.ascontiguousarray(np.tile(inp["L0_attn_post_w"][None, :], (128, 1)))
    out["postbrep"] = np.ascontiguousarray(np.tile(inp["L0_attn_post_b"][None, :], (128, 1)))
    out["lngrep"] = np.ascontiguousarray(np.tile(inp["L0_ffn_ln_g"][None, :], (128, 1)))
    out["lnbrep"] = np.ascontiguousarray(np.tile(inp["L0_ffn_ln_b"][None, :], (128, 1)))
    out["gatebrep"] = np.ascontiguousarray(np.tile(inp["L0_ffn_gate_b"][None, :], (128, 1)))
    out["epsc2"] = np.ascontiguousarray(np.tile(np.array([[1e-6, 1e-5]], np.float32), (128, 1)))
    gcw = inp["L0_ffn_gc_w"][:, 0]
    out["gcw1rep"] = np.ascontiguousarray(np.tile(gcw[None, :1024], (128, 1)))
    out["gcw2rep"] = np.ascontiguousarray(np.tile(gcw[None, 1024:], (128, 1)))
    out["gcbrep"] = np.full((128, 1), float(inp["L0_ffn_gc_b"][0]), np.float32)
    aw1 = inp["L0_ffn_aw1"]; aw2 = inp["L0_ffn_aw2"]
    ab1 = inp["L0_ffn_ab1"]; ab2 = inp["L0_ffn_ab2"]
    ew1s = np.concatenate([aw1[None, :, 0:2048], aw1[None, :, 2048:4096],
                           inp["L0_ffn_ew1"]], 0)
    ew2s = np.concatenate([aw2[None, 0:2048, :], aw2[None, 2048:4096, :],
                           inp["L0_ffn_ew2"]], 0)
    out["ew1s"] = np.ascontiguousarray(ew1s)
    out["ew2s"] = np.ascontiguousarray(ew2s)
    eb1s = np.concatenate([ab1[None, 0:2048], ab1[None, 2048:4096], inp["L0_ffn_eb1"]], 0)
    out["eb1c"] = np.ascontiguousarray(eb1s.reshape(8, 16, 128).transpose(0, 2, 1))
    eb2s = np.concatenate([ab2[None, :], np.zeros((1, 1024), np.float32), inp["L0_ffn_eb2"]], 0)
    out["eb2rep"] = np.ascontiguousarray(np.tile(eb2s[:, None, :], (1, 128, 1)))
    g = inp["L0_ffn_gens"]
    out["gensE"] = np.ascontiguousarray(np.stack([g[3], g[0].T, g[1].T, g[2].T]))
    out["gensET"] = np.ascontiguousarray(np.stack([g[3].T, g[0], g[1], g[2]]))
    out["idIbf"] = np.eye(1024).astype(ml_dtypes.bfloat16)
    ii = np.arange(128)
    rows = []
    for gg in range(4):
        rows.append(gg * 16 + ((ii // 8) % 2) * 8 + ii % 8)
    for gg in range(4):
        rows.append(gg * 128 + ii)
    rows.append(768 * (ii // 8 + 1))
    rows.append(256 * (ii // 8))
    for dd in range(8):
        rows.append(8 * ii + dd)
    rows.append(416 * (ii // 8 + 1))
    out["sctab"] = np.stack(rows).astype(np.int32)
    return out


class Ctr:
    def __init__(self):
        self.v = {}
    def inc(self, sem, n=1):
        k = id(sem)
        self.v[k] = self.v.get(k, 0) + n
        return self.v[k]
    def val(self, sem):
        return self.v.get(id(sem), 0)




def dram3(ap2d, p=128):
    return ap2d.rearrange("(j p) c -> p j c", p=p)


def sbuf3(tile_slice, cwidth):
    return tile_slice.rearrange("p (j c) -> p j c", c=cwidth)

class Prog:
    def __init__(self, nc, dram):
        self.nc = nc
        self.B = dram
        self._keep = []

    def sb(self, name, shape, dtype):
        t = self.nc.sbuf_tensor(name, shape, dtype)
        h = t.__enter__(); self._keep.append(t)
        setattr(self, "h_" + name, h)
        return h

    def ps(self, name, shape, dtype=F32):
        t = self.nc.psum_tensor(name, shape, dtype)
        h = t.__enter__(); self._keep.append(t)
        setattr(self, "h_" + name, h)
        return h

    def sem(self, name):
        t = self.nc.semaphore(name)
        h = t.__enter__(); self._keep.append(t)
        setattr(self, "h_" + name, h)
        return h


def p0_emit(p, tiles=None):
    B = p.B
    if tiles is None:
        xt_ = p.sb("p0_xt", [128, 2 * 8192], BF16)
        tdb_ = p.sb("p0_td", [128, 2 * 8192], BF16)
        xth = [xt_[:, 0:8192], xt_[:, 8192:16384]]
        tdh = [tdb_[:, 0:8192], tdb_[:, 8192:16384]]
    else:
        xth = tiles["xt"]
        tdh = tiles["xt"]
    s_cast = [p.sem("p0_cast0"), p.sem("p0_cast1")]
    s_out = [p.sem("p0_out0"), p.sem("p0_out1")]
    s_td = p.sem("p0_td")
    s_fm = [p.sem("p0_fm0"), p.sem("p0_fm1")]

    def gps(g):
        for h in range(8):
            if h >= 2:
                g.wait_ge(s_out[h % 2], 16 * (h // 2))
            src = dram3(B["x"][h * 1024:(h + 1) * 1024, :])
            g.dma_start(out=sbuf3(xth[h % 2], 1024), in_=src).then_inc(s_cast[h % 2], 16)

    def sync(q):
        for h in range(8):
            q.wait_ge(s_cast[h % 2], 16 * (h // 2 + 1))
            dst = dram3(B["xbf"][h * 1024:(h + 1) * 1024, :])
            q.dma_start(out=dst, in_=sbuf3(xth[h % 2], 1024)).then_inc(s_out[h % 2], 16)
        q.wait_ge(s_out[0], 16 * 4)
        q.wait_ge(s_out[1], 16 * 4)
        for c in range(8):
            if c >= 2:
                q.wait_ge(s_fm[c % 2], 16 * (c // 2))
            q.dma_start(out=tdh[c % 2],
                        in_=B["xbf"][:, 128 * c:128 * (c + 1)], transpose=True).then_inc(s_td, 16)
            q.wait_ge(s_td, 16 * (c + 1))
            q.dma_start(out=B["xfm"][128 * c:128 * (c + 1), :],
                        in_=tdh[c % 2]).then_inc(s_fm[c % 2], 16)
        q.wait_ge(s_fm[0], 16 * 4)
        q.wait_ge(s_fm[1], 16 * 4)

    return {"gpsimd": gps, "sync": sync}


def p1_emit(p, nproj=4, do_lr=True, do_lrq=True, tiles=None):
    B = p.B
    if tiles is None:
        xin_ = p.sb("p1_xin", [128, 2 * 4096], BF16)
        wsb_ = p.sb("p1_w", [128, 2 * 8192], BF16)
        stg_ = p.sb("p1_stg", [128, 2 * 4096], BF16)
        gtmp_ = p.sb("p1_gt", [128, 3 * 4096], BF16)
        lrs_ = p.sb("lrs_fm", [16, 8192], BF16)
        xinh = [xin_[:, 0:4096], xin_[:, 4096:8192]]
        wsbh = [wsb_[:, 0:8192], wsb_[:, 8192:16384]]
        stgh = [stg_[:, 0:4096], stg_[:, 4096:8192]]
        gth = [gtmp_[:, 0:4096], gtmp_[:, 4096:8192], gtmp_[:, 8192:12288]]
        lrsq = [lrs_[0:16, 2048 * q:2048 * (q + 1)] for q in range(4)]
        etsb = p.sb("p1_et", [16, 2 * 16 * 128], BF16)
        lrw = p.sb("p1_lrw", [128, 128], BF16)
        lrbt = p.sb("p1_lrb", [16, 1], F32)
        ppp_ = p.ps("p1_ps", [128, 2048])
        pppv = [ppp_[:, 512 * j:512 * (j + 1)] for j in range(4)]
        plr = p.ps("p1_plr", [16, 512])
        plq = p.ps("p1_plq", [128, 128])
    else:
        xinh = tiles["xin"]
        wsbh = tiles["wsb"]
        stgh = tiles["stg"]
        gth = tiles["gtmp"]
        lrsq = tiles["lrsq"]
        etsb = tiles["etsb"]
        lrw = tiles["lrw"]
        lrbt = tiles["lrbt"]
        pppv = tiles["ppp"]
        plr = tiles["plr"]
        plq = tiles["plq"]
    lrq = p.sb("lrq", [128, 8 * 128], F32)
    tokcol = p.sb("tokcol", [128, 4], F32)

    def lrsv(tt):
        return lrsq[tt // 4][:, (tt % 4) * 512:(tt % 4) * 512 + 512]
    s_in = [p.sem("p1_in0"), p.sem("p1_in1")]
    s_w = p.sem("p1_w")
    s_t = p.sem("p1_t")
    s_ea = p.sem("p1_ea")
    s_ev = p.sem("p1_ev")
    s_o = [p.sem("p1_o0"), p.sem("p1_o1")]
    s_gd = p.sem("p1_gd")
    s_gs = p.sem("p1_gs")
    s_gf = p.sem("p1_gf")
    s_lr = p.sem("p1_lr")
    s_lq = p.sem("p1_lq")
    PROJ = ["xqfm", "xkfm", "xvfm", "gfm"]
    WKEY = ["wqT", "wkT", "wvT", "wgT"]
    UPT = 4
    UPP = NTT * UPT

    def _ev_wait(u):
        pr = u // UPP
        if pr == 0:
            return ("a", u + 1)
        if pr in (1, 2):
            return ("v", u - UPP + 1)
        it = u // UPT; tt = it - 3 * NTT; k = u % UPT
        if k < 3:
            return ("a", UPP + tt * 3 + k + 1)
        return ("g", 2 * tt + 1)

    def sync(q):
        for pr in range(nproj + (1 if do_lr else 0)):
            for tt in range(NTT):
                it = pr * NTT + tt
                if it >= 2:
                    q.wait_ge(s_t, UPT * (it - 1))
                src = dram3(B["xfm"][:, tt * 512:(tt + 1) * 512])
                q.dma_start(out=sbuf3(xinh[it % 2], 512), in_=src).then_inc(s_in[it % 2], 16)

    def gps(g):
        for pr in range(nproj):
            if pr >= 2:
                g.wait_ge(s_t, UPP * (pr - 1))
            if pr >= 1:
                g.wait_ge(s_w, 16 * pr)
            src = dram3(B[WKEY[pr]])
            g.dma_start(out=sbuf3(wsbh[pr % 2], 1024), in_=src).then_inc(s_w, 16)
        g.wait_ge(s_w, 16 * nproj)
        g.dma_start(out=sbuf3(lrw[:, :], 16), in_=dram3(B["lrwT"])).then_inc(s_w, 16)
        g.wait_ge(s_w, 16 * (nproj + 1))
        g.dma_start(out=lrbt[:, :], in_=B["lrb"].rearrange("(p o) -> p o", o=1)).then_inc(s_w, 16)
        g.wait_ge(s_w, 16 * (nproj + 2))
        g.dma_start(out=etsb[:, :].rearrange("h (a t o) -> h a t o", a=2, t=16), in_=B["et"].rearrange("a t h o -> h a t o")).then_inc(s_w, 16)
        g.wait_ge(s_w, 16 * (nproj + 3))
        g.dma_start(out=tokcol[:, 0:1], in_=B["lti_rep"]).then_inc(s_w, 16)
        g.wait_ge(s_w, 16 * (nproj + 4))
        g.dma_start(out=tokcol[:, 1:2], in_=B["invi"]).then_inc(s_w, 16)
        g.wait_ge(s_w, 16 * (nproj + 5))

    def tensor(t):
        u = 0
        for pr in range(nproj):
            t.wait_ge(s_w, 16 * (pr + 1))
            for tt in range(NTT):
                it = pr * NTT + tt
                t.wait_ge(s_in[it % 2], 16 * (it // 2 + 1))
                for k in range(UPT):
                    if u >= 2:
                        sk, val = _ev_wait(u - 2)
                        t.wait_ge({"a": s_ea, "v": s_ev, "g": s_gs}[sk], val)
                    for half in range(2):
                        oc = 2 * k + half
                        for cc in range(8):
                            mm = t.matmul(pppv[(u % 2) * 2 + half],
                                     wsbh[pr % 2][:, cc * 1024 + oc * 128:cc * 1024 + oc * 128 + 128],
                                     xinh[it % 2][:, cc * 512:cc * 512 + 512],
                                     start=(cc == 0), stop=(cc == 7))
                            if half == 1 and cc == 7:
                                mm.then_inc(s_t, 1)
                    u += 1
        if not do_lr:
            return
        t.wait_ge(s_w, 16 * (nproj + 2))
        for tt in range(NTT):
            it = nproj * NTT + tt
            t.wait_ge(s_in[it % 2], 16 * (it // 2 + 1))
            if tt >= 1:
                t.wait_ge(s_lr, tt)
            for cc in range(8):
                mm = t.matmul(plr[:, :], lrw[:, cc * 16:cc * 16 + 16],
                         xinh[it % 2][:, cc * 512:cc * 512 + 512],
                         start=(cc == 0), stop=(cc == 7))
                if cc == 7:
                    mm.then_inc(s_t, UPT)
        if not do_lrq:
            return
        t.wait_ge(s_w, 16 * (nproj + 3))
        t.wait_ge(s_lr, NTT)
        for g_ in range(8):
            b, hs = g_ // 2, g_ % 2
            if g_ >= 1:
                t.wait_ge(s_lq, g_)
            for tk in range(16):
                rhs = lrsq[b][:, bass.DynSlice(tk, 128, 16)]
                mm = t.matmul(plq[:, :], etsb[:, (hs * 16 + tk) * 128:(hs * 16 + tk) * 128 + 128],
                         rhs, start=(tk == 0), stop=(tk == 15))
                if tk == 15:
                    mm.then_inc(s_t, 1)

    def vector(v):
        for pr in [q for q in (1, 2) if q < nproj]:
            for tt in range(NTT):
                it = pr * NTT + tt
                for k in range(UPT):
                    u = it * UPT + k
                    v.wait_ge(s_t, u + 1)
                    if k == 0 and it >= 2:
                        v.wait_ge(s_o[it % 2], 16 * (it // 2))
                    v.tensor_copy(stgh[it % 2][:, k * 1024:k * 1024 + 512], pppv[(u % 2) * 2])
                    v.drain()
                    v.tensor_copy(stgh[it % 2][:, k * 1024 + 512:k * 1024 + 1024],
                                  pppv[(u % 2) * 2 + 1]).then_inc(s_ev, 1)
        pr = 3
        for tt in (range(NTT) if nproj > 3 else []):
            it = pr * NTT + tt
            zb = stgh[it % 2][:, 0:4096]
            gu = gth[0][:, 0:4096]
            gt1 = gth[1][:, 0:4096]
            v.wait_ge(s_gs, 2 * tt + 1)
            v.tensor_tensor(gu, zb, zb, OP.mult)
            v.drain()
            v.tensor_scalar(gt1, gu, SC_B, SC_A, OP.mult, OP.add)
            v.drain()
            v.tensor_tensor(gth[2][:, 0:4096], gt1, zb, OP.mult).then_inc(s_gd, 1)
            v.wait_ge(s_gs, 2 * tt + 2)
            v.tensor_tensor(zb, gth[2][:, 0:4096], zb, OP.mult).then_inc(s_gf, 1)
        if not do_lr:
            return
        v.wait_ge(s_w, 16 * (nproj + 5))
        v.tensor_tensor(tokcol[:, 1:2], tokcol[:, 0:1], tokcol[:, 1:2], OP.add)
        v.drain()
        v.tensor_scalar(tokcol[:, 0:1], tokcol[:, 1:2], 0.0, None, OP.max)
        v.drain()
        v.tensor_scalar(tokcol[:, 2:3], tokcol[:, 0:1], -0.5, None, OP.mult)
        for g_ in (range(8) if do_lrq else []):
            v.wait_ge(s_t, nproj * UPP + NTT * UPT + g_ + 1)
            v.tensor_scalar(lrq[:, g_ * 128:g_ * 128 + 128], plq[:, :], 1.0 / 64, None, OP.mult).then_inc(s_lq, 1)

    def scalar(s):
        for pr in range(nproj):
            for tt in range(NTT):
                it = pr * NTT + tt
                if pr == 0:
                    for k in range(UPT):
                        u = it * UPT + k
                        s.wait_ge(s_t, u + 1)
                        if k == 0 and it >= 2:
                            s.wait_ge(s_o[it % 2], 16 * (it // 2))
                        s.activation(stgh[it % 2][:, k * 1024:k * 1024 + 512],
                                     pppv[(u % 2) * 2], AF.Identity)
                        s.activation(stgh[it % 2][:, k * 1024 + 512:k * 1024 + 1024],
                                     pppv[(u % 2) * 2 + 1], AF.Identity).then_inc(s_ea, 1)
                elif pr == 3:
                    for k in range(UPT):
                        u = it * UPT + k
                        s.wait_ge(s_t, u + 1)
                        if k == 0 and it >= 2:
                            s.wait_ge(s_o[it % 2], 16 * (it // 2))
                        s.activation(stgh[it % 2][:, k * 1024:k * 1024 + 512],
                                     pppv[(u % 2) * 2], AF.Identity)
                        a = s.activation(stgh[it % 2][:, k * 1024 + 512:k * 1024 + 1024],
                                     pppv[(u % 2) * 2 + 1], AF.Identity)
                        a.then_inc(s_gs if k == 3 else s_ea, 1)
                    s.wait_ge(s_gd, tt + 1)
                    s.activation(gth[2][:, 0:4096], gth[2][:, 0:4096], AF.Sigmoid).then_inc(s_gs, 1)
                if pr == 0:
                    s.wait_ge(s_ea, 4 * (tt + 1))
                elif pr in (1, 2):
                    s.wait_ge(s_ev, (pr - 1) * UPP + 4 * (tt + 1))
                else:
                    s.wait_ge(s_gf, tt + 1)
                dst = dram3(B[PROJ[pr]][:, tt * 512:(tt + 1) * 512])
                s.dma_start(out=dst, in_=sbuf3(stgh[it % 2][:, 0:4096], 512)).then_inc(s_o[it % 2], 16)
        for tt in (range(NTT) if do_lr else []):
            s.wait_ge(s_t, nproj * UPP + (tt + 1) * UPT)
            s.activation(lrsv(tt), plr[:, :], AF.Sigmoid,
                         bias=lrbt[:, 0:1]).then_inc(s_lr, 1)
        s.wait_ge(s_o[0], 16 * NTT * nproj // 2)
        s.wait_ge(s_o[1], 16 * NTT * nproj // 2)
        if do_lrq:
            s.wait_ge(s_lq, 8)

    return {"sync": sync, "gpsimd": gps, "tensor": tensor, "vector": vector, "scalar": scalar}




F32 = mybir.dt.float32
BF16 = mybir.dt.bfloat16
AF = mybir.ActivationFunctionType
OP = AluOpType
C1 = 0.79788456
C2 = 0.044715
SC_A = 2 * C1
SC_B = 2 * C1 * C2
C4 = 0.1070322243
EPS = 1e-6


def ts(i, n):
    return bass.ts(i, n)


def scan_alloc(p, ng=4):
    a = {}
    a["ng"] = ng
    for t, w in [("w1f", 1024), ("w2tf", 1024), ("w2cf", 1024)]:
        a[t] = p.sb("sc_" + t, [128, ng * w], F32)
    for t, w in [("w1b", 1024), ("w2tb", 1024), ("w2cb", 1024)]:
        a[t] = p.sb("sc_" + t, [128, ng * w], BF16)
    a["b1rep"] = p.sb("sc_b1rep", [128, ng * 256], F32)
    a["b2rep"] = p.sb("sc_b2rep", [128, ng * 128], F32)
    a["lnw"] = p.sb("sc_lnw", [128, 2 * 128], F32)
    a["lnb"] = p.sb("sc_lnb", [128, 2 * 128], F32)
    a["bd01"] = p.sb("sc_bd01", [128, 128], F32)
    a["triltnT"] = p.sb("sc_triltnT", [128, 128], F32)
    a["ident"] = p.sb("sc_ident", [128, 128], BF16)
    a["tril01b"] = p.sb("sc_tril01b", [128, 128], BF16)
    a["ident4"] = p.sb("sc_ident4", [128, 32], BF16)
    a["tgtf"] = p.sb("sc_tgtf", [128, 0], BF16) if False else p.sb("sc_tgtf", [128, ng * 64], BF16)
    a["tgtbdt"] = p.sb("sc_tgtbdt", [128, ng * 128], BF16)
    a["onest15"] = p.sb("sc_onest15", [128, 128], BF16)
    a["t15n"] = p.sb("sc_t15n", [128, 1], F32)
    a["epsc"] = p.sb("sc_epsc", [128, 1], F32)
    a["xw3"] = p.sb("sc_xw3", [128, 3 * ng * 1024], BF16)
    a["xkw"] = a["xw3"][:, 0:ng * 1024]
    a["xqw"] = a["xw3"][:, ng * 1024:2 * ng * 1024]
    a["xvw"] = a["xw3"][:, 2 * ng * 1024:3 * ng * 1024]
    a["scur"] = p.sb("sc_scur", [128, ng * 192], BF16)
    a["bdxk"] = p.sb("sc_bdxk", [128, ng * 128], BF16)
    a["bdxq"] = p.sb("sc_bdxq", [128, ng * 128], BF16)
    a["bdtgt"] = p.sb("sc_bdtgt", [128, ng * 128], BF16)
    a["z1"] = p.sb("sc_z1", [128, ng * 256], BF16)
    a["u"] = p.sb("sc_u", [128, ng * 256], BF16)
    a["wv"] = p.sb("sc_wv", [128, ng * 256], BF16)
    a["sg"] = p.sb("sc_sg", [128, ng * 256], BF16)
    a["x2"] = p.sb("sc_x2", [128, ng * 256], BF16)
    a["x2s"] = p.sb("sc_x2s", [128, ng * 256], BF16)
    a["ga"] = p.sb("sc_ga", [128, ng * 256], BF16)
    a["gb"] = p.sb("sc_gb", [128, ng * 256], BF16)
    a["gz1s"] = p.sb("sc_gz1s", [128, ng * 256], BF16)
    a["x2t"] = p.sb("sc_x2t", [128, ng * 256], BF16)
    a["z1b"] = p.sb("sc_z1b", [128, ng * 256], BF16)
    a["u2"] = p.sb("sc_u2", [128, ng * 256], BF16)
    a["sg2"] = p.sb("sc_sg2", [128, ng * 256], BF16)
    a["x2b"] = p.sb("sc_x2b", [128, ng * 256], BF16)
    a["x2bt"] = p.sb("sc_x2bt", [128, ng * 256], BF16)
    a["z2c"] = p.sb("sc_z2c", [128, ng * 128], F32)
    a["xh"] = p.sb("sc_xh", [128, ng * 128], F32)
    a["gx"] = p.sb("sc_gx", [128, ng * 128], F32)
    a["gxs"] = p.sb("sc_gxs", [128, ng * 128], F32)
    a["gz2s"] = p.sb("sc_gz2s", [128, ng * 128], BF16)
    a["gz2bdt"] = p.sb("sc_gz2bdt", [128, ng * 128], BF16)
    a["xkbdts"] = p.sb("sc_xkbdts", [128, ng * 128], BF16)
    a["gz1bd"] = p.sb("sc_gz1bd", [128, ng * 512], BF16)
    a["x2sbd"] = p.sb("sc_x2sbd", [128, ng * 512], BF16)
    a["gz2sbd"] = p.sb("sc_gz2sbd", [128, ng * 256], BF16)
    a["z2bc"] = p.sb("sc_z2bc", [128, ng * 128], F32)
    a["xh2"] = p.sb("sc_xh2", [128, ng * 128], F32)
    a["lnout"] = p.sb("sc_lnout", [128, ng * 128], BF16)
    a["cc1"] = p.sb("sc_cc1", [128, ng * 128], BF16)
    a["cc2"] = p.sb("sc_cc2", [128, ng * 128], BF16)
    a["cols"] = p.sb("sc_cols", [128, ng * 16], F32)
    a["dep"] = p.sb("sc_dep", [128, ng * 2 * 512], BF16)
    a["pz1s"] = p.ps("sc_pz1s", [128, 512])
    a["pzb1"] = p.ps("sc_pzb1", [128, 512])
    a["pgz1"] = p.ps("sc_pgz1", [128, 512])
    a["pz2s"] = p.ps("sc_pz2s", [128, 512])
    a["pw"] = p.ps("sc_pw", [128, 512])
    a["pab"] = p.ps("sc_pab", [128, 512])
    a["ptT"] = p.ps("sc_ptT", [128, 2048], BF16)
    a["sctab"] = p.sb("sctabs", [19, 128], mybir.dt.int32)
    a["lrqc4"] = p.sb("sc_lrqc4", [128, 8], F32)
    for s in ["s_win", "s_dep", "s_pe", "s_v", "s_a", "s_pl", "s_ini", "s_iv", "s_ipe", "s_ia"]:
        a[s] = p.sem("sc_" + s)
    return a




@contextmanager
def fori_shared(h, start, end):
    b = h.bass
    from concourse.bass import OrderedEngineSet
    if not hasattr(h, "_loopregs"):
        h._loopregs = b.alloc_registers("shloop_i", engines=OrderedEngineSet([h.engine]))
    regs = h._loopregs
    loop_id = b.next_id()
    loop_start = f"shfori_{loop_id}_loop"
    loop_end = f"shfori_{loop_id}_end"
    engines = OrderedEngineSet([h.engine])
    b.regs_mov(regs, start)
    b.br(loop_start, engines=engines)
    with b.body(loop_start, valid_engines=engines):
        yield b.snap(regs, donate=True, min_val=start, max_val=end - 1)
        b.regs_alu(regs, regs, 1, op=mybir.AluOpType.add)
        b.br_lt(regs, end, on_true=loop_start, on_false=loop_end, engines=engines)
    b.switch_bb(loop_end)


def prep_vals(h, a, i):
    b = h.bass
    from concourse.bass import OrderedEngineSet
    if not hasattr(h, "_xcore"):
        h._xcore = b.alloc_registers("xcore", engines=OrderedEngineSet([h.engine]))
        h._xtmp = b.alloc_registers("xtmp", engines=OrderedEngineSet([h.engine]))
    b.regs_alu(h._xcore, i, 8, op=mybir.AluOpType.python_divide)
    b.regs_alu(h._xcore, h._xcore, 2, op=mybir.AluOpType.mod)
    b.regs_alu(h._xcore, h._xcore, 8, op=mybir.AluOpType.mult)
    b.regs_alu(h._xtmp, i, 8, op=mybir.AluOpType.mod)
    b.regs_alu(h._xcore, h._xcore, h._xtmp, op=mybir.AluOpType.add)
    a["_corev"] = b.snap(h._xcore, donate=True, min_val=0, max_val=15)
    a["_iv"] = i


def wdiv8(h, i, mul, const, sem):
    if isinstance(i, int):
        v = (i // 8) * mul + const
        if v > 0:
            h.wait_ge(sem, v)
        return
    b = h.bass
    from concourse.bass import OrderedEngineSet
    if not hasattr(h, "_w8"):
        h._w8 = b.alloc_registers("w8", engines=OrderedEngineSet([h.engine]))
    b.regs_alu(h._w8, i, 8, op=mybir.AluOpType.python_divide)
    b.regs_alu(h._w8, h._w8, mul, op=mybir.AluOpType.mult)
    b.regs_alu(h._w8, h._w8, const, op=mybir.AluOpType.add)
    h.wait_ge(sem, b.snap(h._w8, donate=True, min_val=0, max_val=1 << 30))


def shared_regs(h):
    if not hasattr(h, "_p34regs"):
        h._p34regs = {t: h.alloc_register("shr_" + t)
                      for t in ("pe", "v", "a", "pl", "xw", "xd", "xs", "_w")}
    return h._p34regs


ENG_SEM = {"pe": "s_pe", "v": "s_v", "a": "s_a", "pl": "s_pl"}


class Sched:
    def __init__(self):
        self.ops = []
        self.idx = {}
        self.S = {}
        self.base = {}

    def add(self, eng, name, fn, deps=()):
        self.ops.append((eng, name, fn, list(deps)))

    def finalize(self):
        cnt = {}
        for eng, name, fn, deps in self.ops:
            cnt[eng] = cnt.get(eng, 0) + 1
            self.idx[name] = (eng, cnt[eng])
        self.S = cnt

    def wait_val(self, dep_name, delta, i):
        eng, k = self.idx[dep_name]
        return eng, (i + delta) * self.S[eng] + k

    def alloc_regs(self, h, eng):
        self._regs = getattr(self, "_regs", {})
        self._regs[eng] = shared_regs(h)

    def reg_init(self, h, eng, i0):
        rs = self._regs[eng]
        for t in ("pe", "v", "a", "pl"):
            h.reg_mov(rs[t], i0 * self.S.get(t, 0))
        for t, c in getattr(self, "aux", {}).items():
            h.reg_mov(rs[t], i0 * c)

    def reg_step(self, h, eng):
        rs = self._regs[eng]
        for t in ("pe", "v", "a", "pl"):
            if self.S.get(t, 0):
                h.reg_alu(rs[t], rs[t], self.S[t], mybir.AluOpType.add)
        for t, c in getattr(self, "aux", {}).items():
            h.reg_alu(rs[t], rs[t], c, mybir.AluOpType.add)

    def prep(self, h, eng, i):
        raise RuntimeError("use reg_init/reg_step")

    def emit_engine(self, h, eng, a, i):
        for e, name, fn, deps in self.ops:
            if e != eng:
                continue
            for (dn, delta) in deps:
                deng, k = self.idx[dn]
                S = self.S[deng]
                b = self.base.get(deng, 0)
                if isinstance(i, int):
                    val = b + (i + delta) * S + k
                    if val <= 0:
                        continue
                    h.wait_ge(a[ENG_SEM[deng]], val)
                else:
                    rs = self._regs[eng]
                    h.reg_add(rs["_w"], rs[deng], b + delta * S + k)
                    h.wait_ge(a[ENG_SEM[deng]], rs["_w"])
            instr = fn(h, i)
            if instr is not None:
                instr.then_inc(a[ENG_SEM[eng]], 1)
            else:
                h.nop().then_inc(a[ENG_SEM[eng]], 1)
            if eng in ("v", "pl", "a"):
                h.drain()


def build_sched(a, b_of, hs_of):
    ng = a["ng"]
    sc = Sched()
    G = list(range(ng))
    DS = bass.DynSlice

    def njs(t, g, w=256):
        return a[t][:, g * w:(g + 1) * w]

    def bdt(t, g):
        return a[t][:, g * 128:g * 128 + 128]

    def col(g, k):
        return a["cols"][:, g * 16 + k:g * 16 + k + 1]

    def xidx(g, i):
        assert isinstance(i, int)
        return g * 16 + ((i // 8) % 2) * 8 + (i % 8)

    def winx(t, g, i, r0=0, r1=128):
        if isinstance(i, int):
            v = a[t][r0:r1, :].rearrange("p (x q c) -> p x q c", q=4, c=16)
            return v[:, DS(xidx(g, i), 1), :, :]
        v = a[t][r0:r1, g * 1024:(g + 1) * 1024].rearrange("p (x q c) -> p x q c", q=4, c=16)
        return v[:, DS(a["_corev"], 1), :, :]

    def winf(t, g, i, r0=0, r1=128):
        if isinstance(i, int):
            v = a[t][r0:r1, :].rearrange("p (x s) -> p x s", s=64)
            return v[:, DS(xidx(g, i), 1), :]
        v = a[t][r0:r1, g * 1024:(g + 1) * 1024].rearrange("p (x s) -> p x s", s=64)
        return v[:, DS(a["_corev"], 1), :]

    def pi(i, g):
        return (i * ng + g) % 2

    def pz1(i, g): return a["pz1s"][:, 0:256]
    def pg1_(i, g): return a["pgz1"][:, 0:256]
    def pzb(i, g): return a["pzb1"][:, 0:256]
    def pz2(i, g): return a["pz2s"][:, 0:128]
    def pz2b(i, g): return a["pgz1"][:, 256:384]
    def pa_(g): return a["pab"][:, (g % 2) * 128:(g % 2) * 128 + 128]
    def pb1_(): return a["pab"][:, 256:512]

    def ptA(i, g, r):
        return a["ptT"][:, pi(i, g) * 1024 + r * 128:pi(i, g) * 1024 + r * 128 + 128]

    def ptB(i, g, r):
        return a["ptT"][:, pi(i, g) * 1024 + 512 + r * 256:pi(i, g) * 1024 + 512 + r * 256 + 256]

    def prevdep(evac, g, ww=0):
        if g >= 1:
            return (evac + str(g - 1), 0 + ww)
        return (evac + str(ng - 1), -1 + ww)

    def slotdep(evac, g, ww=0):
        if g >= 2:
            return (evac + str(g - 2), 0 + ww)
        return (evac + str(g + ng - 2), -1 + ww)

    NW = 3 * ng

    for g in G:
        def f_wcur(h, i, g=g):
            wdiv8(h, i, 16 * 4 * NW, a.get("_sc_w0", 0) + 16 * 4 * NW, a["s_win"])
            dst = a["scur"][:, g * 192:(g + 1) * 192].rearrange("p (t o s) -> p t o s", t=3, o=1)
            v3 = a["xw3"].rearrange("p (t g x s) -> p t g x s", t=3, g=a["ng"], s=64)
            if isinstance(i, int):
                src = v3[:, :, g, DS(xidx(0, i), 1), :]
            else:
                src = v3[:, :, g, DS(a["_corev"], 1), :]
            return h.tensor_copy(dst, src)
        sc.add("pl", "pl_wcur" + str(g), f_wcur,
               deps=[("pe_z1" + str(g), -1), ("pl_bdtg" + str(g), -1), ("v_dep2" + str(g), -1)])

    def scur(g, which, r0=0, r1=128):
        off = {"xk": 0, "xq": 64, "xv": 128}[which]
        return a["scur"][r0:r1, g * 192 + off:g * 192 + off + 64]

    for g in G:
        def f_bdk(h, i, g=g):
            d0 = bdt("bdxk", g)[0:64, :].rearrange("p (q c) -> p q c", c=32)
            d1 = bdt("bdxk", g)[64:128, :].rearrange("p (q c) -> p q c", c=32)
            h.tensor_copy(d0[:, :, 0:16], scur(g, "xk", 0, 64).rearrange("p (q t) -> p q t", t=16))
            return h.tensor_copy(d1[:, :, 16:32], scur(g, "xk", 64, 128).rearrange("p (q t) -> p q t", t=16))
        sc.add("pl", "pl_bdk" + str(g), f_bdk, deps=[("pe_z1" + str(g), -1), ("pl_wcur" + str(g), 0)])
    for g in G:
        def f_bdq(h, i, g=g):
            d0 = bdt("bdxq", g)[0:64, :].rearrange("p (q c) -> p q c", c=32)
            d1 = bdt("bdxq", g)[64:128, :].rearrange("p (q c) -> p q c", c=32)
            h.tensor_copy(d0[:, :, 0:16], scur(g, "xq", 0, 64).rearrange("p (q t) -> p q t", t=16))
            return h.tensor_copy(d1[:, :, 16:32], scur(g, "xq", 64, 128).rearrange("p (q t) -> p q t", t=16))
        sc.add("pl", "pl_bdq" + str(g), f_bdq, deps=[("pe_zb1" + str(g), -1), ("pl_wcur" + str(g), 0)])
    for g in G:
        def f_tgtf(h, i, g=g):
            return h.tensor_tensor(a["tgtf"][:, g * 64:g * 64 + 64], scur(g, "xv"), scur(g, "xk"), OP.subtract)
        sc.add("pl", "pl_tgtf" + str(g), f_tgtf, deps=[("pe_ttgt" + str(g), -1), ("pl_wcur" + str(g), 0)])
    for g in G:
        def f_bdtg(h, i, g=g):
            d = bdt("bdtgt", g).rearrange("p (q c) -> p q c", c=32)
            s = a["tgtf"][:, g * 64:g * 64 + 64].rearrange("p (q t) -> p q t", t=16)
            h.tensor_copy(d[0:64, :, 0:16], s[0:64])
            return h.tensor_copy(d[64:128, :, 16:32], s[64:128])
        sc.add("pl", "pl_bdtg" + str(g), f_bdtg, deps=[("pe_ttgt" + str(g), -1)])
    for g in G:
        def f_sig1(h, i, g=g):
            return h.tensor_scalar(njs("sg", g), njs("sg", g), 1.0, None, OP.add)
        sc.add("pl", "pl_sig1" + str(g), f_sig1, deps=[("a_exp1" + str(g), 0)])
    for g in G:
        def f_x2s(h, i, g=g):
            return h.tensor_scalar(njs("x2s", g), njs("x2", g), a["t15n"][:, 0:1], None, OP.mult)
        sc.add("pl", "pl_x2s" + str(g), f_x2s, deps=[("v_x2" + str(g), 0), ("pe_w2cub" + str(g), -1)])
    for g in G:
        def f_sig2(h, i, g=g):
            return h.tensor_scalar(njs("sg2", g), njs("sg2", g), 1.0, None, OP.add)
        sc.add("pl", "pl_sig2" + str(g), f_sig2, deps=[("a_exp2" + str(g), 0)])
    for g in G:
        def f_cast2(h, i, g=g):
            return h.tensor_copy(a["w2tb"][:, g * 1024:(g + 1) * 1024], a["w2tf"][:, g * 1024:(g + 1) * 1024])
        sc.add("pl", "pl_cast2" + str(g), f_cast2, deps=[("v_wrm2b" + str(g), 0), ("pe_gz1" + str(g), 0)])
    for g in G:
        def f_cast3(h, i, g=g):
            return h.tensor_copy(a["w2cb"][:, g * 1024:(g + 1) * 1024], a["w2cf"][:, g * 1024:(g + 1) * 1024])
        sc.add("pl", "pl_cast3" + str(g), f_cast3, deps=[("v_wrm3b" + str(g), 0), ("pe_zb2" + str(g), 0)])

    _EV = {"a1": "v_cc1a", "a2": "v_cc2a", "b2": "v_b2rm"}

    def pa_prev(cls, g):
        s = g % 2
        chain = [("a1", gg) for gg in G if gg % 2 == s]
        for gg in G:
            if gg % 2 == s:
                chain += [("a2", gg), ("b2", gg)]
        k = chain.index((cls, g))
        if k == 0:
            pc, pg2 = chain[-1]
            return (_EV[pc] + str(pg2), -1)
        pc, pg2 = chain[k - 1]
        return (_EV[pc] + str(pg2), 0)

    for g in G:
        def f_a1(h, i, g=g):
            r = None
            for q in range(4):
                r = h.matmul(a["pab"][q * 32:q * 32 + 32, (g % 2) * 128 + q * 32:(g % 2) * 128 + q * 32 + 32],
                             bdt("bdxk", g)[:, q * 32:q * 32 + 32],
                             bdt("bdxq", g)[:, q * 32:q * 32 + 32],
                             start=True, stop=True, tile_position=(0, q * 32))
            return r
        d = [("pl_bdk" + str(g), 0), ("pl_bdq" + str(g), 0), pa_prev("a1", g), prevdep("v_cc1a", g)]
        if g < 2:
            other_last = max(gg for gg in G if gg % 2 != g % 2) if ng > 1 else g
            d.append(("v_b2rm" + str(other_last), -1))
        sc.add("pe", "pe_a1" + str(g), f_a1, deps=d)
    for g in G:
        def f_tbdk(h, i, g=g):
            return h.transpose(ptA(i, g, 0), bdt("bdxk", g), a["ident"][:, :])
        sc.add("pe", "pe_tbdk" + str(g), f_tbdk,
               deps=[("pl_bdk" + str(g), 0), slotdep("a_exkt", g), slotdep("v_dep1", g),
                     slotdep("a_ex2bt", g)])
    for g in G:
        def f_ttgt(h, i, g=g):
            return h.transpose(ptA(i, g, 1), bdt("bdtgt", g), a["ident"][:, :])
        sc.add("pe", "pe_ttgt" + str(g), f_ttgt,
               deps=[("pl_bdtg" + str(g), 0), slotdep("a_etgt", g), ("a_exkt" + str(g), 0)])
    for g in G:
        def f_z1(h, i, g=g):
            r = None
            for q in range(4):
                r = h.matmul(pz1(i, g)[q * 32:q * 32 + 32, :],
                             bdt("bdxk", g)[:, q * 32:q * 32 + 32],
                             a["w1b"][:, g * 1024 + q * 256:g * 1024 + q * 256 + 256],
                             start=True, stop=True, tile_position=(0, q * 32))
            return r
        sc.add("pe", "pe_z1" + str(g), f_z1,
               deps=[("pl_bdk" + str(g), 0), ("a_cast1" + str(g), -1), prevdep("v_z1e", g)])
    for g in G:
        def f_tx2(h, i, g=g):
            h.transpose(ptB(i, g, 0)[:, 0:128], njs("x2", g)[:, 0:128], a["ident"][:, :])
            return h.transpose(ptB(i, g, 0)[:, 128:256], njs("x2", g)[:, 128:256], a["ident"][:, :])
        sc.add("pe", "pe_tx2" + str(g), f_tx2,
               deps=[("v_x2" + str(g), 0), slotdep("a_ex2t", g), slotdep("v_z1be", g),
                     ("a_etgt" + str(g), 0)])
    for g in G:
        def f_z2(h, i, g=g):
            r = None
            for q in range(4):
                for c in range(2):
                    r = h.matmul(pz2(i, g)[q * 32:q * 32 + 32, :],
                                 a["x2t"][:, g * 256 + c * 128 + q * 32:g * 256 + c * 128 + q * 32 + 32],
                                 a["w2cb"][:, g * 1024 + q * 256 + c * 128:g * 1024 + q * 256 + c * 128 + 128],
                                 start=(c == 0), stop=(c == 1), tile_position=(0, q * 32))
            return r
        sc.add("pe", "pe_z2" + str(g), f_z2,
               deps=[("a_ex2t" + str(g), 0), ("pl_cast3" + str(g), -1), prevdep("v_z2e", g)])
    for g in G:
        def f_tgz2(h, i, g=g):
            return h.transpose(ptA(i, g, 2), bdt("gz2s", g), a["ident"][:, :])
        sc.add("pe", "pe_tgz2" + str(g), f_tgz2,
               deps=[("v_gz2m" + str(g), 0), slotdep("a_egz2t", g), ("a_etgt" + str(g), 0)])
    for g in G:
        def f_gz1(h, i, g=g):
            r = None
            for q in range(4):
                r = h.matmul(pg1_(i, g)[q * 32:q * 32 + 32, :],
                             bdt("gz2bdt", g)[:, q * 32:q * 32 + 32],
                             a["w2tb"][:, g * 1024 + q * 256:g * 1024 + q * 256 + 256],
                             start=True, stop=True, tile_position=(0, q * 32))
            return r
        sc.add("pe", "pe_gz1" + str(g), f_gz1,
               deps=[("a_egz2t" + str(g), 0), ("pl_cast2" + str(g), -1), prevdep("v_gz1a", g),
                     prevdep("v_z2be", g)])
    for g in G:
        def f_zb1(h, i, g=g):
            r = None
            for q in range(4):
                h.matmul(pzb(i, g)[q * 32:q * 32 + 32, :],
                         bdt("cc1", g)[:, q * 32:q * 32 + 32],
                         njs("gz1s", g), start=True, stop=False, tile_position=(0, q * 32))
                r = h.matmul(pzb(i, g)[q * 32:q * 32 + 32, :],
                             bdt("bdxq", g)[:, q * 32:q * 32 + 32],
                             a["w1b"][:, g * 1024 + q * 256:g * 1024 + q * 256 + 256],
                             start=False, stop=True, tile_position=(0, q * 32))
            return r
        sc.add("pe", "pe_zb1" + str(g), f_zb1,
               deps=[("v_cc1b" + str(g), 0), ("v_gz1s" + str(g), 0), ("pl_bdq" + str(g), 0),
                     ("a_cast1" + str(g), -1), prevdep("v_z1be", g)])
    for g in G:
        def f_tx2b(h, i, g=g):
            h.transpose(ptB(i, g, 1)[:, 0:128], njs("x2b", g)[:, 0:128], a["ident"][:, :])
            return h.transpose(ptB(i, g, 1)[:, 128:256], njs("x2b", g)[:, 128:256], a["ident"][:, :])
        sc.add("pe", "pe_tx2b" + str(g), f_tx2b,
               deps=[("v_x2b" + str(g), 0), slotdep("a_ex2bt", g)])
    for g in G:
        def f_a2(h, i, g=g):
            r = None
            for q in range(4):
                for c in range(2):
                    r = h.matmul(a["pab"][q * 32:q * 32 + 32, (g % 2) * 128 + q * 32:(g % 2) * 128 + q * 32 + 32],
                                 a["x2t"][:, g * 256 + c * 128 + q * 32:g * 256 + c * 128 + q * 32 + 32],
                                 a["x2bt"][:, g * 256 + c * 128 + q * 32:g * 256 + c * 128 + q * 32 + 32],
                                 start=(c == 0), stop=(c == 1), tile_position=(0, q * 32))
            return r
        d = [("a_ex2bt" + str(g), 0), ("a_ex2t" + str(g), 0), pa_prev("a2", g)]
        sc.add("pe", "pe_a2" + str(g), f_a2, deps=d)
    for g in G:
        def f_zb2(h, i, g=g):
            r = None
            for q in range(4):
                h.matmul(pz2b(i, g)[q * 32:q * 32 + 32, :],
                         bdt("cc2", g)[:, q * 32:q * 32 + 32],
                         bdt("gz2s", g), start=True, stop=False, tile_position=(0, q * 32))
                for c in range(2):
                    r = h.matmul(pz2b(i, g)[q * 32:q * 32 + 32, :],
                                 a["x2bt"][:, g * 256 + c * 128 + q * 32:g * 256 + c * 128 + q * 32 + 32],
                                 a["w2cb"][:, g * 1024 + q * 256 + c * 128:g * 1024 + q * 256 + c * 128 + 128],
                                 start=False, stop=(c == 1), tile_position=(0, q * 32))
            return r
        sc.add("pe", "pe_zb2" + str(g), f_zb2,
               deps=[("v_cc2b" + str(g), 0), ("v_gz2m" + str(g), 0), ("a_ex2bt" + str(g), 0),
                     ("pl_cast3" + str(g), -1), prevdep("v_z2be", g), ("v_gz1a" + str(g), 0)])
    for g in G:
        def f_tdep(h, i, g=g):
            return h.transpose(ptA(i, g, 3)[:, :], bdt("lnout", g)[:, :], a["ident"][:, :])
        sc.add("pe", "pe_tdep" + str(g), f_tdep,
               deps=[("v_lo2" + str(g), 0), slotdep("v_dep1", g), ("a_egz2t" + str(g), 0)])
    for g in G:
        pwprev = ("v_wrm3b" + str(g - 1), 0) if g > 0 else ("v_wrm3b" + str(ng - 1), -1)
        def f_w1ua(h, i, g=g):
            return h.matmul(a["pw"][:, 0:512], bdt("xkbdts", g)[0:64, :],
                            a["gz1bd"][0:64, g * 512:(g + 1) * 512], start=True, stop=True)
        sc.add("pe", "pe_w1ua" + str(g), f_w1ua,
               deps=[("a_exkt" + str(g), 0), ("a_gz1bde" + str(g), 0),
                     ("a_gz1bdo" + str(g), 0), pwprev])
        def f_w1ub(h, i, g=g):
            return h.matmul(a["pw"][:, 0:512], bdt("xkbdts", g)[64:128, :],
                            a["gz1bd"][64:128, g * 512:(g + 1) * 512], start=True, stop=True,
                            tile_position=(64, 0))
        sc.add("pe", "pe_w1ub" + str(g), f_w1ub, deps=[("v_wrm1a" + str(g), 0)])
        def f_w2tua(h, i, g=g):
            return h.matmul(a["pw"][:, 0:512], a["gz2s"][0:64, g * 128:g * 128 + 128],
                            a["x2sbd"][0:64, g * 512:(g + 1) * 512], start=True, stop=True)
        sc.add("pe", "pe_w2tua" + str(g), f_w2tua,
               deps=[("v_wrm1b" + str(g), 0), ("a_x2sbde" + str(g), 0),
                     ("a_x2sbdo" + str(g), 0)])
        def f_w2tub(h, i, g=g):
            return h.matmul(a["pw"][:, 0:512], a["gz2s"][64:128, g * 128:g * 128 + 128],
                            a["x2sbd"][64:128, g * 512:(g + 1) * 512], start=True, stop=True,
                            tile_position=(64, 0))
        sc.add("pe", "pe_w2tub" + str(g), f_w2tub, deps=[("v_wrm2a" + str(g), 0)])
        def f_w2cua(h, i, g=g):
            r = None
            for c in range(2):
                r = h.matmul(a["pw"][:, c * 256:c * 256 + 256],
                             a["x2s"][0:64, g * 256 + c * 128:g * 256 + c * 128 + 128],
                             a["gz2sbd"][0:64, g * 256:(g + 1) * 256], start=True, stop=True)
            return r
        sc.add("pe", "pe_w2cua" + str(g), f_w2cua,
               deps=[("v_wrm2b" + str(g), 0), ("a_gz2sbde" + str(g), 0),
                     ("a_gz2sbdo" + str(g), 0), ("pl_x2s" + str(g), 0)])
        def f_w2cub(h, i, g=g):
            r = None
            for c in range(2):
                r = h.matmul(a["pw"][:, c * 256:c * 256 + 256],
                             a["x2s"][64:128, g * 256 + c * 128:g * 256 + c * 128 + 128],
                             a["gz2sbd"][64:128, g * 256:(g + 1) * 256], start=True, stop=True,
                             tile_position=(64, 0))
            return r
        sc.add("pe", "pe_w2cub" + str(g), f_w2cub, deps=[("v_wrm3a" + str(g), 0)])
        b1prev = ("v_b1rm" + str(g - 1), 0) if g > 0 else ("v_b1rm" + str(ng - 1), -1)
        def f_b1u(h, i, g=g):
            return h.matmul(pb1_(), a["onest15"][:, :], njs("gz1s", g), start=True, stop=True)
        sc.add("pe", "pe_b1u" + str(g), f_b1u,
               deps=[("v_gz1s" + str(g), 0), b1prev, ("v_wrm3b" + str(g), 0)])
        b2prev = pa_prev("b2", g)
        def f_b2u(h, i, g=g):
            return h.matmul(pa_(g), a["onest15"][:, :], bdt("gz2s", g), start=True, stop=True)
        sc.add("pe", "pe_b2u" + str(g), f_b2u, deps=[("v_gz2m" + str(g), 0), b2prev, ("v_b1rm" + str(g), 0)])
    def interleave(n1, n2):
        ops = sc.ops
        idx1 = [j for j, o in enumerate(ops) if o[1].startswith(n1)]
        idx2 = [j for j, o in enumerate(ops) if o[1].startswith(n2)]
        block = []
        for j1, j2 in zip(idx1, idx2):
            block.append(ops[j1]); block.append(ops[j2])
        lo = min(idx1 + idx2)
        keep = [o for j, o in enumerate(ops) if j not in set(idx1 + idx2)]
        sc.ops = keep[:lo] + block + keep[lo:]
    sc._pz1 = pz1; sc._pzb = pzb; sc._pg1 = pg1_; sc._pz2 = pz2; sc._pz2b = pz2b
    sc._pa = pa_; sc._pb1 = pb1_; sc._ptA = ptA; sc._ptB = ptB
    sc._njs = njs; sc._bdt = bdt; sc._col = col; sc._winf = winf; sc._winx = winx; sc._xidx = xidx; sc._scur = scur
    return sc


def add_va_ops(sc, a, b_of, hs_of):
    ng = a["ng"]
    G = list(range(ng))
    DS = bass.DynSlice
    njs, bdt, col = sc._njs, sc._bdt, sc._col
    pz1, pzb, pg1_, pz2, pz2b = sc._pz1, sc._pzb, sc._pg1, sc._pz2, sc._pz2b
    pa_, pb1_, ptA, ptB = sc._pa, sc._pb1, sc._ptA, sc._ptB

    def lnw(g):
        return a["lnw"][:, hs_of[g] * 128:hs_of[g] * 128 + 128]

    def lnb(g):
        return a["lnb"][:, hs_of[g] * 128:hs_of[g] * 128 + 128]

    def lrqcol(g, i):
        return a["lrqc4"][:, g:g + 1]

    for g in G:
        def f(h, i, g=g):
            return h.activation(bdt("xkbdts", g), ptA(i, g, 0), AF.Identity, scale=a["t15n"][:, 0:1])
        sc.add("a", "a_exkt" + str(g), f, deps=[("pe_tbdk" + str(g), 0), ("pe_w1ub" + str(g), -1)])
    for g in G:
        def f(h, i, g=g):
            return h.activation(bdt("tgtbdt", g), ptA(i, g, 1), AF.Identity)
        sc.add("a", "a_etgt" + str(g), f, deps=[("pe_ttgt" + str(g), 0), ("v_lt3" + str(g), -1)])
    for g in G:
        def f(h, i, g=g):
            return h.activation(njs("sg", g), njs("wv", g), AF.Exp, scale=-1.0)
        sc.add("a", "a_exp1" + str(g), f, deps=[("v_v2arg" + str(g), 0), ("v_gz1a" + str(g), -1)])
    for g in G:
        def f(h, i, g=g):
            return h.activation(njs("x2t", g), ptB(i, g, 0), AF.Identity)
        sc.add("a", "a_ex2t" + str(g), f, deps=[("pe_tx2" + str(g), 0), ("pe_a2" + str(g), -1)])
    for g in G:
        def f(h, i, g=g):
            return h.activation(col(g, 3), col(g, 2), AF.Ln, scale=-1.0, bias=a["epsc"][:, 0:1])
        sc.add("a", "a_lnv" + str(g), f, deps=[("v_st2" + str(g), 0)])
    for g in G:
        def f(h, i, g=g):
            return h.activation(col(g, 4), col(g, 3), AF.Exp, scale=-0.5)
        sc.add("a", "a_expv" + str(g), f, deps=[])
    for g in G:
        def f(h, i, g=g):
            return h.activation(bdt("gz2bdt", g), ptA(i, g, 2), AF.Identity)
        sc.add("a", "a_egz2t" + str(g), f, deps=[("pe_tgz2" + str(g), 0), ("pe_gz1" + str(g), -1)])
    for g in G:
        def f(h, i, g=g):
            return h.activation(njs("sg2", g), njs("u2", g), AF.Exp, scale=-1.0)
        sc.add("a", "a_exp2" + str(g), f, deps=[("v_v22arg" + str(g), 0), ("v_x2b" + str(g), -1)])
    for g in G:
        def f(h, i, g=g):
            return h.activation(njs("x2bt", g), ptB(i, g, 1), AF.Identity)
        sc.add("a", "a_ex2bt" + str(g), f, deps=[("pe_tx2b" + str(g), 0), ("pe_zb2" + str(g), -1)])
    for g in G:
        def f(h, i, g=g):
            return h.activation(col(g, 8), col(g, 7), AF.Ln, scale=-1.0, bias=a["epsc"][:, 0:1])
        sc.add("a", "a_lnv2" + str(g), f, deps=[("v_st22" + str(g), 0)])
    for g in G:
        def f(h, i, g=g):
            return h.activation(col(g, 9), col(g, 8), AF.Exp, scale=-0.5)
        sc.add("a", "a_expv2" + str(g), f, deps=[])
    for g in G:
        def f(h, i, g=g):
            return h.activation(a["w1b"][:, g * 1024:(g + 1) * 1024],
                                a["w1f"][:, g * 1024:(g + 1) * 1024], AF.Identity)
        sc.add("a", "a_cast1" + str(g), f, deps=[("v_wrm1b" + str(g), 0), ("pe_zb1" + str(g), 0)])

    def bdsplit(h, src, dst_t, g, blk, par):
        r = None
        for q in (par, par + 2):
            d = dst_t[32 * q:32 * q + 32, g * 2 * blk + par * blk:g * 2 * blk + par * blk + blk]
            r = h.activation(d, src[32 * q:32 * q + 32, :], AF.Identity)
        return r
    for g in G:
        for par, sfx in ((0, "e"), (1, "o")):
            def f(h, i, g=g, par=par):
                return bdsplit(h, njs("gz1s", g), a["gz1bd"], g, 256, par)
            sc.add("a", "a_gz1bd" + sfx + str(g), f,
                   deps=[("v_gz1s" + str(g), 0), ("v_wrm1b" + str(g), -1)])
    for g in G:
        for par, sfx in ((0, "e"), (1, "o")):
            def f(h, i, g=g, par=par):
                return bdsplit(h, njs("x2s", g), a["x2sbd"], g, 256, par)
            sc.add("a", "a_x2sbd" + sfx + str(g), f,
                   deps=[("pl_x2s" + str(g), 0), ("v_wrm2b" + str(g), -1)])
    for g in G:
        for par, sfx in ((0, "e"), (1, "o")):
            def f(h, i, g=g, par=par):
                return bdsplit(h, bdt("gz2s", g), a["gz2sbd"], g, 128, par)
            sc.add("a", "a_gz2sbd" + sfx + str(g), f,
                   deps=[("v_gz2m" + str(g), 0), ("v_wrm3b" + str(g), -1)])

    def V(name, fn, deps=()):
        for g in G:
            sc.add("v", name + str(g), (lambda h, i, g=g, fn=fn: fn(h, i, g)), deps=[
                (d[0] + str(g), d[1]) for d in deps])

    def f_lrq4(h, i, g=0):
        base = (2 * b_of[0] + 0) * 128
        lv = a["lrq"][:, base:base + 512].rearrange("p (gg c) -> p gg c", gg=4)
        if isinstance(i, int):
            src = lv[:, :, DS(i, 1)]
        else:
            src = lv[:, :, DS(a["_iv"], 1)]
        return h.tensor_copy(a["lrqc4"][:, 0:4].rearrange("p (gg o) -> p gg o", o=1), src)
    sc.add("v", "v_lrq40", f_lrq4)
    V("v_cc1a", lambda h, i, g: h.tensor_tensor(njs("gx", g, 128), pa_(g), a["triltnT"][:, :], OP.mult),
      deps=[("pe_a1", 0)])
    V("v_cc1b", lambda h, i, g: h.tensor_tensor(bdt("cc1", g), njs("gx", g, 128), a["triltnT"][:, :], OP.add),
      deps=[("pe_zb1", -1)])
    V("v_z1e", lambda h, i, g: h.tensor_tensor(njs("z1", g), pz1(i, g), njs("b1rep", g), OP.add),
      deps=[("pe_z1", 0)])
    V("v_uw", lambda h, i, g: h.scalar_tensor_tensor(njs("u", g), njs("z1", g), SC_B, njs("z1", g), OP.mult, OP.mult))
    V("v_v2arg", lambda h, i, g: h.scalar_tensor_tensor(njs("wv", g), njs("u", g), SC_A, njs("z1", g), OP.add, OP.mult))
    V("v_sgr", lambda h, i, g: h.reciprocal(njs("sg", g), njs("sg", g)), deps=[("pl_sig1", 0)])
    V("v_x2", lambda h, i, g: h.tensor_tensor(njs("x2", g), njs("sg", g), njs("z1", g), OP.mult),
      deps=[("pe_tx2", -1), ("pl_x2s", -1)])
    V("v_gau", lambda h, i, g: h.scalar_tensor_tensor(njs("ga", g), njs("z1", g), C4, njs("z1", g), OP.mult, OP.mult))
    V("v_gbb", lambda h, i, g: h.scalar_tensor_tensor(njs("gb", g), njs("sg", g), 1.0, njs("z1", g), OP.subtract, OP.mult))
    V("v_gbc", lambda h, i, g: h.scalar_tensor_tensor(njs("gb", g), njs("ga", g), C1, njs("gb", g), OP.add, OP.mult))
    V("v_gad", lambda h, i, g: h.tensor_scalar(njs("ga", g), njs("gb", g), -2.0, 1.0, OP.mult, OP.add))
    V("v_z2e", lambda h, i, g: h.tensor_tensor(njs("z2c", g, 128), pz2(i, g), njs("b2rep", g, 128), OP.add),
      deps=[("pe_z2", 0)])
    V("v_z2m", lambda h, i, g: h.tensor_tensor(njs("z2c", g, 128), njs("z2c", g, 128), a["bd01"][:, :], OP.mult))
    V("v_mu", lambda h, i, g: h.reduce_sum(col(g, 0), njs("z2c", g, 128), axis=mybir.AxisListType.X))
    V("v_e2a", lambda h, i, g: h.tensor_tensor(njs("gxs", g, 128), njs("z2c", g, 128), njs("z2c", g, 128), OP.mult))
    V("v_e2", lambda h, i, g: h.reduce_sum(col(g, 1), njs("gxs", g, 128), axis=mybir.AxisListType.X))
    V("v_st1", lambda h, i, g: h.tensor_scalar(a["cols"][:, g * 16:g * 16 + 2], a["cols"][:, g * 16:g * 16 + 2],
                                               1.0 / 64, None, OP.mult))
    V("v_st2", lambda h, i, g: h.scalar_tensor_tensor(col(g, 2), col(g, 0), col(g, 0), col(g, 1), OP.mult, OP.subtract))
    V("v_xh", lambda h, i, g: h.tensor_scalar(njs("xh", g, 128), njs("z2c", g, 128), col(g, 0), col(g, 4),
                                              OP.subtract, OP.mult), deps=[("a_expv", 0)])
    V("v_lt1", lambda h, i, g: h.tensor_tensor(njs("gxs", g, 128), njs("xh", g, 128), lnw(g), OP.mult))
    V("v_lt2", lambda h, i, g: h.tensor_tensor(njs("gxs", g, 128), njs("gxs", g, 128), lnb(g), OP.add))
    V("v_lt3", lambda h, i, g: h.tensor_tensor(njs("gxs", g, 128), njs("gxs", g, 128), bdt("tgtbdt", g), OP.subtract),
      deps=[("a_etgt", 0)])
    V("v_gx", lambda h, i, g: h.tensor_tensor(njs("gx", g, 128), njs("gxs", g, 128), lnw(g), OP.mult))
    V("v_m1", lambda h, i, g: h.reduce_sum(col(g, 5), njs("gx", g, 128), axis=mybir.AxisListType.X))
    V("v_m2a", lambda h, i, g: h.tensor_tensor(njs("z2c", g, 128), njs("gx", g, 128), njs("xh", g, 128), OP.mult))
    V("v_m2", lambda h, i, g: h.reduce_sum(col(g, 6), njs("z2c", g, 128), axis=mybir.AxisListType.X))
    V("v_m12", lambda h, i, g: h.tensor_scalar(a["cols"][:, g * 16 + 5:g * 16 + 7],
                                               a["cols"][:, g * 16 + 5:g * 16 + 7], -1.0 / 64, None, OP.mult))
    V("v_g1", lambda h, i, g: h.tensor_scalar(njs("gxs", g, 128), njs("gx", g, 128), col(g, 5), None, OP.add))
    V("v_g2", lambda h, i, g: h.scalar_tensor_tensor(njs("gxs", g, 128), njs("xh", g, 128), col(g, 6),
                                                     njs("gxs", g, 128), OP.mult, OP.add))
    V("v_gz2s", lambda h, i, g: h.tensor_scalar(bdt("gz2s", g), njs("gxs", g, 128), col(g, 4), lrqcol(g, i),
                                                OP.mult, OP.mult), deps=[("pe_w2tub", -1), ("a_gz2sbdo", -1)])
    V("v_gz2m", lambda h, i, g: h.tensor_tensor(bdt("gz2s", g), bdt("gz2s", g), a["bd01"][:, :], OP.mult))
    V("v_gz1a", lambda h, i, g: h.tensor_tensor(njs("wv", g), pg1_(i, g), njs("sg", g), OP.mult),
      deps=[("pe_gz1", 0)])
    V("v_gz1s", lambda h, i, g: h.tensor_tensor(njs("gz1s", g), njs("wv", g), njs("ga", g), OP.mult),
      deps=[("pe_b1u", -1)])
    V("v_z1be", lambda h, i, g: h.tensor_tensor(njs("z1b", g), pzb(i, g), njs("b1rep", g), OP.add),
      deps=[("pe_zb1", 0)])
    V("v_u2", lambda h, i, g: h.scalar_tensor_tensor(njs("u2", g), njs("z1b", g), SC_B, njs("z1b", g), OP.mult, OP.mult))
    V("v_v22arg", lambda h, i, g: h.scalar_tensor_tensor(njs("u2", g), njs("u2", g), SC_A, njs("z1b", g), OP.add, OP.mult))
    V("v_sgr2", lambda h, i, g: h.reciprocal(njs("sg2", g), njs("sg2", g)), deps=[("pl_sig2", 0)])
    V("v_x2b", lambda h, i, g: h.tensor_tensor(njs("x2b", g), njs("sg2", g), njs("z1b", g), OP.mult),
      deps=[("pe_tx2b", -1)])
    V("v_cc2a", lambda h, i, g: h.tensor_tensor(njs("gx", g, 128), pa_(g), a["triltnT"][:, :], OP.mult),
      deps=[("pe_a2", 0)])
    V("v_cc2b", lambda h, i, g: h.tensor_tensor(bdt("cc2", g), njs("gx", g, 128), a["triltnT"][:, :], OP.add),
      deps=[("pe_zb2", -1)])
    V("v_z2be", lambda h, i, g: h.tensor_tensor(njs("z2bc", g, 128), pz2b(i, g), njs("b2rep", g, 128), OP.add),
      deps=[("pe_zb2", 0)])
    V("v_z2bm", lambda h, i, g: h.tensor_tensor(njs("z2bc", g, 128), njs("z2bc", g, 128), a["bd01"][:, :], OP.mult))
    V("v_mu2", lambda h, i, g: h.reduce_sum(col(g, 5), njs("z2bc", g, 128), axis=mybir.AxisListType.X))
    V("v_e22a", lambda h, i, g: h.tensor_tensor(njs("gxs", g, 128), njs("z2bc", g, 128), njs("z2bc", g, 128), OP.mult))
    V("v_e22", lambda h, i, g: h.reduce_sum(col(g, 6), njs("gxs", g, 128), axis=mybir.AxisListType.X))
    V("v_st21", lambda h, i, g: h.tensor_scalar(a["cols"][:, g * 16 + 5:g * 16 + 7],
                                                a["cols"][:, g * 16 + 5:g * 16 + 7], 1.0 / 64, None, OP.mult))
    V("v_st22", lambda h, i, g: h.scalar_tensor_tensor(col(g, 7), col(g, 5), col(g, 5), col(g, 6), OP.mult, OP.subtract))
    V("v_xh2", lambda h, i, g: h.tensor_scalar(njs("xh2", g, 128), njs("z2bc", g, 128), col(g, 5), col(g, 9),
                                               OP.subtract, OP.mult), deps=[("a_expv2", 0)])
    V("v_lo1", lambda h, i, g: h.tensor_tensor(njs("gxs", g, 128), njs("xh2", g, 128), lnw(g), OP.mult))
    V("v_lo2", lambda h, i, g: h.tensor_tensor(bdt("lnout", g), njs("gxs", g, 128), lnb(g), OP.add),
      deps=[("pe_tdep", -1)])

    def f_dep1(h, i, g):
        wdiv8(h, i, 16 * 4 * a["ng"], a.get("_sc_d0", 0), a["s_dep"])
        if isinstance(i, int):
            d = a["dep"][:, :].rearrange("p (x q t2) -> p x q t2", q=4, t2=16)[:, DS(sc._xidx(g, i), 1), :, :]
        else:
            d = a["dep"][:, g * 1024:(g + 1) * 1024].rearrange(
                "p (x q t2) -> p x q t2", q=4, t2=16)[:, DS(a["_corev"], 1), :, :]
        pA = ptA(i, g, 3).rearrange("p (o q c) -> p o q c", o=1, c=32)
        h.tensor_copy(d, pA[:, :, :, 0:16])
        h.drain()
        return h.tensor_tensor(d, d, pA[:, :, :, 16:32], OP.add)
    V("v_dep1", f_dep1, deps=[("pe_tdep", 0)])

    def f_dep2(h, i, g):
        d = sc._winf("dep", g, i)
        return h.tensor_tensor(d, d, sc._scur(g, "xq").rearrange("p (o s) -> p o s", o=1), OP.add)
    V("v_dep2", f_dep2)

    for g in G:
        def addw(name, fn, deps=()):
            sc.add("v", name + str(g), (lambda h, i, g=g, fn=fn: fn(h, i, g)),
                   deps=[(d[0] + str(g), d[1]) for d in deps])
        addw("v_wrm1a", lambda h, i, g: h.tensor_tensor(a["w1f"][:, g * 1024:g * 1024 + 512],
             a["w1f"][:, g * 1024:g * 1024 + 512], a["pw"][:, 0:512], OP.add),
             deps=[("pe_w1ua", 0), ("a_cast1", -1)])
        addw("v_wrm1b", lambda h, i, g: h.tensor_tensor(a["w1f"][:, g * 1024 + 512:g * 1024 + 1024],
             a["w1f"][:, g * 1024 + 512:g * 1024 + 1024], a["pw"][:, 0:512], OP.add),
             deps=[("pe_w1ub", 0)])
        addw("v_wrm2a", lambda h, i, g: h.tensor_tensor(a["w2tf"][:, g * 1024:g * 1024 + 512],
             a["w2tf"][:, g * 1024:g * 1024 + 512], a["pw"][:, 0:512], OP.add),
             deps=[("pe_w2tua", 0), ("pl_cast2", -1)])
        addw("v_wrm2b", lambda h, i, g: h.tensor_tensor(a["w2tf"][:, g * 1024 + 512:g * 1024 + 1024],
             a["w2tf"][:, g * 1024 + 512:g * 1024 + 1024], a["pw"][:, 0:512], OP.add),
             deps=[("pe_w2tub", 0)])
        def wrm3(h, g, half):
            r = None
            for c in range(2):
                d = a["w2cf"][:, g * 1024 + half * 512:g * 1024 + half * 512 + 512].rearrange(
                    "p (s k) -> p s k", s=2)[:, :, c * 128:c * 128 + 128]
                s_ = a["pw"][:, c * 256:c * 256 + 256].rearrange("p (s k) -> p s k", s=2)
                r = h.tensor_tensor(d, d, s_, OP.add)
            return r
        addw("v_wrm3a", lambda h, i, g: wrm3(h, g, 0),
             deps=[("pe_w2cua", 0), ("pl_cast3", -1)])
        addw("v_wrm3b", lambda h, i, g: wrm3(h, g, 1),
             deps=[("pe_w2cub", 0)])
        addw("v_b1rm", lambda h, i, g: h.tensor_tensor(njs("b1rep", g), njs("b1rep", g), pb1_(), OP.add),
             deps=[("pe_b1u", 0)])
        addw("v_b2rm", lambda h, i, g: h.tensor_tensor(njs("b2rep", g, 128), njs("b2rep", g, 128), pa_(g), OP.add),
             deps=[("pe_b2u", 0)])
    EARLY = {
        "pe": ["pe_a1", "pe_z1"],
        "v": ["v_lrq4", "v_cc1a", "v_cc1b", "v_z1e", "v_uw", "v_v2arg", "v_sgr", "v_x2", "v_gau",
              "v_gbb", "v_gbc", "v_gad"],
        "a": ["a_exp1"],
        "pl": ["pl_wcur", "pl_bdk", "pl_bdq", "pl_tgtf", "pl_bdtg", "pl_sig1"],
    }
    LATE = {
        "pe": ["pe_tbdk", "pe_ttgt", "pe_tx2", "pe_z2", "pe_tgz2", "pe_gz1", "pe_zb1",
               "pe_tx2b", "pe_a2", "pe_zb2", "pe_tdep", "pe_w1ua", "pe_w1ub", "pe_w2tua",
               "pe_w2tub", "pe_w2cua", "pe_w2cub", "pe_b1u", "pe_b2u"],
        "v": ["v_z2e", "v_z2m", "v_mu", "v_e2a", "v_e2", "v_st1", "v_st2", "v_xh",
              "v_lt1", "v_lt2", "v_lt3", "v_gx", "v_m1", "v_m2a", "v_m2", "v_m12",
              "v_g1", "v_g2", "v_gz2s", "v_gz2m", "v_gz1a", "v_gz1s", "v_z1be", "v_u2",
              "v_v22arg", "v_sgr2", "v_x2b", "v_cc2a", "v_cc2b", "v_z2be", "v_z2bm",
              "v_mu2", "v_e22a", "v_e22", "v_st21", "v_st22", "v_xh2", "v_lo1", "v_lo2",
              "v_dep1", "v_dep2", "v_wrm1a", "v_wrm1b", "v_wrm2a", "v_wrm2b", "v_wrm3a",
              "v_wrm3b", "v_b1rm", "v_b2rm"],
        "a": ["a_exkt", "a_etgt", "a_ex2t", "a_lnv", "a_expv", "a_egz2t", "a_gz1bde",
              "a_gz1bdo", "a_exp2", "a_ex2bt", "a_lnv2", "a_expv2", "a_x2sbde",
              "a_x2sbdo", "a_gz2sbde", "a_gz2sbdo", "a_cast1"],
        "pl": ["pl_x2s", "pl_sig2", "pl_cast2", "pl_cast3"],
    }

    def keyf(op):
        eng, name = op[0], op[1]
        base, g = name[:-1], int(name[-1])
        if base in EARLY[eng]:
            return (0, EARLY[eng].index(base), g)
        return (1, g, LATE[eng].index(base))
    sc.ops = sorted(sc.ops, key=keyf)
    sc.finalize()
    return sc


def scan_emitters(p, a, B, sc, b_of, hs_of, nsteps=128, fori=True, lrq_dma=False,
                  half=0, bases=None, pre_sync=None, pre_gps=None):
    ng = a["ng"]
    NWIN = (nsteps + 7) // 8
    NW = 3 * ng
    S = sc.S
    DS = bass.DynSlice
    bz = bases or {}
    w0 = bz.get("win", 0)
    d0 = bz.get("dep", 0)
    ini0 = bz.get("ini", 0)
    e0 = {e: bz.get(e, 0) for e in ("pe", "v", "a", "pl")}
    sc.base = dict(e0)

    def winslice(t, g, w, pg):
        hs, b = hs_of[g], b_of[g]
        return B[t][512 * hs + 128 * pg:512 * hs + 128 * pg + 128,
                    b * 2048 + 128 * w:b * 2048 + 128 * w + 128].rearrange(
            "p (s t) -> p s t", t=16)

    def winbuf(t, g, w, pg):
        return a[t][:, g * 1024 + (w % 2) * 512:g * 1024 + (w % 2) * 512 + 512].rearrange(
            "p (s pg t) -> p s pg t", pg=4, t=16)[:, :, pg, :]

    init_dmas = []
    for g in range(ng):
        hs = hs_of[g]
        for dst, key in [("w1f", "w1duo"), ("w1b", "w1duo"), ("w2tf", "w2t"), ("w2tb", "w2t"),
                         ("w2cf", "w2cm"), ("w2cb", "w2cm")]:
            init_dmas.append((a[dst][:, g * 1024:(g + 1) * 1024],
                              B[key][4 * hs:4 * hs + 4].rearrange("a p c -> p a c")))
        init_dmas.append((a["b1rep"][:, g * 256:(g + 1) * 256], B["b1rep"][hs, :, :]))
        init_dmas.append((a["b2rep"][:, g * 128:(g + 1) * 128], B["b2rep_bdt"][hs, :, :]))
    if half == 0:
        for dst, key in [("lnw", "lnw_bdt"), ("lnb", "lnb_bdt")]:
            init_dmas.append((a[dst][:, 0:128], B[key][0, :, :]))
            init_dmas.append((a[dst][:, 128:256], B[key][1, :, :]))
        init_dmas.append((a["bd01"][:, :], B["bd01"][:, :]))
        init_dmas.append((a["triltnT"][:, :], B["trilT01"][:, :]))
        init_dmas.append((a["ident"][:, :], B["identc"][:, :]))
        init_dmas.append((a["tril01b"][:, :], B["tril01b"][:, :]))
        init_dmas.append((a["ident4"][:, :], B["ident4"][:, :]))
        init_dmas.append((a["t15n"][:, 0:1], B["lti15_rep"][:, :]))
        init_dmas.append((a["cols"][:, 0:1], B["lti_rep"][:, :]))
        init_dmas.append((a["cols"][:, 1:2], B["invi"][:, :]))
        init_dmas.append((a["onest15"][:, :], B["ones01"][:, :]))
        init_dmas.append((a["sctab"][:, :], B["sctab"][:, :]))
        if lrq_dma:
            init_dmas.append((a["lrq"][:, :], B["lrq_in"][:, :]))
    NINI = len(init_dmas)

    def emit_steps(h, eng):
        a["_sc_w0"] = w0
        a["_sc_d0"] = d0
        a["_memo"] = {}
        sc.base = dict(e0)
        if fori:
            for i in range(2):
                sc.emit_engine(h, eng, a, i)
            sc.alloc_regs(h, eng)
            sc.reg_init(h, eng, 2)
            with fori_shared(h, 2, nsteps) as i:
                prep_vals(h, a, i)
                sc.emit_engine(h, eng, a, i)
                sc.reg_step(h, eng)
        else:
            for i in range(nsteps):
                sc.emit_engine(h, eng, a, i)

    def gps(g_):
        if pre_gps is not None:
            pre_gps(g_)
        if half == 1:
            for e in ("pe", "v", "a", "pl"):
                g_.wait_ge(a["s_" + e], e0[e])
        for k, (dst, srcin) in enumerate(init_dmas):
            if ini0 + k > 0:
                g_.wait_ge(a["s_ini"], ini0 + 16 * k)
            g_.dma_start(out=dst, in_=srcin).then_inc(a["s_ini"], 16)
        g_.wait_ge(a["s_ini"], ini0 + 16 * NINI)
        g_.wait_ge(a["s_iv"], 4)
        emit_steps(g_, "pl")

    def sync(q):
        if pre_sync is not None:
            pre_sync(q)
        q.sem_inc(a["s_dep"], 16 * 4 * ng)
        dpre = d0 + 16 * 4 * ng
        ndep = [0]

        def load_win(w):
            if w >= NWIN:
                return
            if w >= 2:
                q.wait_ge(a["s_pl"], e0["pl"] + S["pl"] * (8 * (w - 1)))
                q.wait_ge(a["s_v"], e0["v"] + S["v"] * (8 * (w - 1)))
            elif half == 1:
                q.wait_ge(a["s_pl"], e0["pl"])
                q.wait_ge(a["s_v"], e0["v"])
            for g in range(ng):
                for j, (t, srckey) in enumerate([("xqw", "xqfm"), ("xkw", "xkfm"), ("xvw", "xvfm")]):
                    for pg in range(4):
                        cnt = w0 + 16 * ((w * NW + g * 3 + j) * 4 + pg)
                        if cnt > 0:
                            q.wait_ge(a["s_win"], cnt)
                        q.dma_start(out=winbuf(t, g, w, pg), in_=winslice(srckey, g, w, pg)
                                    ).then_inc(a["s_win"], 16)

        load_win(0)
        load_win(1)
        for w in range(NWIN):
            q.wait_ge(a["s_v"], e0["v"] + S["v"] * min(8 * w + 8, nsteps))
            for g in range(ng):
                for pg in range(4):
                    if dpre + 16 * ndep[0] > 0:
                        q.wait_ge(a["s_dep"], dpre + 16 * ndep[0])
                    q.dma_start(out=winslice("xqwfm", g, w, pg), in_=winbuf("dep", g, w, pg)
                                ).then_inc(a["s_dep"], 16)
                    ndep[0] += 1
            load_win(w + 2)
        q.wait_ge(a["s_win"], w0 + 16 * NW * NWIN * 4)
        q.wait_ge(a["s_pe"], e0["pe"] + S["pe"] * nsteps)
        q.wait_ge(a["s_v"], e0["v"] + S["v"] * nsteps)
        q.wait_ge(a["s_a"], e0["a"] + S["a"] * nsteps)
        q.wait_ge(a["s_pl"], e0["pl"] + S["pl"] * nsteps)
        q.wait_ge(a["s_dep"], dpre + 16 * 4 * ng * NWIN)

    def pe(t):
        if half == 0:
            t.wait_ge(a["s_iv"], 2)
            t.matmul(a["pzb1"][:, 0:128], a["tril01b"][:, :], a["gz2bdt"][:, 0:128], start=True,
                     stop=True).then_inc(a["s_ipe"], 1)
        t.wait_ge(a["s_iv"], 4)
        emit_steps(t, "pe")

    def vec(v):
        if half == 1:
            emit_steps(v, "v")
            return
        v.wait_ge(a["s_ini"], ini0 + 16 * NINI)
        v.tensor_tensor(a["cols"][:, 0:1], a["cols"][:, 0:1], a["cols"][:, 1:2], OP.add)
        v.drain()
        v.tensor_scalar(a["cols"][:, 0:1], a["cols"][:, 0:1], 0.0, None, OP.max)
        v.drain()
        v.tensor_scalar(a["cols"][:, 0:1], a["cols"][:, 0:1], -1.0, None, OP.mult).then_inc(a["s_iv"], 1)
        v.drain()
        v.tensor_scalar(a["gz2bdt"][:, 0:128], a["ident"][:, :], a["cols"][:, 0:1], None, OP.mult).then_inc(a["s_iv"], 1)
        v.tensor_scalar(a["t15n"][:, 0:1], a["t15n"][:, 0:1], 1.0 / 16, None, OP.add)
        v.drain()
        v.tensor_scalar(a["t15n"][:, 0:1], a["t15n"][:, 0:1], 0.0, None, OP.max)
        v.drain()
        v.tensor_scalar(a["t15n"][:, 0:1], a["t15n"][:, 0:1], -1.0, None, OP.mult)
        v.drain()
        v.tensor_scalar(a["onest15"][:, :], a["onest15"][:, :], a["t15n"][:, 0:1], None, OP.mult)
        v.drain()
        v.memset(a["epsc"][:, 0:1], EPS)
        v.memset(a["pab"][:, :], 0.0)
        v.drain()
        v.memset(a["dep"][:, :], 0.0)
        v.drain()
        v.memset(a["bdxk"][:, :], 0.0)
        v.memset(a["bdxq"][:, :], 0.0)
        v.memset(a["bdtgt"][:, :], 0.0)
        v.memset(a["gz1bd"][:, :], 0.0)
        v.memset(a["x2sbd"][:, :], 0.0)
        v.memset(a["gz2sbd"][:, :], 0.0).then_inc(a["s_iv"], 1)
        v.wait_ge(a["s_ipe"], 1)
        v.tensor_copy(a["triltnT"][:, :], a["pzb1"][:, 0:128]).then_inc(a["s_iv"], 1)
        emit_steps(v, "v")

    def act(s):
        s.wait_ge(a["s_iv"], 4)
        emit_steps(s, "a")

    return {"sync": sync, "gpsimd": gps, "tensor": pe, "vector": vec, "scalar": act,
            "nini": NINI}




DS = bass.DynSlice
EPS5 = 1e-5
EPS6 = 1e-6
AX = mybir.AxisListType.X


def f32v(h):
    return h.bitcast(F32)


def bf16v(h):
    return h.bitcast(BF16)


class SerialQ:
    def __init__(self, q, a, w0, d0):
        self.q, self.a = q, a
        self.w, self.d = w0, d0

    def load(self, out, in_, transpose=False):
        if self.w > 0:
            self.q.wait_ge(self.a["s_win"], self.w)
        if transpose:
            self.q.dma_start(out=out, in_=in_, transpose=True).then_inc(self.a["s_win"], 16)
        else:
            self.q.dma_start(out=out, in_=in_).then_inc(self.a["s_win"], 16)
        self.w += 16

    def store(self, out, in_):
        if self.d > 0:
            self.q.wait_ge(self.a["s_dep"], self.d)
        self.q.dma_start(out=out, in_=in_).then_inc(self.a["s_dep"], 16)
        self.d += 16


def rwait(h, i, sem, key, coef, const):
    if isinstance(i, int):
        v = coef * i + const
        if v > 0:
            h.wait_ge(sem, v)
    else:
        rs = get_regs(h)
        h.reg_add(rs["xs"], rs[key], const)
        h.wait_ge(sem, rs["xs"])


def p34_carve(a):
    m = {}
    w1fb = bf16v(a["w1f"])
    w2tfb = bf16v(a["w2tf"])
    w2cfb = bf16v(a["w2cf"])
    m["wts"] = f32v(a["b1rep"])[:, 0:512]
    m["xqw_tm"] = [a["xqw"], a["xkw"]]
    m["g_tm"] = [a["xvw"], a["dep"]]
    m["lnG"] = [a["w1b"], a["w2tb"]]
    m["infm"] = [a["w2cb"], w1fb[:, 0:4096]]
    m["xnfm_st"] = [w1fb[:, 4096:8192], w2cfb[:, 0:4096]]
    m["woT"] = w2tfb
    m["x_tm"] = [f32v(a["gz1bd"]), f32v(a["x2sbd"])]
    m["x1"] = [a["w2cf"][:, 2048:3072], a["w2cf"][:, 3072:4096]]
    m["xn_tm"] = [a["z1"], a["u"]]
    m["postw"] = [f32v(a["wv"]), f32v(a["sg"])]
    m["postb"] = [f32v(a["x2"]), f32v(a["x2s"])]
    m["lng"] = [f32v(a["ga"]), f32v(a["gb"])]
    m["lnb"] = [f32v(a["gz1s"]), f32v(a["x2t"])]
    m["sqs"] = [f32v(a["z2c"]), f32v(a["xh"])]
    m["sqs2"] = [f32v(a["gx"]), f32v(a["gxs"])]
    m["cols"] = f32v(a["z1b"])
    m["gatew"] = bf16v(a["scur"])[:, 0:48]
    m["gateb"] = f32v(a["scur"])[:, 32:38]
    m["gt"] = f32v(a["u2"])
    m["e_xn"] = [a["xqw"], a["xkw"]]
    m["ew1"] = [w1fb, w2tfb]
    m["ew2"] = [w2cfb, bf16v(a["w1b"]), bf16v(a["w2tb"])]
    m["h1"] = [a["dep"], a["w2cb"]]
    m["moeo"] = f32v(a["xvw"])
    m["escr"] = [f32v(a["z2c"]), f32v(a["xh"])]
    m["eb2"] = f32v(a["gz1bd"])
    m["eb1"] = f32v(a["scur"])[:, 64:80]
    m["Rs"] = w1fb
    m["abel5"] = [a["w2cf"][:, 0:1024], a["w2cf"][:, 1024:2048]]
    m["moe5"] = [f32v(a["xvw"])[:, 0:1024], f32v(a["xvw"])[:, 1024:2048]]
    m["x15"] = [f32v(a["gz1bd"]), f32v(a["x2sbd"])]
    m["moebf"] = [a["wv"], a["sg"]]
    m["moefm"] = [a["x2"], a["x2s"]]
    m["gcw1"] = [f32v(a["ga"]), f32v(a["gb"])]
    m["gcw2"] = [f32v(a["gz1s"]), f32v(a["x2t"])]
    m["y"] = [a["w2tf"][:, 0:1024], a["w2tf"][:, 1024:2048]]
    m["scr5"] = [a["w2tf"][:, 2048:3072], a["w2tf"][:, 3072:4096]]
    m["sq5"] = [f32v(a["z2bc"]), f32v(a["xh2"])]
    m["eLH"] = [w1fb, w2tfb]
    m["eRH"] = [[w2cfb], [a["w2cb"], a["dep"]]]
    m["eo"] = [a["z1"], a["u"]]
    m["easm"] = [[a["wv"], a["sg"], a["x2"], a["x2s"]],
                 [a["ga"], a["gb"], a["gz1s"], a["x2t"]]]
    return m


def rhtile(m, slot, kb):
    if slot == 0:
        return m["eRH"][0][0][:, kb * 1024:(kb + 1) * 1024]
    h = m["eRH"][1][kb // 4]
    return h[:, (kb % 4) * 1024:(kb % 4 + 1) * 1024]


def build_p3(a, m, psA, psB, psG, ptT, nc):
    sc = Sched()

    def wwin3(h, i, k, sg, st):
        rwait(h, i, a["s_win"], "xw", 384,
              a["_p3w0"] + 160 + 48 * (4 * sg + st) + 16 * k)

    def col(sg, st, k):
        o = (sg * 4 + st) * 32 + k
        return m["cols"][:, o:o + 1]

    def uc(base, sg, st, k=1):
        u = sg * 4 + st - k
        if u >= 0:
            return (base + str(u // 4) + str(u % 4), 0)
        return (base + str((u + 8) // 4) + str((u + 8) % 4), -1)

    for sg in range(2):
        for st in range(4):
            sfx = str(sg) + str(st)
            xqw = m["xqw_tm"][sg][:, st * 1024:(st + 1) * 1024]
            gtm = m["g_tm"][sg][:, st * 1024:(st + 1) * 1024]
            lnG = m["lnG"][sg][:, st * 1024:(st + 1) * 1024]
            x1 = m["x1"][sg]
            xtm = m["x_tm"][sg][:, 0:1024]
            xn = m["xn_tm"][sg][:, 0:1024]

            def f_sqa(h, i, xqw=xqw, sg=sg, st=st):
                wwin3(h, i, 1, sg, st)
                return h.tensor_tensor(m["sqs"][0][:, 0:512], xqw[:, 0:512], xqw[:, 0:512], OP.mult)
            sc.add("pl", "p3sqa" + sfx, f_sqa, deps=[uc("p3r2a", sg, st)])
            def f_sqb(h, i, xqw=xqw):
                return h.tensor_tensor(m["sqs"][1][:, 0:512], xqw[:, 512:1024], xqw[:, 512:1024], OP.mult)
            sc.add("pl", "p3sqb" + sfx, f_sqb, deps=[uc("p3r2b", sg, st)])
            def f_r1(h, i, xqw=xqw, sg=sg, st=st):
                wwin3(h, i, 1, sg, st)
                return h.reduce_sum(col(sg, st, 0), xqw, axis=AX)
            sc.add("v", "p3r1" + sfx, f_r1)
            sc.add("v", "p3r2a" + sfx, (lambda h, i, sg=sg, st=st: h.reduce_sum(
                col(sg, st, 1), m["sqs"][0][:, 0:512], axis=AX)), deps=[("p3sqa" + sfx, 0)])
            sc.add("v", "p3r2b" + sfx, (lambda h, i, sg=sg, st=st: h.reduce_sum(
                col(sg, st, 2), m["sqs"][1][:, 0:512], axis=AX)), deps=[("p3sqb" + sfx, 0)])
            sc.add("v", "p3mu" + sfx, (lambda h, i, sg=sg, st=st: h.tensor_scalar(
                col(sg, st, 0), col(sg, st, 0), 1.0 / 1024, None, OP.mult)))
            sc.add("v", "p3ad" + sfx, (lambda h, i, sg=sg, st=st: h.tensor_tensor(
                col(sg, st, 1), col(sg, st, 1), col(sg, st, 2), OP.add)))
            sc.add("v", "p3m2" + sfx, (lambda h, i, sg=sg, st=st: h.tensor_scalar(
                col(sg, st, 1), col(sg, st, 1), 1.0 / 1024, None, OP.mult)))
            sc.add("v", "p3nv" + sfx, (lambda h, i, sg=sg, st=st: h.scalar_tensor_tensor(
                col(sg, st, 2), col(sg, st, 0), col(sg, st, 0), col(sg, st, 1), OP.mult, OP.subtract)))
            sc.add("a", "p3ln" + sfx, (lambda h, i, sg=sg, st=st: h.activation(
                col(sg, st, 3), col(sg, st, 2), AF.Ln, scale=-1.0, bias=m["cols"][:, 440:441])),
                deps=[("p3nv" + sfx, 0)])
            sc.add("a", "p3rs" + sfx, (lambda h, i, sg=sg, st=st: h.activation(
                col(sg, st, 4), col(sg, st, 3), AF.Exp, scale=-0.5)))
            sc.add("v", "p3ap" + sfx, (lambda h, i, xqw=xqw, lnG=lnG, sg=sg, st=st: h.tensor_scalar(
                lnG, xqw, col(sg, st, 0), col(sg, st, 4), OP.subtract, OP.mult)),
                deps=[("p3rs" + sfx, 0), ("p3t" + sfx, -1)])
            for hh in range(2):
                def f_pw(h, i, lnG=lnG, hh=hh):
                    d = lnG[:, hh * 512:(hh + 1) * 512]
                    return h.tensor_tensor(d, d, m["postw"][hh][:, 0:512], OP.mult)
                sc.add("pl", "p3pw" + str(hh) + sfx, f_pw, deps=[("p3ap" + sfx, 0)])
                def f_pb(h, i, lnG=lnG, hh=hh):
                    d = lnG[:, hh * 512:(hh + 1) * 512]
                    return h.tensor_tensor(d, d, m["postb"][hh][:, 0:512], OP.add)
                sc.add("pl", "p3pb" + str(hh) + sfx, f_pb)
            def f_gg(h, i, lnG=lnG, gtm=gtm, sg=sg, st=st):
                wwin3(h, i, 2, sg, st)
                return h.tensor_tensor(lnG, lnG, gtm, OP.mult)
            sc.add("v", "p3gg" + sfx, f_gg, deps=[("p3pb1" + sfx, 0)])
            def f_t(h, i, lnG=lnG, sg=sg):
                r = None
                for c in range(8):
                    r = h.transpose(ptT[:, sg * 1024 + c * 128:sg * 1024 + c * 128 + 128],
                                    lnG[:, c * 128:c * 128 + 128], a["ident"][:, :])
                return r
            sc.add("pe", "p3t" + sfx, f_t, deps=[("p3gg" + sfx, 0), uc("p3ex", sg, st)])
            def f_et(h, i, sg=sg, st=st):
                dst = m["infm"][sg].rearrange("p (c s) -> p c s", s=512)[:, :, st * 128:st * 128 + 128]
                src = ptT[:, sg * 1024:(sg + 1) * 1024].rearrange("p (c t) -> p c t", t=128)
                return h.activation(dst, src, AF.Identity)
            sc.add("a", "p3et" + sfx, f_et, deps=[("p3t" + sfx, 0), ("p3wo1" + sfx, -1)])
            for hh in range(2):
                def f_wo(h, i, sg=sg, st=st, hh=hh):
                    h.wait_ge(a["s_ia"], a["_p3ia"])
                    ps = (psA if sg == 0 else psB)[hh]
                    r = None
                    for c in range(8):
                        r = h.matmul(ps, m["infm"][sg][:, c * 512 + st * 128:c * 512 + st * 128 + 128],
                                     m["woT"][:, c * 1024 + hh * 512:c * 1024 + hh * 512 + 512],
                                     start=(c == 0), stop=(c == 7))
                    return r
                sc.add("pe", "p3wo" + str(hh) + sfx, f_wo,
                       deps=[("p3et" + sfx, 0), uc("p3res" + str(hh), sg, st)])
            for hh in range(2):
                def f_res(h, i, sg=sg, st=st, hh=hh, x1=x1, xtm=xtm):
                    rwait(h, i, a["s_win"], "xw", 384,
                          a["_p3w0"] + 160 + 48 * (4 * sg + st) + 48)
                    if hh == 0 and (not isinstance(i, int) or 16 * (5 * (i * 2 + sg) + st) > 0):
                        rwait(h, i, a["s_dep"], "xd", 160,
                              a["_p3d0"] + 16 * (5 * sg + st))
                    ps = (psA if sg == 0 else psB)[hh]
                    return h.tensor_tensor(x1[:, hh * 512:(hh + 1) * 512], ps,
                                           xtm[:, hh * 512:(hh + 1) * 512], OP.add)
                sc.add("v", "p3res" + str(hh) + sfx, f_res, deps=[("p3wo" + str(hh) + sfx, 0)])
            for hh in range(2):
                def f_sq2(h, i, hh=hh, x1=x1):
                    return h.tensor_tensor(m["sqs2"][hh][:, 0:512], x1[:, hh * 512:(hh + 1) * 512],
                                           x1[:, hh * 512:(hh + 1) * 512], OP.mult)
                sc.add("pl", "p3sq2" + str(hh) + sfx, f_sq2,
                       deps=[("p3res" + str(hh) + sfx, 0), uc("p3r2" + str(hh) + "c", sg, st)])
            sc.add("v", "p3r1b" + sfx, (lambda h, i, x1=x1, sg=sg, st=st: h.reduce_sum(
                col(sg, st, 5), x1, axis=AX)), deps=[("p3res1" + sfx, 0)])
            for hh in range(2):
                sc.add("v", "p3r2" + str(hh) + "c" + sfx, (lambda h, i, sg=sg, st=st, hh=hh: h.reduce_sum(
                    col(sg, st, 6 + hh), m["sqs2"][hh][:, 0:512], axis=AX)),
                    deps=[("p3sq2" + str(hh) + sfx, 0)])
            sc.add("v", "p3mu2" + sfx, (lambda h, i, sg=sg, st=st: h.tensor_scalar(
                col(sg, st, 5), col(sg, st, 5), 1.0 / 1024, None, OP.mult)))
            sc.add("v", "p3ad2" + sfx, (lambda h, i, sg=sg, st=st: h.tensor_tensor(
                col(sg, st, 6), col(sg, st, 6), col(sg, st, 7), OP.add)))
            sc.add("v", "p3m22" + sfx, (lambda h, i, sg=sg, st=st: h.tensor_scalar(
                col(sg, st, 6), col(sg, st, 6), 1.0 / 1024, None, OP.mult)))
            sc.add("v", "p3nv2" + sfx, (lambda h, i, sg=sg, st=st: h.scalar_tensor_tensor(
                col(sg, st, 7), col(sg, st, 5), col(sg, st, 5), col(sg, st, 6), OP.mult, OP.subtract)))
            sc.add("a", "p3l2" + sfx, (lambda h, i, sg=sg, st=st: h.activation(
                col(sg, st, 8), col(sg, st, 7), AF.Ln, scale=-1.0, bias=m["cols"][:, 441:442])),
                deps=[("p3nv2" + sfx, 0)])
            sc.add("a", "p3rs2" + sfx, (lambda h, i, sg=sg, st=st: h.activation(
                col(sg, st, 9), col(sg, st, 8), AF.Exp, scale=-0.5)))
            sc.add("a", "p3iv" + sfx, (lambda h, i, sg=sg, st=st: h.activation(
                col(sg, st, 10), col(sg, st, 8), AF.Exp, scale=-1.0)))
            sc.add("v", "p3t2" + sfx, (lambda h, i, sg=sg, st=st: h.scalar_tensor_tensor(
                col(sg, st, 11), col(sg, st, 7), -1.0, col(sg, st, 10), OP.mult, OP.mult)),
                deps=[("p3iv" + sfx, 0)])
            sc.add("v", "p3t3" + sfx, (lambda h, i, sg=sg, st=st: h.tensor_scalar(
                col(sg, st, 11), col(sg, st, 11), EPS5, None, OP.add)))
            sc.add("a", "p3l3" + sfx, (lambda h, i, sg=sg, st=st: h.activation(
                col(sg, st, 12), col(sg, st, 11), AF.Ln, scale=1.0)), deps=[("p3t3" + sfx, 0)])
            sc.add("a", "p3r3" + sfx, (lambda h, i, sg=sg, st=st: h.activation(
                col(sg, st, 13), col(sg, st, 12), AF.Exp, scale=-0.5)))
            sc.add("v", "p3x1n" + sfx, (lambda h, i, sg=sg, st=st, x1=x1: h.tensor_scalar(
                x1, x1, col(sg, st, 5), col(sg, st, 9), OP.subtract, OP.mult)),
                deps=[("p3rs2" + sfx, 0)])
            sc.add("v", "p3xn" + sfx, (lambda h, i, sg=sg, st=st, x1=x1, xn=xn: h.tensor_scalar(
                xn, x1, col(sg, st, 13), None, OP.mult)),
                deps=[("p3r3" + sfx, 0), ("p3x1n" + sfx, 0), uc("p3tx", sg, st)])
            for hh in range(2):
                def f_lg(h, i, xn=xn, hh=hh):
                    d = xn[:, hh * 512:(hh + 1) * 512]
                    return h.tensor_tensor(d, d, m["lng"][hh][:, 0:512], OP.mult)
                sc.add("pl", "p3lg" + str(hh) + sfx, f_lg, deps=[("p3xn" + sfx, 0)])
                def f_lb(h, i, xn=xn, hh=hh):
                    d = xn[:, hh * 512:(hh + 1) * 512]
                    return h.tensor_tensor(d, d, m["lnb"][hh][:, 0:512], OP.add)
                sc.add("pl", "p3lb" + str(hh) + sfx, f_lb)
            def f_tx(h, i, xn=xn, sg=sg):
                r = None
                for c in range(8):
                    r = h.transpose(ptT[:, sg * 1024 + c * 128:sg * 1024 + c * 128 + 128],
                                    xn[:, c * 128:c * 128 + 128], a["ident"][:, :])
                return r
            sc.add("pe", "p3tx" + sfx, f_tx,
                   deps=[("p3lb1" + sfx, 0), ("p3et" + sfx, 0), ("p3ex" + sfx, -1)])
            def f_ex(h, i, sg=sg, st=st):
                rwait(h, i, a["s_dep"], "xd", 160, a["_p3d0"] + 80 * sg - 80)
                dst = m["xnfm_st"][sg].rearrange("p (c s) -> p c s", s=512)[:, :, st * 128:st * 128 + 128]
                src = ptT[:, sg * 1024:(sg + 1) * 1024].rearrange("p (c t) -> p c t", t=128)
                return h.activation(dst, src, AF.Identity)
            sc.add("a", "p3ex" + sfx, f_ex, deps=[("p3tx" + sfx, 0), ("p3ga" + sfx, -1)])
            def f_ga(h, i, sg=sg, st=st):
                h.wait_ge(a["s_ia"], a["_p3ia"])
                r = None
                for c in range(8):
                    r = h.matmul(psG[sg], m["xnfm_st"][sg][:, c * 512 + st * 128:c * 512 + st * 128 + 128],
                                 m["gatew"][:, c * 6:c * 6 + 6], start=(c == 0), stop=(c == 7))
                return r
            sc.add("pe", "p3ga" + sfx, f_ga, deps=[("p3ex" + sfx, 0), uc("p3g1", sg, st, 4)])
            gt = m["gt"][:, (sg * 4 + st) * 32:(sg * 4 + st) * 32 + 30]
            sc.add("v", "p3g1" + sfx, (lambda h, i, gt=gt, sg=sg: h.tensor_tensor(
                gt[:, 0:6], psG[sg], m["gateb"], OP.add)), deps=[("p3ga" + sfx, 0)])
            sc.add("v", "p3g2" + sfx, (lambda h, i, gt=gt, sg=sg, st=st: h.reduce_max(
                col(sg, st, 14), gt[:, 0:6], axis=AX)))
            sc.add("v", "p3g3" + sfx, (lambda h, i, gt=gt, sg=sg, st=st: h.tensor_scalar(
                gt[:, 8:14], gt[:, 0:6], col(sg, st, 14), None, OP.is_equal)))
            sc.add("v", "p3g4" + sfx, (lambda h, i, gt=gt: h.scalar_tensor_tensor(
                gt[:, 24:30], gt[:, 8:14], -1e30, gt[:, 0:6], OP.mult, OP.add)))
            sc.add("v", "p3g5" + sfx, (lambda h, i, gt=gt, sg=sg, st=st: h.reduce_max(
                col(sg, st, 15), gt[:, 24:30], axis=AX)))
            sc.add("v", "p3g6" + sfx, (lambda h, i, gt=gt, sg=sg, st=st: h.tensor_scalar(
                gt[:, 16:22], gt[:, 0:6], col(sg, st, 15), None, OP.is_equal)))
            sc.add("v", "p3g7" + sfx, (lambda h, i, sg=sg, st=st: h.tensor_tensor(
                col(sg, st, 16), col(sg, st, 15), col(sg, st, 14), OP.subtract)))
            sc.add("a", "p3g8" + sfx, (lambda h, i, sg=sg, st=st: h.activation(
                col(sg, st, 16), col(sg, st, 16), AF.Exp, scale=1.0)), deps=[("p3g7" + sfx, 0)])
            sc.add("v", "p3g9" + sfx, (lambda h, i, sg=sg, st=st: h.tensor_scalar(
                col(sg, st, 17), col(sg, st, 16), 1.0, None, OP.add)), deps=[("p3g8" + sfx, 0)])
            sc.add("v", "p3g10" + sfx, (lambda h, i, sg=sg, st=st: h.reciprocal(
                col(sg, st, 17), col(sg, st, 17))))
            sc.add("v", "p3g11" + sfx, (lambda h, i, sg=sg, st=st: h.tensor_tensor(
                col(sg, st, 16), col(sg, st, 16), col(sg, st, 17), OP.mult)))
            sc.add("v", "p3g12" + sfx, (lambda h, i, gt=gt, sg=sg, st=st: h.tensor_scalar(
                gt[:, 8:14], gt[:, 8:14], col(sg, st, 17), None, OP.mult)),
                deps=[("p3g11" + sfx, 0)])
            sc.add("v", "p3g13" + sfx, (lambda h, i, gt=gt, sg=sg, st=st: h.scalar_tensor_tensor(
                gt[:, 8:14], gt[:, 16:22], col(sg, st, 16), gt[:, 8:14], OP.mult, OP.add)))
    def f_wtall(h, i):
        d = m["wts"].rearrange("p (e j d) -> p e j d", e=8, j=8)[:, 2:8, DS(i, 1), :]
        src = m["gt"].rearrange("p (u o c) -> p u o c", o=1, c=32)[:, 0:8, :, 8:14]
        return h.tensor_copy(d, src.rearrange("p u o e -> p e o u"))
    sc.add("pl", "p3wtall0", f_wtall,
           deps=[("p3g13" + str(sg_) + str(st_), 0) for sg_ in range(2) for st_ in range(4)])
    sc.finalize()
    return sc


def p3_sync(q, a, B, m, sc, core_rows=8192):
    Q = SerialQ(q, a, a["_p3w0"], a["_p3d0"])
    for hh in range(2):
        Q.load(m["postw"][hh][:, 0:512], B["postwrep"][:, hh * 512:(hh + 1) * 512])
        Q.load(m["postb"][hh][:, 0:512], B["postbrep"][:, hh * 512:(hh + 1) * 512])
        Q.load(m["lng"][hh][:, 0:512], B["lngrep"][:, hh * 512:(hh + 1) * 512])
        Q.load(m["lnb"][hh][:, 0:512], B["lnbrep"][:, hh * 512:(hh + 1) * 512])
    Q.load(m["gateb"], B["gatebrep"][:, :])
    Q.load(m["cols"][:, 440:442], B["epsc2"][:, :])
    bv = sc.base.get("v", 0)
    ba = sc.base.get("a", 0)
    Sv, Sa = sc.S["v"], sc.S["a"]

    def vdone(name, i):
        eng, k = sc.idx[name]
        return (bv if eng == "v" else ba) + i * (Sv if eng == "v" else Sa) + k

    for s in range(16):
        i, sg = s // 2, s % 2

        def x1store(st, s=s, sg=sg, i=i):
            q.wait_ge(a["s_v"], vdone("p3x1n" + str(sg) + str(st), i))
            Q.store(B["x1h"][(s * 4 + st) * 128:(s * 4 + st + 1) * 128, :], m["x1"][sg][:, :])
        for st in range(4):
            u = s * 4 + st
            if u >= 8:
                q.wait_ge(a["s_v"], vdone("p3ap" + str(sg) + str(st), i - 1))
            Q.load(m["xqw_tm"][sg][:, st * 1024:(st + 1) * 1024],
                   B["xqwfm"][:, u * 128:(u + 1) * 128], transpose=True)
            if u >= 8:
                q.wait_ge(a["s_v"], vdone("p3gg" + str(sg) + str(st), i - 1))
            Q.load(m["g_tm"][sg][:, st * 1024:(st + 1) * 1024],
                   B["gfm"][:, u * 128:(u + 1) * 128], transpose=True)
            if st >= 1:
                q.wait_ge(a["s_v"], vdone("p3res1" + str(sg) + str(st - 1), i))
            elif s >= 2:
                q.wait_ge(a["s_v"], vdone("p3res1" + str(sg) + "3", i - 1))
            Q.load(m["x_tm"][sg][:, :], B["x"][u * 128:(u + 1) * 128, :])
            if st >= 1:
                x1store(st - 1)
        x1store(3)
        q.wait_ge(a["s_a"], vdone("p3ex" + str(sg) + "3", i))
        Q.store(B["xnfmh"][:, s * 512:(s + 1) * 512].rearrange("(c p) s2 -> p c s2", p=128),
                m["xnfm_st"][sg].rearrange("p (c s2) -> p c s2", s2=512))
    a["_p3w_end"] = Q.w
    a["_p3d_end"] = Q.d


def p3_weights(pl, a, B, m, ia0):
    n = ia0
    for c in range(8):
        if n > 0:
            pl.wait_ge(a["s_ia"], n)
        pl.dma_start(out=m["woT"][:, c * 1024:(c + 1) * 1024],
                     in_=B["woT"][c * 128:(c + 1) * 128, :]).then_inc(a["s_ia"], 16)
        n += 16
    pl.wait_ge(a["s_ia"], n)
    gw = B["gatew"].rearrange("(c p) e -> p c e", p=128)
    pl.dma_start(out=m["gatew"].rearrange("p (c e) -> p c e", e=6), in_=gw).then_inc(a["s_ia"], 16)


def build_p4(a, m, ps1, ps2, nc):
    sc = Sched()

    def f_wc(h, i, g=0):
        src = m["wts"].rearrange("p (s d) -> p s d", d=8)[:, DS(i, 1), :]
        return h.tensor_copy(a["lrqc4"][:, 0:8].rearrange("p (o d) -> p o d", o=1), src)
    sc.add("v", "p4wc0", f_wc)
    for sg in range(2):
        sfx = str(sg)
        xnt = m["e_xn"][sg]
        for hbk in range(16):
            hc, hb = hbk // 8, hbk % 8
            def f_m1(h, i, sg=sg, hc=hc, hb=hb, hbk=hbk, xnt=xnt):
                if hbk == 0:
                    wdiv8(h, i, 112, a["_p4ia0"] + 112, a["s_ia"])
                    rwait(h, i, a["s_win"], "xw", 160, a["_p4w0"] + 16 * (5 * sg + 1))
                ps = ps1[hbk % 2]
                r = None
                for kb in range(8):
                    lh = m["ew1"][kb // 4][:, (kb % 4) * 2048 + hc * 1024 + hb * 128:
                                           (kb % 4) * 2048 + hc * 1024 + hb * 128 + 128]
                    r = h.matmul(ps, lh, xnt[:, kb * 512:kb * 512 + 512],
                                 start=(kb == 0), stop=(kb == 7))
                return r
            if hbk >= 2:
                d = [("p4h" + str(hbk - 2) + sfx, 0)]
            else:
                d = [("p4h" + str(14 + hbk) + str(1 - sg), 0 if sg == 1 else -1)]
            if hbk == 0:
                d.append(("p4m2_31" + str(1 - sg), 0 if sg == 1 else -1))
            sc.add("pe", "p4m1_" + str(hbk) + sfx, f_m1, deps=d)
            def f_h(h, i, hc=hc, hb=hb, hbk=hbk):
                return h.activation(m["h1"][hc][:, hb * 512:hb * 512 + 512], ps1[hbk % 2], AF.Gelu,
                                    bias=m["eb1"][:, hbk:hbk + 1], scale=1.0)
            sc.add("a", "p4h" + str(hbk) + sfx, f_h, deps=[("p4m1_" + str(hbk) + sfx, 0)])
        for st in range(4):
            for hh in range(2):
                def f_m2(h, i, st=st, hh=hh):
                    r = None
                    for hbk in range(16):
                        lh = m["h1"][hbk // 8][:, (hbk % 8) * 512 + st * 128:(hbk % 8) * 512 + st * 128 + 128]
                        if hbk < 8:
                            rt = m["ew2"][0][:, hbk * 1024 + hh * 512:hbk * 1024 + hh * 512 + 512]
                        else:
                            t = m["ew2"][1 + (hbk - 8) // 4]
                            rt = t[:, ((hbk - 8) % 4) * 1024 + hh * 512:((hbk - 8) % 4) * 1024 + hh * 512 + 512]
                        r = h.matmul(ps2[hh], lh, rt, start=(hbk == 0), stop=(hbk == 15))
                    return r
                if st >= 1:
                    war = ("p4mo" + str(st - 1) + str(hh) + sfx, 0)
                else:
                    war = ("p4mo3" + str(hh) + str(1 - sg), 0 if sg == 1 else -1)
                sc.add("pe", "p4m2_" + str(st) + str(hh) + sfx, f_m2,
                       deps=[("p4h15" + sfx, 0), war])
            for hh in range(2):
                def f_sc(h, i, sg=sg, st=st, hh=hh):
                    rwait(h, i, a["s_win"], "xw", 160, a["_p4w0"] + 16 * (5 * sg + 2 + st))
                    return h.tensor_tensor(m["escr"][hh][:, 0:512], ps2[hh],
                                           m["eb2"][:, hh * 512:(hh + 1) * 512], OP.add)
                sc.add("v", "p4sc" + str(st) + str(hh) + sfx, f_sc,
                       deps=[("p4m2_" + str(st) + str(hh) + sfx, 0), ("p4mo" + str(st) + str(hh) + sfx, -1)])
            for hh in range(2):
                def f_mo(h, i, sg=sg, st=st, hh=hh):
                    if hh == 0 and (not isinstance(i, int) or i * 8 + sg * 4 + st >= 1):
                        rwait(h, i, a["s_dep"], "xd", 128, a["_p4d0"] + 16 * (4 * sg + st))
                    mo = m["moeo"][:, sg * 1024 + hh * 512:sg * 1024 + hh * 512 + 512]
                    scr = m["escr"][hh][:, 0:512]
                    dlt = sg * 4 + st
                    return h.scalar_tensor_tensor(mo, scr, a["lrqc4"][:, dlt:dlt + 1], mo,
                                                  OP.mult, OP.add)
                sc.add("v", "p4mo" + str(st) + str(hh) + sfx, f_mo,
                       deps=[("p4sc" + str(st) + str(hh) + sfx, 0)])
    sc.finalize()
    return sc


def p4_sync(q, a, B, m, sc):
    Q = SerialQ(q, a, a["_p4w0"], a["_p4d0"])
    bv = sc.base["v"]
    bpe = sc.base["pe"]
    Sv, Spe = sc.S["v"], sc.S["pe"]
    for ep in range(8):
        tgt = B["abelh"] if ep < 2 else B["moeh"]
        src_old = B["abelh"] if ep < 2 else B["moeh"]
        for s in range(16):
            i = ep * 8 + s // 2
            sg = s % 2
            if ep * 16 + s >= 2:
                eng, k = sc.idx["p4m1_15" + str(sg)]
                q.wait_ge(a["s_pe"], bpe + (i - 1) * Spe + k)
            Q.load(m["e_xn"][sg].rearrange("p (c s2) -> p c s2", s2=512),
                   B["xnfmh"][:, s * 512:(s + 1) * 512].rearrange("(c p) s2 -> p c s2", p=128))
            for st in range(4):
                u = s * 4 + st
                gu = ep * 64 + u
                if gu >= 1:
                    q.wait_ge(a["s_dep"], a["_p4d0"] + 16 * gu)
                if ep >= 1:
                    q.wait_ge(a["s_dep"], a["_p4d0"] + 16 * ((ep - 1) * 64 + u + 1))
                Q.load(m["moeo"][:, sg * 1024:(sg + 1) * 1024],
                       src_old[u * 128:(u + 1) * 128, :])
                eng, k = sc.idx["p4mo" + str(st) + "1" + str(sg)]
                q.wait_ge(a["s_v"], bv + i * Sv + k)
                Q.store(tgt[(s * 4 + st) * 128:(s * 4 + st + 1) * 128, :],
                        m["moeo"][:, sg * 1024:(sg + 1) * 1024])
    a["_p4w_end"] = Q.w
    a["_p4d_end"] = Q.d


def p4_weights(g, a, B, m, ep, pe_done_val, v_done_val, ia0):
    g.wait_ge(a["s_pe"], pe_done_val)
    g.wait_ge(a["s_v"], v_done_val)
    n = ia0
    def d(out, in_):
        nonlocal n
        if n > 0:
            g.wait_ge(a["s_ia"], n)
        g.dma_start(out=out, in_=in_).then_inc(a["s_ia"], 16)
        n += 16
    d(m["ew1"][0].rearrange("p (k c) -> p k c", c=2048),
      B["ew1s"][ep, 0:512, :].rearrange("(k p) c -> p k c", p=128))
    d(m["ew1"][1].rearrange("p (k c) -> p k c", c=2048),
      B["ew1s"][ep, 512:1024, :].rearrange("(k p) c -> p k c", p=128))
    d(m["ew2"][0].rearrange("p (k c) -> p k c", c=1024),
      B["ew2s"][ep, 0:1024, :].rearrange("(k p) c -> p k c", p=128))
    d(m["ew2"][1].rearrange("p (k c) -> p k c", c=1024),
      B["ew2s"][ep, 1024:1536, :].rearrange("(k p) c -> p k c", p=128))
    d(m["ew2"][2].rearrange("p (k c) -> p k c", c=1024),
      B["ew2s"][ep, 1536:2048, :].rearrange("(k p) c -> p k c", p=128))
    d(m["eb1"][:, :], B["eb1c"][ep, :, :])
    d(m["eb2"][:, :], B["eb2rep"][ep, :, :])
    return n


EXPM_CHUNKS = [
    [1.0, 1.0, 0.5, 1.0 / 6],
    [1.0 / math.factorial(4), 1.0 / math.factorial(5), 1.0 / math.factorial(6), 1.0 / math.factorial(7)],
    [1.0 / math.factorial(8), 1.0 / math.factorial(9), 1.0 / math.factorial(10), 1.0 / math.factorial(11)],
    [1.0 / math.factorial(12), 1.0 / math.factorial(13)],
]


def expm_tasks():
    c = EXPM_CHUNKS
    tasks = []
    for ei in range(4):
        X, XT = ("geh", ei), ("gehT", ei)
        tasks.append(("mm", XT, X, ("A2h", -1), None, ei))
        tasks.append(("mm", X, XT, ("A2Th", -1), None, ei))
        tasks.append(("mm", XT, ("A2h", -1), ("A3h", -1), None, ei))
        tasks.append(("mm", ("A2h", -1), ("A2Th", -1), ("A4Th", -1), None, ei))
        tasks.append(("pinit", X, None, ("ph", -1), None, ei))
        tasks.append(("mm", ("A4Th", -1), ("ph", -1), ("qh", -1), c[2], ei))
        tasks.append(("mm", ("A4Th", -1), ("qh", -1), ("ph", -1), c[1], ei))
        tasks.append(("mm", ("A4Th", -1), ("ph", -1), ("Eh", ei), c[0], ei))
    tasks.append(("mm", ("Eh", 3), ("Eh", 0), ("T1h", -1), None, -1))
    tasks.append(("mm", ("Eh", 2), ("T1h", -1), ("T2h", -1), None, -1))
    tasks.append(("mm", ("Eh", 1), ("T2h", -1), ("Rh", -1), None, -1))
    return tasks


def _bt(B, key):
    k, idx = key
    return B[k] if idx < 0 else B[k][idx]


def expm_make_plan(ct):
    tasks = expm_tasks()
    plan = []
    for ti, t in enumerate(tasks):
        kind = t[0]
        e = {"kind": kind, "sl": ti % 2, "ti": ti, "t": t}
        nop_ = 2 if (ti % 2 == 0) else 3
        e["nop"] = nop_ if kind == "mm" else 0
        e["nload"] = (nop_ + (32 if t[4] is not None else 0)) if kind == "mm" else 16
        plan.append(e)
    w_base = ct["w"]
    acc = ct["w"]
    for e in plan:
        e["w0"] = acc
        acc += 16 * e["nload"]
    ct["w"] = acc
    pe_cnt, pl_cnt, d_cnt = ct["pe"], ct["v"], ct["d"]
    pe_of, pl_of, d_of = {}, {}, {}
    for e in plan:
        if e["kind"] == "mm":
            for rb in range(8):
                pe_cnt += 1
                pe_of[(e["ti"], rb)] = pe_cnt
            e["pe_end"] = pe_cnt
    for e in plan:
        for rb in range(8):
            pl_cnt += 1
            pl_of[(e["ti"], rb)] = pl_cnt
        e["pl_end"] = pl_cnt
    for e in plan:
        for rb in range(8):
            d_cnt += 16
            d_of[(e["ti"], rb)] = d_cnt
        e["d_end"] = d_cnt
    d_base = ct["d"]
    ct["pe"], ct["v"], ct["d"] = pe_cnt, pl_cnt, d_cnt
    return {"plan": plan, "pe_of": pe_of, "pl_of": pl_of, "d_of": d_of,
            "w_base": w_base, "d_base": d_base}


def gens_cast(pl, a, B, ia0):
    n = ia0
    for ei in range(4):
        for src, dst in ((B["gensE"][ei], B["geh"][ei]), (B["gensET"][ei], B["gehT"][ei])):
            if n > 0:
                pl.wait_ge(a["s_ia"], n)
            pl.dma_start(out=dst, in_=src).then_inc(a["s_ia"], 16)
            n += 16
    return n


def expm_sync(q, a, B, m, P):
    plan, pl_of = P["plan"], P["pl_of"]
    q.wait_ge(a["s_ia"], P["ia_gens"])
    Q = SerialQ(q, a, P["w_base"], P["d_base"])
    for e in plan:
        ti, sl, t = e["ti"], e["sl"], e["t"]
        if ti >= 2:
            prev = plan[ti - 2]
            if prev["kind"] == "mm":
                q.wait_ge(a["s_pe"], prev["pe_end"])
            q.wait_ge(a["s_v"], prev["pl_end"])
        for tj in range(max(0, ti - 2), ti):
            q.wait_ge(a["s_dep"], plan[tj]["d_end"])
        if e["kind"] == "mm":
            lh, rh, asm = t[1], t[2], t[4]
            Q.load(m["eLH"][sl].rearrange("p (k c) -> p k c", c=1024),
                   _bt(B, lh).rearrange("(k p) c -> p k c", p=128))
            if sl == 0:
                Q.load(m["eRH"][0][0].rearrange("p (k c) -> p k c", c=1024),
                       _bt(B, rh).rearrange("(k p) c -> p k c", p=128))
            else:
                for hnd in range(2):
                    Q.load(m["eRH"][1][hnd].rearrange("p (k c) -> p k c", c=1024),
                           _bt(B, rh)[hnd * 512:(hnd + 1) * 512, :].rearrange("(k p) c -> p k c", p=128))
        out = t[3]

        def st(rb):
            q.wait_ge(a["s_v"], pl_of[(ti, rb)])
            Q.store(_bt(B, out)[rb * 128:(rb + 1) * 128, :], m["eo"][rb % 2][:, :])
        if e["kind"] == "mm" and t[4] is not None:
            ei = t[5]
            srcs = [_bt(B, ("geh", ei)), B["A2h"], B["A3h"], B["idI"]]
            for rb in range(8):
                if rb >= 2:
                    st(rb - 2)
                for si in range(4):
                    Q.load(m["easm"][rb % 2][si][:, :], srcs[si][rb * 128:(rb + 1) * 128, :])
            st(6)
            st(7)
        elif e["kind"] == "pinit":
            ei = t[5]
            for rb in range(8):
                if rb >= 2:
                    st(rb - 2)
                Q.load(m["easm"][rb % 2][0][:, :], _bt(B, ("geh", ei))[rb * 128:(rb + 1) * 128, :])
                Q.load(m["easm"][rb % 2][3][:, :], B["idI"][rb * 128:(rb + 1) * 128, :])
            st(6)
            st(7)
        else:
            for rb in range(8):
                st(rb)


def expm_pe(pe, a, m, psE, P):
    plan, pl_of = P["plan"], P["pl_of"]
    for e in plan:
        if e["kind"] != "mm":
            continue
        ti, sl = e["ti"], e["sl"]
        for rb in range(8):
            pe.wait_ge(a["s_win"], e["w0"] + 16 * e["nop"])
            if rb >= 2:
                pe.wait_ge(a["s_v"], pl_of[(ti, rb - 2)])
            else:
                for tj in range(ti - 1, -1, -1):
                    pe.wait_ge(a["s_v"], pl_of[(tj, 6 + rb)])
                    break
            ps = psE[rb % 2]
            r = None
            for kb in range(8):
                for hh in range(2):
                    r = pe.matmul(ps[hh][:, :],
                                  m["eLH"][sl][:, kb * 1024 + rb * 128:kb * 1024 + rb * 128 + 128],
                                  rhtile(m, sl, kb)[:, hh * 512:(hh + 1) * 512],
                                  start=(kb == 0), stop=(kb == 7))
            r.then_inc(a["s_pe"], 1)


def expm_pl(pl, a, m, psE, P):
    plan, pe_of, d_of = P["plan"], P["pe_of"], P["d_of"]
    c13 = EXPM_CHUNKS[3]
    for e in plan:
        ti, sl, t = e["ti"], e["sl"], e["t"]
        asm = t[4]
        for rb in range(8):
            if rb >= 2:
                pl.wait_ge(a["s_dep"], d_of[(ti, rb - 2)])
            elif ti >= 1:
                pl.wait_ge(a["s_dep"], d_of[(ti - 1, 6 + rb)])
            if e["kind"] == "mm":
                pl.wait_ge(a["s_pe"], pe_of[(ti, rb)])
                ps = psE[rb % 2]
                if asm is None:
                    pl.tensor_copy(m["eo"][rb % 2][:, 0:512], ps[0][:, :])
                    pl.tensor_copy(m["eo"][rb % 2][:, 512:1024], ps[1][:, :]).then_inc(a["s_v"], 1)
                    pl.drain()
                else:
                    cI, cA, cA2, cA3 = asm
                    pl.wait_ge(a["s_win"], e["w0"] + 16 * (e["nop"] + 4 * (rb + 1)))
                    X_, A2_, A3_, I_ = m["easm"][rb % 2]
                    r = None
                    for hh in range(2):
                        hsl = slice(hh * 512, (hh + 1) * 512)
                        scr = m["escr"][hh][:, 0:512]
                        pl.scalar_tensor_tensor(scr, X_[:, hsl], cA, ps[hh][:, :], OP.mult, OP.add)
                        pl.drain()
                        pl.scalar_tensor_tensor(scr, A2_[:, hsl], cA2, scr, OP.mult, OP.add)
                        pl.drain()
                        pl.scalar_tensor_tensor(scr, A3_[:, hsl], cA3, scr, OP.mult, OP.add)
                        pl.drain()
                        r = pl.scalar_tensor_tensor(m["eo"][rb % 2][:, hsl], I_[:, hsl], cI, scr,
                                                    OP.mult, OP.add)
                    r.then_inc(a["s_v"], 1)
                    pl.drain()
            else:
                pl.wait_ge(a["s_win"], e["w0"] + 16 * 2 * (rb + 1))
                r = None
                for hh in range(2):
                    hsl = slice(hh * 512, (hh + 1) * 512)
                    scr = m["escr"][hh][:, 0:512]
                    pl.tensor_scalar(scr, m["easm"][rb % 2][0][:, hsl], c13[1], None, OP.mult)
                    pl.drain()
                    r = pl.scalar_tensor_tensor(m["eo"][rb % 2][:, hsl], m["easm"][rb % 2][3][:, hsl],
                                                c13[0], scr, OP.mult, OP.add)
                r.then_inc(a["s_v"], 1)
                pl.drain()


def build_p5(a, m, ps2, ptT):
    sc = Sched()

    def wwin5(h, i, k, sg, st):
        rwait(h, i, a["s_win"], "xw", 384,
              a["_p5w0"] + 16 * 6 + 48 * (4 * sg + st) + 16 * k)

    def col(sg, st, k):
        o = (sg * 4 + st) * 32 + k
        return m["cols"][:, o:o + 1]

    def uc(base, sg, st, k=1):
        u = sg * 4 + st - k
        if u >= 0:
            return (base + str(u // 4) + str(u % 4), 0)
        return (base + str((u + 8) // 4) + str((u + 8) % 4), -1)

    for sg in range(2):
        for st in range(4):
            sfx = str(sg) + str(st)

            def f_c5(h, i, sg=sg, st=st):
                wwin5(h, i, 1, sg, st)
                return h.tensor_copy(m["moebf"][sg][:, :], m["moe5"][sg][:, :])
            sc.add("pl", "p5c" + sfx, f_c5, deps=[("p5t" + sfx, -1)])
            def f_t5(h, i, sg=sg):
                r = None
                for c in range(8):
                    r = h.transpose(ptT[:, sg * 1024 + c * 128:sg * 1024 + c * 128 + 128],
                                    m["moebf"][sg][:, c * 128:c * 128 + 128], a["ident"][:, :])
                return r
            sc.add("pe", "p5t" + sfx, f_t5, deps=[("p5c" + sfx, 0), ("p5e" + sfx, -1)])
            def f_e5(h, i, sg=sg):
                return h.activation(m["moefm"][sg][:, :], ptT[:, sg * 1024:(sg + 1) * 1024], AF.Identity)
            sc.add("a", "p5e" + sfx, f_e5, deps=[("p5t" + sfx, 0), ("p5rot1" + sfx, -1)])
            for hh in range(2):
                def f_r5(h, i, sg=sg, hh=hh):
                    h.wait_ge(a["s_win"], a["_p5w0"] + 16 * 6)
                    r = None
                    for c in range(8):
                        r = h.matmul(ps2[hh], m["moefm"][sg][:, c * 128:c * 128 + 128],
                                     m["Rs"][:, c * 1024 + hh * 512:c * 1024 + hh * 512 + 512],
                                     start=(c == 0), stop=(c == 7))
                    return r
                sc.add("pe", "p5rot" + str(hh) + sfx, f_r5,
                       deps=[("p5e" + sfx, 0), ("p5mx" + str(hh) + sfx, -1), ("p5d2" + str(hh) + sfx, -1)])
            for hh in range(2):
                def f_d1(h, i, sg=sg, st=st, hh=hh):
                    wwin5(h, i, 2, sg, st)
                    return h.tensor_tensor(m["sq5"][hh][:, 0:512],
                                           m["abel5"][sg][:, hh * 512:(hh + 1) * 512],
                                           m["gcw1"][hh][:, 0:512], OP.mult)
                sc.add("v", "p5d1" + str(hh) + sfx, f_d1, deps=[("p5rd1" + str(hh) + sfx, -1)])
                sc.add("v", "p5rd1" + str(hh) + sfx, (lambda h, i, sg=sg, st=st, hh=hh: h.reduce_sum(
                    col(sg, st, hh), m["sq5"][hh][:, 0:512], axis=AX)),
                    deps=[("p5d1" + str(hh) + sfx, 0)])
            for hh in range(2):
                def f_d2(h, i, sg=sg, hh=hh):
                    return h.tensor_tensor(m["sq5"][hh][:, 0:512], ps2[hh],
                                           m["gcw2"][hh][:, 0:512], OP.mult)
                sc.add("v", "p5d2" + str(hh) + sfx, f_d2,
                       deps=[("p5rot" + str(hh) + sfx, 0), ("p5rd2" + str(hh) + sfx, -1)])
                sc.add("v", "p5rd2" + str(hh) + sfx, (lambda h, i, sg=sg, st=st, hh=hh: h.reduce_sum(
                    col(sg, st, 2 + hh), m["sq5"][hh][:, 0:512], axis=AX)),
                    deps=[("p5d2" + str(hh) + sfx, 0)])
            sc.add("v", "p5s1" + sfx, (lambda h, i, sg=sg, st=st: h.tensor_tensor(
                col(sg, st, 0), col(sg, st, 0), col(sg, st, 1), OP.add)),
                deps=[("p5rd1" + "1" + sfx, 0)])
            sc.add("v", "p5s2" + sfx, (lambda h, i, sg=sg, st=st: h.tensor_tensor(
                col(sg, st, 2), col(sg, st, 2), col(sg, st, 3), OP.add)),
                deps=[("p5rd2" + "1" + sfx, 0)])
            sc.add("v", "p5s3" + sfx, (lambda h, i, sg=sg, st=st: h.tensor_tensor(
                col(sg, st, 0), col(sg, st, 0), col(sg, st, 2), OP.add)))
            sc.add("v", "p5s4" + sfx, (lambda h, i, sg=sg, st=st: h.tensor_tensor(
                col(sg, st, 0), col(sg, st, 0), m["cols"][:, 442:443], OP.add)))
            sc.add("a", "p5sg" + sfx, (lambda h, i, sg=sg, st=st: h.activation(
                col(sg, st, 1), col(sg, st, 0), AF.Exp, scale=-1.0)), deps=[("p5s4" + sfx, 0)])
            sc.add("v", "p5sp" + sfx, (lambda h, i, sg=sg, st=st: h.tensor_scalar(
                col(sg, st, 1), col(sg, st, 1), 1.0, None, OP.add)), deps=[("p5sg" + sfx, 0)])
            sc.add("v", "p5sr" + sfx, (lambda h, i, sg=sg, st=st: h.reciprocal(
                col(sg, st, 1), col(sg, st, 1))))
            for hh in range(2):
                def f_mx(h, i, sg=sg, st=st, hh=hh):
                    d = m["scr5"][hh][:, 0:512]
                    return h.tensor_tensor(d, ps2[hh], m["abel5"][sg][:, hh * 512:(hh + 1) * 512],
                                           OP.subtract)
                sc.add("v", "p5mx" + str(hh) + sfx, f_mx,
                       deps=[("p5sr" + sfx, 0), ("p5d2" + str(hh) + sfx, 0)])
                def f_mx2(h, i, sg=sg, st=st, hh=hh):
                    if hh == 0 and (not isinstance(i, int) or (i * 2 + sg) * 4 + st >= 1):
                        rwait(h, i, a["s_dep"], "xd", 128, a["_p5d0"] + 16 * (4 * sg + st))
                    d = m["y"][sg][:, hh * 512:(hh + 1) * 512]
                    return h.scalar_tensor_tensor(d, m["scr5"][hh][:, 0:512], col(sg, st, 1),
                                                  m["abel5"][sg][:, hh * 512:(hh + 1) * 512],
                                                  OP.mult, OP.add)
                sc.add("v", "p5mx2" + str(hh) + sfx, f_mx2, deps=[uc("p5on", sg, st)])
                def f_y(h, i, sg=sg, st=st, hh=hh):
                    wwin5(h, i, 3, sg, st)
                    d = m["y"][sg][:, hh * 512:(hh + 1) * 512]
                    return h.scalar_tensor_tensor(d, m["x15"][sg][:, hh * 512:(hh + 1) * 512],
                                                  2.0, d, OP.mult, OP.add)
                sc.add("v", "p5y" + str(hh) + sfx, f_y, deps=[("p5mx2" + str(hh) + sfx, 0)])
            for hh in range(2):
                def f_sq(h, i, sg=sg, hh=hh):
                    return h.tensor_tensor(m["sq5"][hh][:, 0:512], m["y"][sg][:, hh * 512:(hh + 1) * 512],
                                           m["y"][sg][:, hh * 512:(hh + 1) * 512], OP.mult)
                sc.add("pl", "p5sq" + str(hh) + sfx, f_sq,
                       deps=[("p5y" + str(hh) + sfx, 0), uc("p5rq" + str(hh), sg, st)])
                sc.add("v", "p5rq" + str(hh) + sfx, (lambda h, i, sg=sg, st=st, hh=hh: h.reduce_sum(
                    col(sg, st, 6 + hh), m["sq5"][hh][:, 0:512], axis=AX)),
                    deps=[("p5sq" + str(hh) + sfx, 0)])
            sc.add("v", "p5r1" + sfx, (lambda h, i, sg=sg, st=st: h.reduce_sum(
                col(sg, st, 5), m["y"][sg], axis=AX)), deps=[("p5y1" + sfx, 0)])
            sc.add("v", "p5mu" + sfx, (lambda h, i, sg=sg, st=st: h.tensor_scalar(
                col(sg, st, 5), col(sg, st, 5), 1.0 / 1024, None, OP.mult)))
            sc.add("v", "p5ad" + sfx, (lambda h, i, sg=sg, st=st: h.tensor_tensor(
                col(sg, st, 6), col(sg, st, 6), col(sg, st, 7), OP.add)),
                deps=[("p5rq1" + sfx, 0)])
            sc.add("v", "p5m2" + sfx, (lambda h, i, sg=sg, st=st: h.tensor_scalar(
                col(sg, st, 6), col(sg, st, 6), 1.0 / 1024, None, OP.mult)))
            sc.add("v", "p5nv" + sfx, (lambda h, i, sg=sg, st=st: h.scalar_tensor_tensor(
                col(sg, st, 7), col(sg, st, 5), col(sg, st, 5), col(sg, st, 6), OP.mult, OP.subtract)))
            sc.add("a", "p5ln" + sfx, (lambda h, i, sg=sg, st=st: h.activation(
                col(sg, st, 8), col(sg, st, 7), AF.Ln, scale=-1.0, bias=m["cols"][:, 441:442])),
                deps=[("p5nv" + sfx, 0)])
            sc.add("a", "p5rs" + sfx, (lambda h, i, sg=sg, st=st: h.activation(
                col(sg, st, 9), col(sg, st, 8), AF.Exp, scale=-0.5)))
            def f_on(h, i, sg=sg, st=st):
                return h.tensor_scalar(m["y"][sg], m["y"][sg], col(sg, st, 5), col(sg, st, 9),
                                       OP.subtract, OP.mult)
            sc.add("v", "p5on" + sfx, f_on, deps=[("p5rs" + sfx, 0)])
    sc.finalize()
    return sc


def p5_sync(q, a, B, m, sc):
    Q = SerialQ(q, a, a["_p5w0"], a["_p5d0"])
    Q.load(m["Rs"].rearrange("p (k c) -> p k c", c=1024),
           B["Rh"].rearrange("(k p) c -> p k c", p=128))
    for hh in range(2):
        Q.load(m["gcw1"][hh][:, 0:512], B["gcw1rep"][:, hh * 512:(hh + 1) * 512])
        Q.load(m["gcw2"][hh][:, 0:512], B["gcw2rep"][:, hh * 512:(hh + 1) * 512])
    Q.load(m["cols"][:, 442:443], B["gcbrep"][:, :])
    bv = sc.base["v"]
    bpl = sc.base["pl"]
    Sv, Spl = sc.S["v"], sc.S["pl"]

    def vdone(name, i):
        eng, k = sc.idx[name]
        b = {"v": bv, "pl": bpl}[eng]
        S = {"v": Sv, "pl": Spl}[eng]
        return b + i * S + k

    for s in range(16):
        i, sg = s // 2, s % 2

        def ostore(st, s=s, sg=sg, i=i):
            q.wait_ge(a["s_v"], vdone("p5on" + str(sg) + str(st), i))
            Q.store(B["out"][(s * 4 + st) * 128:(s * 4 + st + 1) * 128, :], m["y"][sg][:, :])
        for st in range(4):
            u = s * 4 + st

            def pwait(sem, name):
                if st >= 1:
                    q.wait_ge(sem, vdone(name + str(sg) + str(st - 1), i))
                elif s >= 2:
                    q.wait_ge(sem, vdone(name + str(sg) + "3", i - 1))
            pwait(a["s_pl"], "p5c")
            Q.load(m["moe5"][sg][:, :], B["moeh"][u * 128:(u + 1) * 128, :])
            pwait(a["s_v"], "p5mx21")
            Q.load(m["abel5"][sg][:, :], B["abelh"][u * 128:(u + 1) * 128, :])
            pwait(a["s_v"], "p5y1")
            Q.load(m["x15"][sg][:, :], B["x1h"][u * 128:(u + 1) * 128, :])
            if st >= 1:
                ostore(st - 1)
        ostore(3)
    a["_p5w_end"] = Q.w
    a["_p5d_end"] = Q.d


get_regs = shared_regs


def emit_phase(h, eng, sc, a, keys, nsteps=8):
    for k, v in keys.items():
        a[k] = v
    if not sc.S.get(eng):
        return
    sc.emit_engine(h, eng, a, 0)
    sc.alloc_regs(h, eng)
    sc.reg_init(h, eng, 1)
    a["_memo"] = {}
    with fori_shared(h, 1, nsteps) as i:
        sc.emit_engine(h, eng, a, i)
        sc.reg_step(h, eng)


def build_all(p, a, B, ct0, phases=15):
    m = p34_carve(a)
    psA = [a["pz1s"][:, 0:512], a["pzb1"][:, 0:512]]
    psB = [a["pgz1"][:, 0:512], a["pz2s"][:, 0:512]]
    psG = [a["pw"][:, 0:6], a["pab"][:, 0:6]]
    ps1 = [a["pz1s"][:, 0:512], a["pzb1"][:, 0:512]]
    ps2 = [a["pgz1"][:, 0:512], a["pz2s"][:, 0:512]]
    psE = [[a["pz1s"][:, 0:512], a["pzb1"][:, 0:512]],
           [a["pgz1"][:, 0:512], a["pz2s"][:, 0:512]]]
    ptT = a["ptT"]
    ct = dict(ct0)

    PH3, PHX, PH4, PH5 = phases & 1, phases & 2, phases & 4, phases & 8
    sc3 = build_p3(a, m, psA, psB, psG, ptT, p.nc)
    sc3.aux = {"xw": 384, "xd": 160}
    sc3.base = {e: ct[e] for e in ("pe", "v", "a", "pl")}
    k3 = {"_p3w0": ct["w"], "_p3d0": ct["d"], "_p3ia": ct["ia"] + 16 * 9}
    if PH3:
        for e in sc3.S:
            ct[e] += 8 * sc3.S[e]
        ct["w"] += 16 * (10 + 3 * 64)
        ct["d"] += 16 * 5 * 16
        ct["ia"] += 16 * 9
    bar3 = dict(ct)

    if PHX:
        PX = expm_make_plan(ct)
        PX["ia_gens"] = bar3["ia"] + 16 * 8
        ct["ia"] += 16 * 8
    else:
        PX = None
    barX = dict(ct)

    sc4 = build_p4(a, m, ps1, ps2, p.nc)
    sc4.aux = {"xw": 160, "xd": 128}
    sc4.base = {e: ct[e] for e in ("pe", "v", "a", "pl")}
    k4 = {"_p4w0": ct["w"], "_p4d0": ct["d"], "_p4ia0": ct["ia"]}
    p4_pe_ends = [sc4.base["pe"] + 8 * sc4.S["pe"] * ep for ep in range(8)]
    p4_v_ends = [sc4.base["v"] + 8 * sc4.S["v"] * ep for ep in range(8)]
    if PH4:
        for e in sc4.S:
            ct[e] += 64 * sc4.S[e]
        ct["w"] += 16 * 5 * 16 * 8
        ct["d"] += 16 * 4 * 16 * 8
        ct["ia"] += 16 * 7 * 8
    bar4 = dict(ct)

    sc5 = build_p5(a, m, ps2, ptT)
    sc5.aux = {"xw": 384, "xd": 128}
    sc5.base = {e: ct[e] for e in ("pe", "v", "a", "pl")}
    k5 = {"_p5w0": ct["w"], "_p5d0": ct["d"]}
    if PH5:
        for e in sc5.S:
            ct[e] += 8 * sc5.S[e]
        ct["w"] += 16 * (6 + 3 * 64)
        ct["d"] += 16 * 4 * 16
    ct_end = dict(ct)

    def barrier(h, vals):
        for k in ("pe", "v", "a", "pl"):
            h.wait_ge(a["s_" + k], vals[k])

    def compute(eng):
        def f(h):
            barrier(h, ct0)
            if PH3:
                if eng == "pl":
                    p3_weights(h, a, B, m, ct0['ia'])
                if eng == "v":
                    h.memset(m["wts"][:, :], 1.0)
                    h.drain()
                emit_phase(h, eng, sc3, a, k3)
            if PHX:
                if eng == "pe":
                    h.wait_ge(a["s_v"], bar3["v"])
                    expm_pe(h, a, m, psE, PX)
                if eng == "pl":
                    gens_cast(h, a, B, bar3["ia"])
                if eng == "v":
                    expm_pl(h, a, m, psE, PX)
            if PH4:
                if eng == "pl":
                    nia = barX["ia"]
                    for ep in range(8):
                        pdv = (PX["plan"][-1]["pe_end"] if PHX else barX["pe"]) if ep == 0 else p4_pe_ends[ep]
                        vdv = bar3["v"] if ep == 0 else p4_v_ends[ep]
                        nia = p4_weights(h, a, B, m, ep, pdv, vdv, nia)
                if eng == "pe":
                    h.wait_ge(a["s_v"], barX["v"])
                emit_phase(h, eng, sc4, a, k4, nsteps=64)
            if PH5:
                emit_phase(h, eng, sc5, a, k5)
            barrier(h, ct_end)
            h.wait_ge(a["s_dep"], ct_end["d"])
        return f

    def syncf(q):
        barrier(q, ct0)
        if PH3:
            for k, v in k3.items():
                a[k] = v
            p3_sync(q, a, B, m, sc3)
        if PHX:
            q.wait_ge(a["s_pe"], bar3["pe"])
            q.wait_ge(a["s_a"], bar3["a"])
            q.wait_ge(a["s_v"], bar3["v"])
            q.wait_ge(a["s_dep"], bar3["d"])
            expm_sync(q, a, B, m, PX)
        if PH4:
            for k, v in k4.items():
                a[k] = v
            p4_sync(q, a, B, m, sc4)
        if PH5:
            q.wait_ge(a["s_pe"], bar4["pe"])
            q.wait_ge(a["s_v"], bar4["v"])
            q.wait_ge(a["s_dep"], bar4["d"])
            for k, v in k5.items():
                a[k] = v
            p5_sync(q, a, B, m, sc5)
        q.wait_ge(a["s_dep"], ct_end["d"])
        barrier(q, ct_end)
    return {"compute": compute, "sync": syncf, "ct_end": ct_end, "m": m}




NT = 8192
NG = 4
NSTEPS = 128
HS_OF = [0, 1, 0, 1]


def declare_dram(nc):
    B = {}

    def inp(nm, shp, dt=F32):
        B[nm] = nc.dram_tensor(nm, shp, dt, kind="ExternalInput").ap()

    def scratch(nm, shp, dt):
        B[nm] = nc.dram_tensor(nm, shp, dt, kind="Internal").ap()

    inp("x", [NT, 1024])
    for nm in ("wqT", "wkT", "wvT", "wgT", "woT"):
        inp(nm, [1024, 1024])
    inp("lrwT", [1024, 16]); inp("lrb", [16])
    inp("lti_rep", [128, 1]); inp("invi", [128, 1]); inp("lti15_rep", [128, 1])
    inp("et", [2, 16, 16, 128], BF16)
    inp("sctab", [19, 128], mybir.dt.int32)
    inp("w1duo", [8, 128, 256]); inp("w2t", [8, 128, 256]); inp("w2cm", [8, 128, 256])
    inp("b1rep", [2, 128, 256]); inp("b2rep_bdt", [2, 128, 128])
    inp("lnw_bdt", [2, 128, 128]); inp("lnb_bdt", [2, 128, 128])
    inp("bd01", [128, 128]); inp("trilT01", [128, 128])
    inp("identc", [128, 128], BF16); inp("tril01b", [128, 128], BF16)
    inp("ident4", [128, 32], BF16); inp("ones01", [128, 128], BF16)
    inp("gatew", [1024, 6])
    for nm in ("postwrep", "postbrep", "lngrep", "lnbrep", "gcw1rep", "gcw2rep"):
        inp(nm, [128, 1024])
    inp("gatebrep", [128, 6]); inp("epsc2", [128, 2]); inp("gcbrep", [128, 1])
    inp("ew1s", [8, 1024, 2048]); inp("ew2s", [8, 2048, 1024])
    inp("eb1c", [8, 128, 16]); inp("eb2rep", [8, 128, 1024])
    inp("gensE", [4, 1024, 1024]); inp("gensET", [4, 1024, 1024])
    inp("idI", [1024, 1024], BF16)
    inp("abelh", [NT, 1024]); inp("moeh", [NT, 1024])
    scratch("xbf", [NT, 1024], BF16); scratch("xfm", [1024, NT], BF16)
    for nm in ("xqfm", "xkfm", "xvfm", "gfm", "xqwfm"):
        scratch(nm, [1024, NT], BF16)
    scratch("x1h", [NT, 1024], F32)
    scratch("xnfmh", [1024, NT], BF16)
    scratch("geh", [4, 1024, 1024], BF16); scratch("gehT", [4, 1024, 1024], BF16)
    for nm in ("A2h", "A2Th", "A3h", "A4Th", "ph", "qh", "T1h", "T2h", "Rh"):
        scratch(nm, [1024, 1024], BF16)
    scratch("Eh", [4, 1024, 1024], BF16)
    B["out"] = nc.dram_tensor("out", [NT, 1024], F32, kind="ExternalOutput").ap()
    return B


def build_program():
    nc = bass.Bass("TRN2", target_bir_lowering=False)
    B = declare_dram(nc)
    p = Prog(nc, B)
    a = scan_alloc(p, ng=NG)
    w1fb = a["w1f"].bitcast(BF16)
    w2tfb = a["w2tf"].bitcast(BF16)
    w2cfb = a["w2cf"].bitcast(BF16)
    t0 = {"xt": [w1fb[:, 0:8192], w2tfb[:, 0:8192]]}
    t1 = {
        "xin": [a["xqw"], a["xkw"]],
        "wsb": [w1fb[:, 0:8192], w2tfb[:, 0:8192]],
        "stg": [a["xvw"], a["dep"]],
        "gtmp": [w2cfb[:, 0:4096], w2cfb[:, 4096:8192], a["w1b"]],
        "lrsq": [a["w2cb"][0:16, 0:2048], a["w2cb"][0:16, 2048:4096],
                 a["gz1bd"][0:16, 0:2048], a["x2sbd"][0:16, 0:2048]],
        "etsb": a["w2tb"][0:16, :],
        "lrw": a["scur"][:, 0:128],
        "lrbt": a["b2rep"].bitcast(F32)[0:16, 0:1],
        "ppp": [a["pz1s"][:, 0:512], a["pzb1"][:, 0:512], a["pgz1"][:, 0:512], a["pz2s"][:, 0:512]],
        "plr": a["pw"][0:16, 0:512],
        "plq": a["pab"][:, 0:128],
    }
    em0 = p0_emit(p, tiles=t0)
    em1 = p1_emit(p, nproj=4, do_lr=True, do_lrq=True, tiles=t1)
    a["lrq"] = p.h_lrq
    NTT, UPT, UPP = 16, 4, 64
    P1_T = 4 * NTT * UPT + NTT * UPT + 8
    P1_O = 16 * NTT * 4 // 2

    b_ofA = [0, 0, 1, 1]
    b_ofB = [2, 2, 3, 3]
    scA = build_sched(a, b_ofA, HS_OF)
    scA = add_va_ops(scA, a, b_ofA, HS_OF)
    scB = build_sched(a, b_ofB, HS_OF)
    scB = add_va_ops(scB, a, b_ofB, HS_OF)
    S = scA.S
    NWIN = NSTEPS // 8
    NW = 3 * NG
    WIN_H = 16 * NW * NWIN * 4
    DEP_H = 16 * 4 * NG * (NWIN + 1)
    NINI_A = 8 * NG + 4 + 9 + 2

    def pre_syncA(q):
        q.wait_ge(p.h_p1_t, P1_T)
        q.wait_ge(p.h_p1_o0, P1_O)
        q.wait_ge(p.h_p1_o1, P1_O)
        q.wait_ge(p.h_p1_lq, 8)

    def pre_gpsA(g):
        g.wait_ge(p.h_p1_t, P1_T)
        g.wait_ge(p.h_p1_lq, 8)

    emA = scan_emitters(p, a, B, scA, b_ofA, HS_OF, nsteps=NSTEPS, fori=True,
                                lrq_dma=False, half=0, pre_sync=pre_syncA, pre_gps=pre_gpsA)
    NINI_A = emA["nini"]
    basesB = {"win": WIN_H, "dep": DEP_H, "ini": 16 * NINI_A,
              "pe": S["pe"] * NSTEPS, "v": S["v"] * NSTEPS,
              "a": S["a"] * NSTEPS, "pl": S["pl"] * NSTEPS}
    emB = scan_emitters(p, a, B, scB, b_ofB, HS_OF, nsteps=NSTEPS, fori=True,
                                lrq_dma=False, half=1, bases=basesB)
    ct0 = {"pe": 2 * S["pe"] * NSTEPS, "v": 2 * S["v"] * NSTEPS,
           "a": 2 * S["a"] * NSTEPS, "pl": 2 * S["pl"] * NSTEPS,
           "w": 2 * WIN_H, "d": 2 * DEP_H, "ia": 0}
    em34 = build_all(p, a, B, ct0)

    def syncf(q):
        em0["sync"](q)
        em1["sync"](q)
        emA["sync"](q)
        emB["sync"](q)
        em34["sync"](q)

    def gpsf(g):
        em0["gpsimd"](g)
        g.wait_ge(p.h_p0_fm0, 64)
        g.wait_ge(p.h_p0_fm1, 64)
        em1["gpsimd"](g)
        emA["gpsimd"](g)
        emB["gpsimd"](g)
        em34["compute"]("pl")(g)

    def tensorf(t):
        em1["tensor"](t)
        t.wait_ge(p.h_p1_o0, P1_O)
        t.wait_ge(p.h_p1_o1, P1_O)
        emA["tensor"](t)
        emB["tensor"](t)
        em34["compute"]("pe")(t)

    def vectorf(v):
        em1["vector"](v)
        v.wait_ge(p.h_p1_o0, P1_O)
        v.wait_ge(p.h_p1_o1, P1_O)
        emA["vector"](v)
        emB["vector"](v)
        em34["compute"]("v")(v)

    def scalarf(s):
        em1["scalar"](s)
        emA["scalar"](s)
        emB["scalar"](s)
        em34["compute"]("a")(s)

    with nc.allow_low_precision(reason="bf16 matmuls with fp32 accumulation, tolerance-tested"), \
            nc.Block() as block:
        block.sync(syncf)
        block.gpsimd(gpsf)
        block.tensor(tensorf)
        block.vector(vectorf)
        block.scalar(scalarf)
    return nc


def make_inmap(inp, core):
    hp = host_prep(inp, core)
    keys = ["x", "wqT", "wkT", "wvT", "wgT", "woT", "lrwT", "lrb", "lti_rep", "invi",
            "lti15_rep", "et", "w1duo", "w2t", "w2cm", "b1rep", "b2rep_bdt", "lnw_bdt",
            "lnb_bdt", "bd01", "trilT01", "identc", "tril01b", "ident4", "ones01",
            "gatew", "postwrep", "postbrep", "lngrep", "lnbrep", "gatebrep", "epsc2",
            "gcw1rep", "gcw2rep", "gcbrep", "ew1s", "ew2s", "eb1c", "eb2rep",
            "gensE", "gensET", "sctab"]
    im = {k: hp[k] for k in keys}
    im["idI"] = hp["idIbf"]
    im["abelh"] = np.zeros((NT, 1024), np.float32)
    im["moeh"] = np.zeros((NT, 1024), np.float32)
    return im



_CACHE = {}


def kernel(**inputs):
    if "nc" not in _CACHE:
        _CACHE["nc"] = build_program()
    nc = _CACHE["nc"]
    in_maps = [make_inmap(inputs, c) for c in range(8)]
    res = run_bass_kernel_spmd(nc, in_maps, core_ids=list(range(8)))
    outs = [np.asarray(res.results[c]["out"]) for c in range(8)]
    return np.concatenate(outs, 0).reshape(32, 2048, 1024).astype(np.float32)
